# version 27
# baseline (speedup 1.0000x reference)
import numpy as np
import concourse.bass as bass
import concourse.mybir as mybir
from concourse.bass_utils import run_bass_kernel_spmd
from contextlib import ExitStack

F32 = mybir.dt.float32
BF16 = mybir.dt.bfloat16
U32 = mybir.dt.uint32
AF = mybir.ActivationFunctionType
ALU = mybir.AluOpType
AX = mybir.AxisListType
AP = bass.AP

S = 2048
D = 2048
HC = 8
TOK = 1024
FF = 512
NE = 8
RSQ = 1.0 / float(np.sqrt(2048.0))

ENGINES = ("sync", "scalar", "vector", "tensor", "gpsimd")


class _Op:
    __slots__ = ("engine", "fn", "kind", "name", "eidx", "deps", "cross",
                 "inc_sem", "inc_val", "reach", "waits", "lane")

    def __init__(self):
        self.cross = False
        self.inc_sem = None
        self.reach = None
        self.waits = []


class _SemDep:

    def __init__(self, sem, val):
        self.engine = None
        self.kind = "sem"
        self.reach = (sem, val)
        self.name = "semdep"
        self.cross = True


class Seq:

    def __init__(self):
        self.eng_ops = {e: [] for e in ENGINES}
        self.res = {}
        self.pending_barrier = {e: [] for e in ENGINES}
        self.dma_counters = {e: 0 for e in ENGINES}
        self.last_async = {}

    def emit(self, engine, fn, reads=(), writes=(), kind="comp", name="",
             stage_lane=False):
        op = _Op()
        op.engine = engine
        op.fn = fn
        op.kind = kind
        op.name = name
        op.eidx = len(self.eng_ops[engine])
        self.eng_ops[engine].append(op)
        if kind == "dma":
            if stage_lane:
                op.lane = self.STAGE_LANES[stage_lane]
            else:
                op.lane = self.dma_counters[engine] % 5
                self.dma_counters[engine] += 1
                self.last_async[(engine, op.lane)] = op
        elif kind == "cc":
            self.last_async[("cc",)] = op
        deps = []
        if self.pending_barrier[engine]:
            deps.extend((d, True) for d in self.pending_barrier[engine])
            self.pending_barrier[engine] = []
        for r in reads:
            st = self.res.get(r)
            if st is not None and st[0] is not None:
                deps.append((st[0], True))
        for w in writes:
            st = self.res.get(w)
            if st is not None:
                if st[0] is not None:
                    deps.append((st[0], False))
                deps.extend((d, False) for d in st[1])
        for r in reads:
            st = self.res.setdefault(r, [None, []])
            st[1].append(op)
        for w in writes:
            self.res[w] = [op, []]
        op.deps = deps
        return op

    def barrier(self):
        lasts = [ops[-1] for ops in self.eng_ops.values() if ops]
        lasts.extend(self.last_async.values())
        for e in ENGINES:
            self.pending_barrier[e] = list(lasts)

    N_DMA_LANES = 8
    STAGE_LANES = {"A": 5, "B": 6, "C": 7}

    def link_and_emit(self, nc, stack):
        sems = {}
        for e in ENGINES:
            sems[(e, "comp")] = stack.enter_context(nc.semaphore(f"sc_{e}"))
            for ln in range(self.N_DMA_LANES):
                sems[(e, "dma", ln)] = stack.enter_context(
                    nc.semaphore(f"sd_{e}{ln}"))
        sems[("gpsimd", "cc")] = stack.enter_context(nc.semaphore("s_cc"))

        for e in ENGINES:
            for op in self.eng_ops[e]:
                for d, is_raw in op.deps:
                    if d.engine == op.engine and d.kind == "comp" \
                            and e == "tensor":
                        continue
                    d.cross = True
        for e in ENGINES:
            counts = {"comp": 0, "cc": 0}
            lane_counts = [0] * self.N_DMA_LANES
            stage_set = set(self.STAGE_LANES.values())
            stage_ops = []
            for op in self.eng_ops[e]:
                if op.kind == "dma":
                    ln = op.lane
                    sem = sems[(e, "dma", ln)]
                    if ln in stage_set:
                        if lane_counts[ln] > 0:
                            op.deps.append(
                                (_SemDep(sem, lane_counts[ln]), True))
                        lane_counts[ln] += 16
                        op.inc_sem, op.inc_val = sem, 16
                        op.reach = (sem, lane_counts[ln])
                        continue
                    if lane_counts[ln] > 0:
                        op.deps.append(
                            (_SemDep(sem, lane_counts[ln]), True))
                    lane_counts[ln] += 16
                    op.inc_sem, op.inc_val = sem, 16
                    op.reach = (sem, lane_counts[ln])
                elif op.kind == "cc":
                    counts["cc"] += 1
                    op.inc_sem, op.inc_val = sems[("gpsimd", "cc")], 1
                    op.reach = (sems[("gpsimd", "cc")], counts["cc"])
                elif op.cross:
                    counts["comp"] += 1
                    op.inc_sem, op.inc_val = sems[(e, "comp")], 1
                    op.reach = (sems[(e, "comp")], counts["comp"])
        for e in ENGINES:
            satisfied = {}
            for op in self.eng_ops[e]:
                waits = {}
                for d, is_raw in op.deps:
                    if d.engine == op.engine and d.kind == "comp" \
                            and e == "tensor":
                        continue
                    assert d.reach is not None, (op.name, d.name)
                    sem, val = d.reach
                    if satisfied.get(sem, 0) >= val:
                        continue
                    if waits.get(sem, 0) < val:
                        waits[sem] = val
                op.waits = sorted(waits.items(), key=lambda kv: str(kv[0]))
                for sem, val in waits.items():
                    satisfied[sem] = val

        with nc.Block() as block:
            for ename in ENGINES:
                ops = self.eng_ops[ename]
                if not ops:
                    continue

                def make_body(ops=ops):
                    def body(eng):
                        for op in ops:
                            for sem, val in op.waits:
                                eng.wait_ge(sem, val)
                            inst = op.fn(eng)
                            if op.inc_sem is not None:
                                inst.then_inc(op.inc_sem, op.inc_val)
                    return body

                getattr(block, ename)(make_body())

    def n_ops(self):
        return sum(len(v) for v in self.eng_ops.values())


ARENA_BYTES = 198 * 1024

LAYOUT = {
    "ident":   (0,     BF16, 128),
    "ones":    (256,   BF16, 128),
    "onesf":   (512,   F32, 128),
    "identf":  (1024,  F32, 128),
    "trimask": (1536,  BF16, 128),
    "bqk":     (1792,  F32, 16),
    "stats":   (1856,  F32, 32),
    "cp":      (1984,  F32, 8),
    "rc":      (2016,  F32, 4),
    "negM2":   (2032,  F32, 2),
    "nb":      (2048,  F32, 16),
    "scoresR": (2112,  F32, 64),
    "W":       (2368,  F32, 64),
    "rwhl":    (2624,  BF16, 256),
    "offs":    (3136,  U32, 8),
    "m1s":     (3200,  F32, 8),
    "m2s":     (3232,  F32, 8),
    "t2a":     (3264,  F32, 64),
    "t2b":     (3520,  F32, 64),
    "t2c":     (3776,  F32, 64),
    "w1w":     (4032,  F32, 8),
    "w2w":     (4064,  F32, 8),
    "rwh":     (4096,  BF16, 128),
    "rwf":     (4352,  F32, 128),
    "fconst":  (4864,  F32, 8),
    "xlnT":    (6 * 1024,   BF16, 16 * 1024),
    "attn_out": (134 * 1024, BF16, 16 * 1024),
    "x2c":     (134 * 1024, F32, 2048),
    "xhc":     (142 * 1024, BF16, 2048),
    "xlc":     (146 * 1024, BF16, 2048),
    "xloT":    (150 * 1024, BF16, 2048),
    "WT":      (184 * 1024, F32, 1024),
    "ind":     (188 * 1024, F32, 1024),
    "Wb":      (134 * 1024, BF16, 8192),
    "w1c":     (150 * 1024, BF16, 2 * 2048),
    "w3c":     (158 * 1024, BF16, 2 * 2048),
    "Q":       (38 * 1024,  BF16, 16384),
    "wout":    (6 * 1024,   BF16, 2 * 8192),
    "h_all":   (102 * 1024, BF16, 16384),
    "K":       (70 * 1024,  BF16, 16384),
    "x2T":     (6 * 1024,   BF16, 16384),
    "V":       (102 * 1024, BF16, 16384),
    "x1":      (38 * 1024,  F32, 16384),
    "attnfull": (134 * 1024, BF16, 16384),
    "xc":      (134 * 1024, F32, 2 * 2048),
    "xlnc":    (150 * 1024, BF16, 2 * 2048),
    "junkS":   (158 * 1024, BF16, 2048),
    "junkV":   (162 * 1024, BF16, 2048),
    "wqk":     (166 * 1024, BF16, 2 * 2048),
    "wv":      (174 * 1024, BF16, 8192),
    "E":       (190 * 1024, BF16, 2 * 2048),
    "scr":     (174 * 1024, BF16, 2 * 512),
    "vs":      (176 * 1024, BF16, 2 * 128),
    "w2c":     (166 * 1024, BF16, 2 * 4096),
    "s6":      (182 * 1024, BF16, 2 * 512),
    "bvb":     (190 * 1024, F32, 1024),
    "bout":    (190 * 1024, F32, 2048),
}


def _fix_layout():
    LAYOUT["attnfull"] = (134 * 1024, BF16, 16384)


_fix_layout()

_ESZ = {BF16: 2, F32: 4, U32: 4}


def _check_layout():
    for name, (off, dt, n) in LAYOUT.items():
        assert off % 4 == 0, name
        assert off + n * _ESZ[dt] <= ARENA_BYTES, name


_check_layout()


def build_program(stop_after=None, dbg=False):
    nc = bass.Bass("TRN2", target_bir_lowering=False)

    x_seq = nc.dram_tensor("x_seq", [S, D], F32, kind="ExternalInput")
    x_half = nc.dram_tensor("x_half", [TOK, D], F32, kind="ExternalInput")
    wqk_d = nc.dram_tensor("wqk", [D, 2048], F32, kind="ExternalInput")
    wv_d = nc.dram_tensor("wv", [D, 1024], F32, kind="ExternalInput")
    bqk_d = nc.dram_tensor("bqk", [128, 16], F32, kind="ExternalInput")
    bv_d = nc.dram_tensor("bv", [128, 1024], F32, kind="ExternalInput")
    wout_d = nc.dram_tensor("wout", [D, D], F32, kind="ExternalInput")
    bout_d = nc.dram_tensor("bout", [128, D], F32, kind="ExternalInput")
    rw_d = nc.dram_tensor("rw", [D, NE], F32, kind="ExternalInput")
    ind_d = nc.dram_tensor("ind", [NE, NE * 128], F32, kind="ExternalInput")
    w1_d = nc.dram_tensor("w1", [NE * 4 * D, 128], F32, kind="ExternalInput")
    w3_d = nc.dram_tensor("w3", [NE * 4 * D, 128], F32, kind="ExternalInput")
    w2_d = nc.dram_tensor("w2", [NE * FF, D], F32, kind="ExternalInput")
    offs_d = nc.dram_tensor("offs", [1, 8], U32, kind="ExternalInput")
    out_d = nc.dram_tensor("out", [TOK, D], F32, kind="ExternalOutput")

    swap_inA = nc.dram_tensor("swap_inA", [768, 2048], BF16, kind="Internal")
    swap_outA = nc.dram_tensor("swap_outA", [1536, 2048], BF16,
                               kind="Internal")
    swap_inB = nc.dram_tensor("swap_inB", [256, 2048], BF16, kind="Internal")
    swap_outB = nc.dram_tensor("swap_outB", [512, 2048], BF16,
                               kind="Internal")

    wqk_b = nc.dram_tensor("wqk_b", [D, 2048], BF16, kind="Internal")
    wv_b = nc.dram_tensor("wv_b", [D, 1024], BF16, kind="Internal")
    wout_b = nc.dram_tensor("wout_b", [D, D], BF16, kind="Internal")
    w1_b = nc.dram_tensor("w1_b", [NE * 4 * D, 128], BF16, kind="Internal")
    w3_b = nc.dram_tensor("w3_b", [NE * 4 * D, 128], BF16, kind="Internal")
    w2_b = nc.dram_tensor("w2_b", [NE * FF, D], BF16, kind="Internal")

    dbg_outs = {}

    seq = Seq()
    r_swap = nc.gpsimd.alloc_register("r_swap")

    with ExitStack() as stack:
        arena_t = stack.enter_context(
            nc.sbuf_tensor("arena", [128, ARENA_BYTES // 2], BF16))

        def V(name):
            off, dt, n = LAYOUT[name]
            base = arena_t.ap().bitcast(dt)
            start = off // _ESZ[dt]
            return base[:, start:start + n]

        def dbg_dump(name, src, rk, dt, n):
            if not dbg:
                return
            t = nc.dram_tensor("dbg_" + name, [128, n], dt,
                               kind="ExternalOutput")
            dbg_outs[name] = t
            seq.emit("sync", lambda e, t=t, src=src: e.dma_start(
                out=t.ap(), in_=src), kind="dma", reads=[rk],
                name=f"dbg_{name}")

        psum_a = stack.enter_context(nc.psum_tensor("ps_a", [128, 2048], F32))
        psum_b = stack.enter_context(nc.psum_tensor("ps_b", [128, 2048], F32))
        psum_abf = psum_a.ap().bitcast(BF16)

        def PA(lo_col, n_col):
            b0 = lo_col // 512
            b1 = (lo_col + n_col - 1) // 512
            return [f"pa{b}" for b in range(b0, b1 + 1)]

        def PB(lo_col, n_col):
            b0 = lo_col // 512
            b1 = (lo_col + n_col - 1) // 512
            return [f"pb{b}" for b in range(b0, b1 + 1)]

        ident = V("ident")
        identf = V("identf")
        ones = V("ones")
        onesf = V("onesf")
        trimask = V("trimask")
        seq.emit("gpsimd", lambda e: e.memset(ident[:, :], 0.0),
                 writes=["ident"], name="set_ident0")
        seq.emit("gpsimd", lambda e: e.affine_select(
            out=ident[:, :], in_=ident[:, :], compare_op=ALU.not_equal,
            fill=1.0, base=0, pattern=[[-1, 128]], channel_multiplier=1),
            reads=["ident"], writes=["ident"], name="set_ident1")
        seq.emit("gpsimd", lambda e: e.memset(identf[:, :], 0.0),
                 writes=["identf"], name="set_identf0")
        seq.emit("gpsimd", lambda e: e.affine_select(
            out=identf[:, :], in_=identf[:, :], compare_op=ALU.not_equal,
            fill=1.0, base=0, pattern=[[-1, 128]], channel_multiplier=1),
            reads=["identf"], writes=["identf"], name="set_identf1")
        seq.emit("gpsimd", lambda e: e.memset(ones[:, :], 1.0),
                 writes=["ones"], name="set_ones")
        seq.emit("gpsimd", lambda e: e.memset(onesf[:, :], 1.0),
                 writes=["onesf"], name="set_onesf")
        fc_t = V("fconst")
        seq.emit("gpsimd", lambda e: e.memset(fc_t[:, 0:1], 1e-5),
                 writes=["fconst"], name="set_fc0")
        seq.emit("gpsimd", lambda e: e.memset(
            fc_t[:, 1:2], -0.5 * float(np.log(2048.0))),
            writes=["fconst"], name="set_fc1")
        seq.emit("gpsimd", lambda e: e.memset(trimask[:, :], 1.0),
                 writes=["trimask"], name="set_tri0")
        seq.emit("gpsimd", lambda e: e.affine_select(
            out=trimask[:, :], in_=trimask[:, :], compare_op=ALU.is_ge,
            fill=0.0, base=0, pattern=[[1, 128]], channel_multiplier=-1),
            reads=["trimask"], writes=["trimask"], name="set_tri1")

        bqk = V("bqk")
        seq.emit("scalar", lambda e: e.dma_start(out=bqk[:, :], in_=bqk_d.ap()),
                 kind="dma", writes=["bqk"], name="ld_bqk")
        bvb = V("bvb")
        seq.emit("scalar", lambda e: e.dma_start(out=bvb[:, :], in_=bv_d.ap()),
                 kind="dma", writes=["bvb"], name="ld_bv")
        offs = V("offs")
        seq.emit("gpsimd", lambda e: e.dma_start(out=offs[0:1, :],
                                                 in_=offs_d.ap()),
                 kind="dma", writes=["offs"], name="ld_offs")

        rwhl = V("rwhl")
        rwh = V("rwh")
        rwf = V("rwf")
        rwhl3 = rwhl.rearrange("p (dc x) -> p dc x", dc=16)
        rwh3 = rwh.rearrange("p (dc e) -> p dc e", dc=16)
        rwf3 = rwf.rearrange("p (dc e) -> p dc e", dc=16)
        seq.emit("scalar", lambda e: e.dma_start(
            out=rwf3, in_=rw_d.ap().rearrange("(dc p) e -> p dc e", p=128)),
            kind="dma", writes=["rwf"], name="ld_rwf")
        seq.emit("gpsimd", lambda e: e.dma_start(
            out=rwh3, in_=rw_d.ap().rearrange("(dc p) e -> p dc e", p=128)),
            kind="dma", writes=["rwh"], name="ld_rwh")
        seq.emit("vector", lambda e: e.tensor_copy(rwhl3[:, :, 0:8], rwh3),
                 reads=["rwh"], writes=["rwhl"], name="rw_hi")
        seq.emit("vector", lambda e: e.tensor_tensor(
            rwhl3[:, :, 8:16], rwf3, rwh3, ALU.subtract),
            reads=["rwf", "rwh"], writes=["rwhl"], name="rw_lo")

        rrctr = [0]

        def stage(name, dst, src, rows, row_chunk, lane="A", defer=False):
            th = []
            for i, r0 in enumerate(range(0, rows, row_chunk)):
                r1 = min(rows, r0 + row_chunk)
                if lane == "RR":
                    ln = "ABC"[rrctr[0] % 3]
                    rrctr[0] += 1
                else:
                    ln = lane
                th.append(lambda dst=dst, src=src, r0=r0, r1=r1, i=i, ln=ln:
                          seq.emit("gpsimd",
                                   lambda e, dst=dst, src=src, r0=r0, r1=r1:
                                   e.dma_start(out=dst[r0:r1, :],
                                               in_=src[r0:r1, :]),
                                   kind="dma", writes=[name],
                                   name=f"{name}_{i}", stage_lane=ln))
            if not defer:
                for t in th:
                    t()
            return th

        xlnT = V("xlnT")
        xlnT3 = xlnT.rearrange("p (dc t) -> p dc t", dc=16)
        Q = V("Q")
        K = V("K")
        Vt = V("V")
        stats = V("stats")
        xc = V("xc")
        xlnc = V("xlnc")
        junkS = V("junkS")
        junkV = V("junkV")
        wqk = V("wqk")
        wv = V("wv")
        wv3 = wv.rearrange("p (dc v) -> p dc v", dc=16)
        E = V("E")
        attnfull = V("attnfull")
        af3 = attnfull.rearrange("p (fc t) -> p fc t", fc=16)
        scr = V("scr")
        vs = V("vs")
        cp = V("cp")
        rc = V("rc")
        negM2 = V("negM2")
        nb = V("nb")
        attn_out = V("attn_out")

        wqk_r = wqk_b.ap().rearrange("(dc p) e -> p dc e", p=128)
        wv_r = wv_b.ap().rearrange("(dc p) e -> p dc e", p=128)

        def emit_ln_half(hh):
            for tc in range(8):
                gtc = hh * 8 + tc
                par = tc % 2
                xcp = xc[:, par * 2048:(par + 1) * 2048]
                xck = f"xc{par}"
                stp = stats[:, par * 16:par * 16 + 16]
                stk = f"st{par}"
                xlnp = xlnc[:, par * 2048:(par + 1) * 2048]
                xlnk = f"xlnc{par}"
                trp = psum_abf[:, par * 2048:(par + 1) * 2048]
                trk = [f"pa{2 * par}", f"pa{2 * par + 1}"]

                seq.emit("sync", lambda e, xcp=xcp, gtc=gtc: e.dma_start(
                    out=xcp, in_=x_seq[gtc * 128:(gtc + 1) * 128, :]),
                    kind="dma", writes=[xck], name=f"ldx{gtc}")
                seq.emit("scalar", lambda e, xcp=xcp, stp=stp: e.activation(
                    junkS[:, :], xcp, AF.Square, accum_out=stp[:, 0:1]),
                    reads=[xck], writes=[stk, "junkS"], name=f"sq{gtc}")
                seq.emit("vector", lambda e, xcp=xcp, stp=stp: e.tensor_scalar(
                    junkV[:, :], xcp, 1.0, None, ALU.mult, ALU.add,
                    accum_out=stp[:, 1:2]),
                    reads=[xck], writes=[stk, "junkV"], name=f"sm{gtc}")
                seq.emit("vector", lambda e, stp=stp: e.tensor_scalar(
                    stp[:, 2:3], stp[:, 1:2], 1.0 / D, None, ALU.mult),
                    reads=[stk], writes=[stk], name=f"mean{gtc}")
                seq.emit("vector", lambda e, stp=stp: e.tensor_scalar(
                    stp[:, 3:4], stp[:, 0:1], 1.0 / D, None, ALU.mult),
                    reads=[stk], writes=[stk], name=f"ex2{gtc}")
                seq.emit("vector", lambda e, stp=stp: e.scalar_tensor_tensor(
                    stp[:, 4:5], stp[:, 2:3], -1.0, stp[:, 2:3],
                    ALU.mult, ALU.mult),
                    reads=[stk], writes=[stk], name=f"nm2{gtc}")
                seq.emit("vector", lambda e, stp=stp: e.tensor_tensor(
                    stp[:, 5:6], stp[:, 3:4], stp[:, 4:5], ALU.add),
                    reads=[stk], writes=[stk], name=f"var{gtc}")
                seq.emit("scalar", lambda e, stp=stp: e.activation(
                    stp[:, 6:7], stp[:, 5:6], AF.Ln, bias=fc_t[:, 0:1],
                    scale=1.0),
                    reads=[stk, "fconst"], writes=[stk], name=f"lnv{gtc}")
                seq.emit("scalar", lambda e, stp=stp: e.activation(
                    stp[:, 7:8], stp[:, 6:7], AF.Exp, scale=-0.5),
                    reads=[stk], writes=[stk], name=f"rstd{gtc}")
                seq.emit("vector", lambda e, stp=stp: e.scalar_tensor_tensor(
                    stp[:, 8:9], stp[:, 2:3], -1.0, stp[:, 7:8],
                    ALU.mult, ALU.mult),
                    reads=[stk], writes=[stk], name=f"nmr{gtc}")
                seq.emit("vector", lambda e, xcp=xcp, stp=stp, xlnp=xlnp:
                         e.tensor_scalar(xlnp, xcp, stp[:, 7:8], stp[:, 8:9],
                                         ALU.mult, ALU.add),
                         reads=[xck, stk], writes=[xlnk], name=f"nrm{gtc}")
                for dc in range(16):
                    seq.emit("tensor", lambda e, xlnp=xlnp, trp=trp, dc=dc:
                             e.transpose(trp[:, dc * 128:(dc + 1) * 128],
                                         xlnp[:, dc * 128:(dc + 1) * 128],
                                         ident[:, :]),
                             reads=[xlnk, "ident"], writes=trk,
                             name=f"tr{gtc}_{dc}")
                seq.emit("vector", lambda e, trp=trp, tc=tc: e.tensor_copy(
                    xlnT3[:, :, tc * 128:(tc + 1) * 128],
                    trp.rearrange("p (dc t) -> p dc t", dc=16)),
                    reads=trk, writes=["xlnT"], name=f"trev{gtc}")

        def emit_vproj(hh):
            for vc in range(2):
                seq.emit("sync", lambda e, vc=vc: e.dma_start(
                    out=wv3, in_=wv_r[:, :, vc * 512:(vc + 1) * 512]),
                    kind="dma", reads=["st_wv"], writes=["wv"],
                    name=f"ldwv{hh}_{vc}")
                for tc in range(8):
                    pcol = (tc % 4) * 512
                    pk = PB(pcol, 512)
                    pp = psum_b[:, pcol:pcol + 512]
                    for dc in range(16):
                        seq.emit("tensor", lambda e, pp=pp, dc=dc, tc=tc:
                                 e.matmul(pp,
                                          xlnT3[:, dc, tc * 128:(tc + 1) * 128],
                                          wv3[:, dc, :],
                                          start=(dc == 0), stop=(dc == 15)),
                                 reads=["xlnT", "wv"], writes=pk,
                                 name=f"vmm{hh}_{vc}_{tc}_{dc}")
                    gtc = hh * 8 + tc
                    seq.emit("vector", lambda e, pp=pp, gtc=gtc, vc=vc:
                             e.tensor_tensor(
                                 Vt[:, gtc * 1024 + vc * 512:
                                    gtc * 1024 + vc * 512 + 512],
                                 pp, bvb[:, vc * 512:(vc + 1) * 512], ALU.add),
                             reads=pk + ["bvb"], writes=["V"],
                             name=f"vev{hh}_{vc}_{tc}")

        wqkrot = [0]
        qkrot = [0]

        def qk_thunks(ec, hh, nbanks=2):
            thunks = []
            wpar = wqkrot[0] % 2
            wqkrot[0] += 1
            wqk3 = wqk[:, wpar * 2048:(wpar + 1) * 2048].rearrange(
                "p (dc e) -> p dc e", dc=16)
            wkk = f"wqk{wpar}"
            thunks.append(("safe", lambda: seq.emit(
                "sync", lambda e: e.dma_start(
                    out=wqk3, in_=wqk_r[:, :, ec * 128:(ec + 1) * 128]),
                kind="dma", reads=["st_wqk"], writes=[wkk],
                name=f"ldwqk{hh}_{ec}")))
            dest = Q if ec < 8 else K
            dk = f"Qh{ec}" if ec < 8 else f"Kh{ec - 8}"
            hloc = ec % 8
            for tq in range(2):
                bank = qkrot[0] % nbanks
                qkrot[0] += 1
                pk = [f"pb{bank}"]
                pp = psum_b[:, bank * 512:bank * 512 + 512]
                for d0 in range(0, 16, 4):
                    def mmth(pp=pp, d0=d0, tq=tq, wqk3=wqk3, pk=pk, wkk=wkk):
                        for dc in range(d0, d0 + 4):
                            seq.emit("tensor", lambda e, pp=pp, wqk3=wqk3,
                                     dc=dc, tq=tq: e.matmul(
                                         pp, wqk3[:, dc, :],
                                         xlnT3[:, dc,
                                               tq * 512:(tq + 1) * 512],
                                         start=(dc == 0), stop=(dc == 15)),
                                     reads=["xlnT", wkk], writes=pk,
                                     name=f"qkmm{hh}_{ec}_{tq}_{dc}")
                    thunks.append(("psum", mmth))
                col = hloc * 2048 + hh * 1024 + tq * 512
                thunks.append(("psum", lambda pp=pp, col=col, pk=pk,
                               dest=dest, dk=dk, ec=ec: seq.emit(
                    "vector", lambda e: e.tensor_scalar(
                        dest[:, col:col + 512], pp, bqk[:, ec:ec + 1],
                        None, ALU.add),
                    reads=pk + ["bqk"], writes=[dk],
                    name=f"qkev{hh}_{ec}_{tq}")))
            return thunks

        def bound_thunks(h):
            thunks = []
            ncol = h % 2
            for wi, (buf, bk) in enumerate(((Q, f"Qh{h}"), (K, f"Kh{h}"))):
                for ch in range(4):
                    spar = ch % 2
                    scrp = scr[:, spar * 512:spar * 512 + 512]
                    sk = f"scr{spar}"
                    bank = qkrot[0] % 2
                    qkrot[0] += 1
                    pk = [f"pb{bank}"]
                    pp = psum_b[:, bank * 512:bank * 512 + 512]
                    slot = 4 + (2 * wi + ch % 2)
                    col0 = h * 2048 + ch * 512

                    def bth(buf=buf, bk=bk, scrp=scrp, sk=sk, pp=pp, pk=pk,
                            slot=slot, col0=col0, wi=wi, ch=ch):
                        seq.emit("scalar", lambda e: e.activation(
                            scrp, buf[:, col0:col0 + 512], AF.Square),
                            reads=[bk], writes=[sk], name=f"nbsq{h}_{wi}_{ch}")
                        seq.emit("tensor", lambda e: e.matmul(
                            pp[0:1, :], ones[:, 0:1], scrp, start=True,
                            stop=True),
                            reads=[sk, "ones"], writes=pk,
                            name=f"nbmm{h}_{wi}_{ch}")
                        seq.emit("vector", lambda e: e.reduce_max(
                            nb[0:1, slot:slot + 1], pp[0:1, :], axis=AX.X),
                            reads=pk, writes=["nb"], name=f"nbrm{h}_{wi}_{ch}")
                        if ch == 0:
                            seq.emit("vector", lambda e: e.tensor_copy(
                                nb[0:1, wi:wi + 1], nb[0:1, slot:slot + 1]),
                                reads=["nb"], writes=["nb"],
                                name=f"nbc{h}_{wi}_{ch}")
                        else:
                            seq.emit("vector", lambda e: e.tensor_tensor(
                                nb[0:1, wi:wi + 1], nb[0:1, wi:wi + 1],
                                nb[0:1, slot:slot + 1], ALU.max),
                                reads=["nb"], writes=["nb"],
                                name=f"nbx{h}_{wi}_{ch}")
                    thunks.append(("psum", bth))

            def bfin():
                seq.emit("vector", lambda e: e.tensor_tensor(
                    nb[0:1, 2:3], nb[0:1, 0:1], nb[0:1, 1:2], ALU.mult),
                    reads=["nb"], writes=["nb"], name=f"nbprod{h}")
                seq.emit("scalar", lambda e: e.activation(
                    nb[0:1, 3:4], nb[0:1, 2:3], AF.Ln),
                    reads=["nb"], writes=["nb"], name=f"nbln{h}")
                seq.emit("scalar", lambda e: e.activation(
                    nb[0:1, 8:9], nb[0:1, 3:4], AF.Exp, scale=0.5,
                    bias=fc_t[0:1, 1:2]),
                    reads=["nb", "fconst"], writes=["nb"], name=f"nbexp{h}")
                bank = qkrot[0] % 2
                qkrot[0] += 1
                pk = [f"pb{bank}"]
                pp = psum_b[:, bank * 512:bank * 512 + 1]
                seq.emit("tensor", lambda e, pp=pp: e.matmul(
                    pp, onesf[0:1, :], nb[0:1, 8:9], start=True, stop=True),
                    reads=["nb", "onesf"], writes=pk, name=f"nbbc{h}")
                seq.emit("vector", lambda e, pp=pp: e.tensor_scalar(
                    negM2[:, ncol:ncol + 1], pp, -1.0, None, ALU.mult),
                    reads=pk, writes=[f"negMp{ncol}"], name=f"nbneg{h}")
            thunks.append(("psum", bfin))
            return thunks

        stage("st_wv", wv_b, wv_d, D, 1024, lane="B")
        stage("st_wqk", wqk_b, wqk_d, D, 512, lane="A")

        emit_ln_half(0)
        emit_vproj(0)
        for ec in range(16):
            for tag, t in qk_thunks(ec, 0, nbanks=4):
                t()
        emit_ln_half(1)
        emit_vproj(1)
        for ec in (0, 8):
            for tag, t in qk_thunks(ec, 1, nbanks=4):
                t()
        for tag, t in bound_thunks(0):
            t()

        seq.barrier()

        stage_th = []
        stage_th += stage("st_wout", wout_b, wout_d, D, 512, lane="RR",
                          defer=True)
        stage_th += stage("st_w1", w1_b, w1_d, NE * 4 * D, 8192, lane="RR",
                          defer=True)
        stage_th += stage("st_w3", w3_b, w3_d, NE * 4 * D, 8192, lane="RR",
                          defer=True)
        stage_th += stage("st_w2", w2_b, w2_d, NE * FF, 512, lane="RR",
                          defer=True)
        si = [0]

        def pop_stage(n):
            for _ in range(n):
                if si[0] < len(stage_th):
                    stage_th[si[0]]()
                    si[0] += 1

        x1 = V("x1")
        x13 = x1.rearrange("p (to d) -> p to d", to=8)
        bout = V("bout")
        wout = V("wout")
        wout_r = wout_b.ap().rearrange("(fc p) e -> p fc e", p=128)

        def emit_xhalf(to):
            if to < 4:
                alias = [f"Qh{2 * to}", f"Qh{2 * to + 1}"]
            else:
                alias = [f"Kh{2 * (to - 4)}", f"Kh{2 * (to - 4) + 1}"]
            seq.emit("sync", lambda e, to=to: e.dma_start(
                out=x13[:, to, :], in_=x_half.ap()[to * 128:(to + 1) * 128, :]),
                kind="dma", writes=[f"x1t{to}"] + alias, name=f"ldxh{to}")

        for h in range(HC):
            pop_stage(6 if h < 5 else 0)
            if h == 7:
                seq.emit("gpsimd", lambda e: e.dma_start(
                    out=af3[:, 0:12, :].rearrange("p (b f) t -> p b f t",
                                                  b=2),
                    in_=AP(swap_outA, r_swap,
                           [[2048, 128], [6 * 128 * 2048, 2],
                            [128 * 2048, 6], [1, 1024]])),
                    kind="dma", reads=["swap_outA", "r_swap"],
                    writes=["afA", "ao0", "ao1", "ao2", "ao3", "ao4", "ao5"],
                    name="sw_readA")
                for ec in (0, 1):
                    wo3p = wout[:, (ec % 2) * 8192:
                                 (ec % 2 + 1) * 8192].rearrange(
                        "p (fc e) -> p fc e", fc=16)
                    seq.emit("sync", lambda e, wo3p=wo3p, ec=ec: e.dma_start(
                        out=wo3p, in_=wout_r[:, :, ec * 512:(ec + 1) * 512]),
                        kind="dma", reads=["st_wout"],
                        writes=[f"wout{ec % 2}", "xlnT"],
                        name=f"ldwo_pre{ec}")
                for to_ in (0, 1, 2, 4, 5, 6):
                    emit_xhalf(to_)
            fsafe = []
            fpsum = []
            if h + 1 < HC:
                alls = (qk_thunks(h + 1, 1) + qk_thunks(8 + h + 1, 1)
                        + bound_thunks(h + 1))
                fsafe = [t for tag, t in alls if tag == "safe"]
                fpsum = [t for tag, t in alls if tag == "psum"]
            fis = [0]
            fip = [0]

            def pop_safe(n, fsafe=fsafe, fis=fis):
                for _ in range(n):
                    if fis[0] < len(fsafe):
                        fsafe[fis[0]]()
                        fis[0] += 1

            def pop_psum(n, fpsum=fpsum, fip=fip):
                for _ in range(n):
                    if fip[0] < len(fpsum):
                        fpsum[fip[0]]()
                        fip[0] += 1

            negMh = negM2[:, h % 2:h % 2 + 1]
            nk = f"negMp{h % 2}"

            def half_ranges(kc):
                qlo = 128 * kc
                out = []
                for hhq in range(2):
                    lo = max(qlo, hhq * 1024)
                    hi = (hhq + 1) * 1024
                    if lo < hi:
                        out.append((lo, hi))
                return out

            strip_info = {}
            hsctr = [0]

            def emit_strip(kc, h=h, negMh=negMh, nk=nk):
                qlo = 128 * kc
                epar = kc % 2
                Ep = E[:, epar * 2048:epar * 2048 + 2048]
                ek = f"E{epar}"
                cpb = cp[:, epar * 4:epar * 4 + 4]
                ck = f"cp{epar}"
                nparts = 0
                for (lo, hi) in half_ranges(kc):
                    reg = hsctr[0] % 2
                    hsctr[0] += 1
                    rbase = reg * 1024
                    lo0 = (lo // 512) * 512
                    for qb in range(lo0, hi, 512):
                        pcol = rbase + (qb - lo0)
                        pp = psum_a[:, pcol:pcol + 512]
                        pk = PA(pcol, 512)
                        seq.emit("tensor", lambda e, pp=pp, kc=kc, qb=qb:
                                 e.matmul(pp,
                                          K[:, h * 2048 + kc * 128:
                                             h * 2048 + kc * 128 + 128],
                                          Q[:, h * 2048 + qb:
                                             h * 2048 + qb + 512],
                                          start=True, stop=True),
                                 reads=[f"Qh{h}", f"Kh{h}"], writes=pk,
                                 name=f"sc{h}_{kc}_{qb}")
                    if lo == qlo:
                        pcol = rbase + (lo - lo0)
                        pp = psum_a[:, pcol:pcol + 128]
                        seq.emit("scalar", lambda e, pp=pp, Ep=Ep, lo=lo:
                                 e.activation(Ep[:, lo:lo + 128], pp, AF.Exp,
                                              bias=negMh, scale=RSQ),
                                 reads=PA(pcol, 128) + [nk], writes=[ek],
                                 name=f"expd{h}_{kc}")
                        seq.emit("vector", lambda e, Ep=Ep, lo=lo, cpb=cpb,
                                 np_=nparts: e.scalar_tensor_tensor(
                                     Ep[:, lo:lo + 128], Ep[:, lo:lo + 128],
                                     1.0, trimask[:, :], ALU.bypass, ALU.mult,
                                     accum_out=cpb[:, np_:np_ + 1]),
                                 reads=[ek, "trimask"], writes=[ek, ck],
                                 name=f"mask{h}_{kc}")
                        nparts += 1
                        lo2 = lo + 128
                        if lo2 < hi:
                            pcol2 = rbase + (lo2 - lo0)
                            pp2 = psum_a[:, pcol2:pcol2 + (hi - lo2)]
                            seq.emit("scalar", lambda e, pp2=pp2, Ep=Ep,
                                     lo2=lo2, hi=hi, cpb=cpb, np_=nparts:
                                     e.activation(
                                         Ep[:, lo2:hi], pp2, AF.Exp,
                                         bias=negMh, scale=RSQ,
                                         accum_out=cpb[:, np_:np_ + 1]),
                                     reads=PA(pcol2, hi - lo2) + [nk],
                                     writes=[ek, ck], name=f"expr{h}_{kc}")
                            nparts += 1
                    else:
                        pcol = rbase + (lo - lo0)
                        pp = psum_a[:, pcol:pcol + (hi - lo)]
                        seq.emit("scalar", lambda e, pp=pp, Ep=Ep, lo=lo,
                                 hi=hi, cpb=cpb, np_=nparts:
                                 e.activation(Ep[:, lo:hi], pp, AF.Exp,
                                              bias=negMh, scale=RSQ,
                                              accum_out=cpb[:, np_:np_ + 1]),
                                 reads=PA(pcol, hi - lo) + [nk],
                                 writes=[ek, ck], name=f"exph{h}_{kc}")
                        nparts += 1
                seq.emit("vector", lambda e, cpb=cpb, np_=nparts:
                         e.reduce_sum(cpb[:, 3:4], cpb[:, 0:np_], axis=AX.X),
                         reads=[ck], writes=[ck], name=f"csum{h}_{kc}")
                seq.emit("vector", lambda e, cpb=cpb, rpar=epar:
                         e.reciprocal(rc[:, rpar:rpar + 1], cpb[:, 3:4]),
                         reads=[ck], writes=[f"rc{epar}"],
                         name=f"crec{h}_{kc}")
                vsp = vs[:, epar * 128:epar * 128 + 128]
                seq.emit("vector", lambda e, vsp=vsp, kc=kc, rpar=epar:
                         e.tensor_scalar(vsp,
                                         Vt[:, kc * 1024 + h * 128:
                                            kc * 1024 + h * 128 + 128],
                                         rc[:, rpar:rpar + 1], None, ALU.mult),
                         reads=["V", f"rc{epar}"], writes=[f"vs{epar}"],
                         name=f"vs{h}_{kc}")
                strip_info[kc] = (Ep, ek, epar, vsp)
                if dbg and h == 0 and kc == 0:
                    dbg_dump("E0", Ep, ek, BF16, 2048)
                    dbg_dump("cp0", cpb[:, 0:4], ck, F32, 4)
                    dbg_dump("rc0", rc[:, 0:1], f"rc{epar}", F32, 1)
                    dbg_dump("negM", negMh, nk, F32, 1)
                    dbg_dump("vs0", vsp, f"vs{epar}", BF16, 128)

            def emit_pv(kc, h=h):
                qlo = 128 * kc
                Ep, ek, epar, vsp = strip_info[kc]
                for qs in range(kc // 4, 4):
                    rlo = max(qs * 512, qlo)
                    rhi = (qs + 1) * 512
                    pb = psum_b[:, rlo:rhi]
                    last = (kc == min(15, 4 * qs + 3))
                    seq.emit("tensor", lambda e, pb=pb, vsp=vsp, Ep=Ep,
                             rlo=rlo, rhi=rhi, kc=kc, qs=qs, last=last:
                             e.matmul(pb, vsp, Ep[:, rlo:rhi],
                                      start=(kc == 0), stop=last),
                             reads=[ek, f"vs{epar}"], writes=[f"pb{qs}"],
                             name=f"pv{h}_{kc}_{qs}")
                    if last:
                        seq.emit("scalar", lambda e, h=h, qs=qs: e.activation(
                            attn_out[:, h * 2048 + qs * 512:
                                     h * 2048 + qs * 512 + 512],
                            psum_b[:, qs * 512:(qs + 1) * 512], AF.Copy),
                            reads=[f"pb{qs}"], writes=[f"ao{h}"],
                            name=f"aev{h}_{qs}")

            for kc in range(16):
                emit_strip(kc)
                if kc > 0:
                    emit_pv(kc - 1)
                if kc == 0:
                    pop_safe(4)
                elif kc >= 9:
                    pop_psum(4)
            emit_pv(15)
            pop_safe(99)
            pop_psum(99)
            if h == 5:
                seq.emit("sync", lambda e: e.dma_start(
                    out=swap_inA.ap().rearrange("(hq p) t -> p hq t", p=128),
                    in_=attn_out[:, 0:12288].rearrange(
                        "p (hq t) -> p hq t", hq=6)),
                    kind="dma",
                    reads=["ao0", "ao1", "ao2", "ao3", "ao4", "ao5"],
                    writes=["swap_inA"], name="swb_A")
                seq.emit("gpsimd", lambda e: e.collective_compute(
                    "AllGather", ALU.bypass,
                    replica_groups=[[0, 1], [2, 3], [4, 5], [6, 7]],
                    ins=[swap_inA.ap().opt()], outs=[swap_outA.ap().opt()]),
                    kind="cc", reads=["swap_inA"], writes=["swap_outA"],
                    name="swap_ccA")
                seq.emit("gpsimd", lambda e: e.reg_load(r_swap, offs[0:1, 0:1]),
                         reads=["offs"], writes=["r_swap"], name="ld_rswap")
            if h == 7:
                seq.emit("sync", lambda e: e.dma_start(
                    out=swap_inB.ap().rearrange("(hq p) t -> p hq t", p=128),
                    in_=attn_out[:, 12288:16384].rearrange(
                        "p (hq t) -> p hq t", hq=2)),
                    kind="dma", reads=["ao6", "ao7"],
                    writes=["swap_inB"], name="swb_B")
                seq.emit("gpsimd", lambda e: e.collective_compute(
                    "AllGather", ALU.bypass,
                    replica_groups=[[0, 1], [2, 3], [4, 5], [6, 7]],
                    ins=[swap_inB.ap().opt()], outs=[swap_outB.ap().opt()]),
                    kind="cc", reads=["swap_inB"], writes=["swap_outB"],
                    name="swap_ccB")

        if dbg:
            t_ao = nc.dram_tensor("dbg_attn_out", [128, 16384], BF16,
                                  kind="ExternalOutput")
            seq.emit("sync", lambda e: e.dma_start(out=t_ao.ap(), in_=attn_out),
                     kind="dma", reads=[f"ao{hh_}" for hh_ in range(8)],
                     name="dbg_attn_out")

        if stop_after == "p2":
            _finish(seq)
            seq.link_and_emit(nc, stack)
            return nc

        SLOT = {0: 0, 1: 1, 2: 2, 3: 3, 4: 4, 5: 5,
                8: 6, 9: 7, 10: 8, 11: 9, 12: 10, 13: 11,
                6: 12, 7: 13, 14: 14, 15: 15}

        seq.emit("gpsimd", lambda e: e.dma_start(
            out=af3[:, 12:16, :].rearrange("p (b f) t -> p b f t", b=2),
            in_=AP(swap_outB, r_swap,
                   [[2048, 128], [2 * 128 * 2048, 2], [128 * 2048, 2],
                    [1, 1024]])),
            kind="dma", reads=["swap_outB", "r_swap"],
            writes=["afB", "ao6", "ao7"], name="sw_readB")

        pop_stage(len(stage_th))

        seq.emit("sync", lambda e: e.dma_start(
            out=bout[:, :], in_=bout_d.ap()),
            kind="dma", writes=["boutb", "bvb", "E0", "E1"], name="ld_bout")
        for to_ in (3, 7):
            emit_xhalf(to_)
        for to in range(8):
            seq.emit("vector", lambda e, to=to: e.tensor_tensor(
                x13[:, to, :], x13[:, to, :], bout[:, :], ALU.add),
                reads=[f"x1t{to}", "boutb"], writes=[f"x1t{to}"],
                name=f"x1b{to}")

        for pss in range(2):
            fcs = ([0, 1, 2, 3, 4, 5, 8, 9, 10, 11, 12, 13] if pss == 0
                   else [6, 7, 14, 15])
            afk = "afA" if pss == 0 else "afB"
            for ec in range(4):
                wpar = ec % 2
                wo3 = wout[:, wpar * 8192:(wpar + 1) * 8192].rearrange(
                    "p (fc e) -> p fc e", fc=16)
                wk = f"wout{wpar}"
                if not (pss == 0 and ec < 2):
                    seq.emit("sync", lambda e, wo3=wo3, ec=ec: e.dma_start(
                        out=wo3, in_=wout_r[:, :, ec * 512:(ec + 1) * 512]),
                        kind="dma", reads=["st_wout"], writes=[wk, "xlnT"],
                        name=f"ldwo{pss}_{ec}")
                for to in range(8):
                    pcol = (((pss * 4 + ec) * 8 + to) % 4) * 512
                    pk = PA(pcol, 512)
                    pp = psum_a[:, pcol:pcol + 512]
                    nj = len(fcs)
                    for j, fc in enumerate(fcs):
                        seq.emit("tensor", lambda e, pp=pp, wo3=wo3, fc=fc,
                                 to=to, j=j, nj=nj: e.matmul(
                                     pp,
                                     af3[:, SLOT[fc], to * 128:(to + 1) * 128],
                                     wo3[:, fc, :],
                                     start=(j == 0), stop=(j == nj - 1)),
                                 reads=[afk, wk], writes=pk,
                                 name=f"opmm{pss}_{ec}_{to}_{fc}")
                    seq.emit("vector", lambda e, pp=pp, to=to, ec=ec:
                             e.tensor_tensor(
                                 x13[:, to, ec * 512:(ec + 1) * 512],
                                 x13[:, to, ec * 512:(ec + 1) * 512],
                                 pp, ALU.add),
                             reads=pk + [f"x1t{to}"], writes=[f"x1t{to}"],
                             name=f"opev{pss}_{ec}_{to}")

        dbg_dump("x1", x1, "x1", F32, 16384)


        if stop_after == "p4":
            _finish(seq)
            seq.link_and_emit(nc, stack)
            return nc

        seq.barrier()

        x2c = V("x2c")
        xhc = V("xhc")
        xlc = V("xlc")
        xloT = V("xloT")
        xloT3 = xloT.rearrange("p (dc t) -> p dc t", dc=16)
        x2T = V("x2T")
        x2T3 = x2T.rearrange("p (dc t) -> p dc t", dc=16)
        WT = V("WT")
        ind = V("ind")
        scoresR = V("scoresR")
        Wr = V("W")
        m1v = V("m1s")
        m2v = V("m2s")
        t2a = V("t2a")
        t2b = V("t2b")
        t2c = V("t2c")
        w1w = V("w1w")
        w2w = V("w2w")

        seq.emit("sync", lambda e: e.dma_start(out=ind[0:8, :],
                                                 in_=ind_d.ap()),
                 kind="dma", writes=["ind"], name="ld_ind")

        st5 = stats[:, 0:16]
        for to in range(8):
            seq.emit("scalar", lambda e, to=to: e.activation(
                xhc[:, :], x13[:, to, :], AF.Square, accum_out=st5[:, 0:1]),
                reads=[f"x1t{to}"], writes=["st5", "xhc"], name=f"l2sq{to}")
            seq.emit("vector", lambda e, to=to: e.tensor_scalar(
                xlc[:, :], x13[:, to, :], 1.0, None, ALU.mult, ALU.add,
                accum_out=st5[:, 1:2]),
                reads=[f"x1t{to}"], writes=["st5", "xlc"], name=f"l2sm{to}")
            seq.emit("vector", lambda e: e.tensor_scalar(
                st5[:, 2:3], st5[:, 1:2], 1.0 / D, None, ALU.mult),
                reads=["st5"], writes=["st5"], name=f"l2mean{to}")
            seq.emit("vector", lambda e: e.tensor_scalar(
                st5[:, 3:4], st5[:, 0:1], 1.0 / D, None, ALU.mult),
                reads=["st5"], writes=["st5"], name=f"l2ex2{to}")
            seq.emit("vector", lambda e: e.scalar_tensor_tensor(
                st5[:, 4:5], st5[:, 2:3], -1.0, st5[:, 2:3],
                ALU.mult, ALU.mult),
                reads=["st5"], writes=["st5"], name=f"l2nm2{to}")
            seq.emit("vector", lambda e: e.tensor_tensor(
                st5[:, 5:6], st5[:, 3:4], st5[:, 4:5], ALU.add),
                reads=["st5"], writes=["st5"], name=f"l2var{to}")
            seq.emit("scalar", lambda e: e.activation(
                st5[:, 6:7], st5[:, 5:6], AF.Ln, bias=fc_t[:, 0:1],
                scale=1.0),
                reads=["st5", "fconst"], writes=["st5"], name=f"l2lnv{to}")
            seq.emit("scalar", lambda e: e.activation(
                st5[:, 7:8], st5[:, 6:7], AF.Exp, scale=-0.5),
                reads=["st5"], writes=["st5"], name=f"l2rstd{to}")
            seq.emit("vector", lambda e: e.scalar_tensor_tensor(
                st5[:, 8:9], st5[:, 2:3], -1.0, st5[:, 7:8],
                ALU.mult, ALU.mult),
                reads=["st5"], writes=["st5"], name=f"l2nmr{to}")
            seq.emit("vector", lambda e, to=to: e.tensor_scalar(
                x2c[:, :], x13[:, to, :], st5[:, 7:8], st5[:, 8:9],
                ALU.mult, ALU.add),
                reads=[f"x1t{to}", "st5"], writes=["x2c"], name=f"l2nrm{to}")
            trk = ["pa0", "pa1", "pa2", "pa3"]
            for dc in range(16):
                seq.emit("tensor", lambda e, dc=dc: e.transpose(
                    psum_a[:, dc * 128:(dc + 1) * 128],
                    x2c[:, dc * 128:(dc + 1) * 128], identf[:, :]),
                    reads=["x2c", "identf"], writes=trk,
                    name=f"l2tr{to}_{dc}")
            pa3 = psum_a.ap().rearrange("p (dc t) -> p dc t", dc=16)
            seq.emit("scalar", lambda e, to=to: e.activation(
                x2T3[:, :, to * 128:(to + 1) * 128], pa3, AF.Copy),
                reads=trk, writes=["x2T"], name=f"l2hi{to}")
            seq.emit("vector", lambda e, to=to: e.tensor_tensor(
                xloT3, pa3, x2T3[:, :, to * 128:(to + 1) * 128],
                ALU.subtract),
                reads=trk + ["x2T"], writes=["xloT"], name=f"l2lo{to}")
            for dc in range(16):
                seq.emit("tensor", lambda e, dc=dc, to=to: e.matmul(
                    psum_b[:, 0:16], x2T3[:, dc, to * 128:(to + 1) * 128],
                    rwhl3[:, dc, :], start=(dc == 0), stop=False),
                    reads=["x2T", "rwhl"], writes=["pb0"],
                    name=f"rmmh{to}_{dc}")
                seq.emit("tensor", lambda e, dc=dc: e.matmul(
                    psum_b[:, 0:8], xloT3[:, dc, :], rwhl3[:, dc, 0:8],
                    start=False, stop=(dc == 15)),
                    reads=["xloT", "rwhl"], writes=["pb0"],
                    name=f"rmml{to}_{dc}")
            seq.emit("scalar", lambda e, to=to: e.activation(
                t2a[:, to * 8:(to + 1) * 8], psum_b[:, 0:8], AF.Copy),
                reads=["pb0"], writes=["t2a"], name=f"rsca{to}")
            seq.emit("vector", lambda e, to=to: e.tensor_tensor(
                scoresR[:, to * 8:(to + 1) * 8],
                t2a[:, to * 8:(to + 1) * 8],
                psum_b[:, 8:16], ALU.add),
                reads=["pb0", "t2a"], writes=["scoresR"], name=f"rsc{to}")

        sc3 = scoresR.rearrange("p (t e) -> p t e", t=8)
        t2a3 = t2a.rearrange("p (t e) -> p t e", t=8)
        t2b3 = t2b.rearrange("p (t e) -> p t e", t=8)
        t2c3 = t2c.rearrange("p (t e) -> p t e", t=8)
        Wr3 = Wr.rearrange("p (t e) -> p t e", t=8)

        def bcast8(v):
            return v.rearrange("p (t o) -> p t o", o=1).broadcast_to(
                [128, 8, 8])

        seq.emit("vector", lambda e: e.tensor_reduce(
            m1v.rearrange("p (t o) -> p t o", o=1), sc3, AX.X, ALU.max),
            reads=["scoresR"], writes=["m1s"], name="t2m1")
        seq.emit("vector", lambda e: e.tensor_tensor(
            t2b3, sc3, bcast8(m1v), ALU.is_ge),
            reads=["scoresR", "m1s"], writes=["t2b"], name="t2mask1")
        seq.emit("vector", lambda e: e.scalar_tensor_tensor(
            t2a3, t2b3, -1e30, sc3, ALU.mult, ALU.add),
            reads=["t2b", "scoresR"], writes=["t2a"], name="t2sub")
        seq.emit("vector", lambda e: e.tensor_reduce(
            m2v.rearrange("p (t o) -> p t o", o=1), t2a3, AX.X, ALU.max),
            reads=["t2a"], writes=["m2s"], name="t2m2")
        seq.emit("vector", lambda e: e.tensor_tensor(
            t2c3, t2a3, bcast8(m2v), ALU.is_ge),
            reads=["t2a", "m2s"], writes=["t2c"], name="t2mask2")
        seq.emit("vector", lambda e: e.tensor_tensor(
            m2v[:, :], m2v[:, :], m1v[:, :], ALU.subtract),
            reads=["m2s", "m1s"], writes=["m2s"], name="t2d")
        seq.emit("scalar", lambda e: e.activation(
            w2w[:, :], m2v[:, :], AF.Exp),
            reads=["m2s"], writes=["w2w"], name="t2exp")
        seq.emit("vector", lambda e: e.tensor_scalar(
            w1w[:, :], w2w[:, :], 1.0, None, ALU.add),
            reads=["w2w"], writes=["w1w"], name="t2den")
        seq.emit("vector", lambda e: e.reciprocal(w1w[:, :], w1w[:, :]),
                 reads=["w1w"], writes=["w1w"], name="t2rec")
        seq.emit("vector", lambda e: e.tensor_tensor(
            w2w[:, :], w2w[:, :], w1w[:, :], ALU.mult),
            reads=["w2w", "w1w"], writes=["w2w"], name="t2w2")
        seq.emit("vector", lambda e: e.tensor_tensor(
            t2b3, t2b3, bcast8(w1w), ALU.mult),
            reads=["t2b", "w1w"], writes=["t2b"], name="t2wa")
        seq.emit("vector", lambda e: e.tensor_tensor(
            t2c3, t2c3, bcast8(w2w), ALU.mult),
            reads=["t2c", "w2w"], writes=["t2c"], name="t2wb")
        seq.emit("vector", lambda e: e.tensor_tensor(
            Wr3, t2b3, t2c3, ALU.add),
            reads=["t2b", "t2c"], writes=["Wr"], name="t2W")

        for to in range(8):
            seq.emit("tensor", lambda e, to=to: e.transpose(
                psum_b[0:8, to * 128:to * 128 + 128], Wr3[:, to, :],
                identf[:, :]),
                reads=["Wr", "identf"], writes=PB(to * 128, 128),
                name=f"wtr{to}")
        seq.emit("vector", lambda e: e.tensor_copy(
            WT[0:8, :], psum_b[0:8, 0:1024]),
            reads=PB(0, 1024), writes=["WTb"], name="wtev")

        dbg_dump("Wr", Wr, "Wr", F32, 64)
        dbg_dump("x2T", x2T, "x2T", BF16, 16384)
        dbg_dump("xloT", xloT, "xloT", BF16, 2048)
        dbg_dump("scoresR", scoresR, "scoresR", F32, 64)

        if stop_after == "p5":
            _finish(seq)
            seq.link_and_emit(nc, stack)
            return nc

        seq.barrier()

        Wb = V("Wb")
        h_all = V("h_all")
        h3 = h_all.rearrange("p (e f t) -> p e f t", e=8, f=4)
        w1c = V("w1c")
        w3c = V("w3c")
        w2c = V("w2c")
        s6 = V("s6")
        w1_3 = [w1c[:, par * 2048:(par + 1) * 2048]
                .rearrange("p (dc f) -> p dc f", dc=16) for par in range(2)]
        w3_3 = [w3c[:, par * 2048:(par + 1) * 2048]
                .rearrange("p (dc f) -> p dc f", dc=16) for par in range(2)]
        w2_3 = [w2c[:, par * 4096:(par + 1) * 4096]
                .rearrange("p (fs d) -> p fs d", fs=4) for par in range(2)]

        for e_i in range(NE):
            pcol = (e_i % 2) * 1024
            pk = PA(pcol, 1024)
            pp = psum_a[:, pcol:pcol + 1024]
            for half in range(2):
                seq.emit("tensor", lambda e, pp=pp, e_i=e_i, half=half:
                         e.matmul(pp[:, half * 512:(half + 1) * 512],
                                  ind[0:8, e_i * 128:(e_i + 1) * 128],
                                  WT[0:8, half * 512:(half + 1) * 512],
                                  start=True, stop=True),
                         reads=["WTb", "ind"], writes=pk,
                         name=f"wbmm{e_i}_{half}")
            seq.emit("scalar", lambda e, pp=pp, e_i=e_i: e.activation(
                Wb[:, e_i * 1024:(e_i + 1) * 1024], pp, AF.Copy),
                reads=pk, writes=["Wbb"], name=f"wbev{e_i}")

        for tc in range(2):
            for e_i in range(NE):
                for fc in range(4):
                    par = (e_i * 4 + fc) % 2
                    wk1, wk3 = f"w1c{par}", f"w3c{par}"
                    rb = (e_i * 4 + fc) % 2
                    seq.emit("sync", lambda e, par=par, e_i=e_i, fc=fc:
                             e.dma_start(
                                 out=w1_3[par],
                                 in_=w1_b[(e_i * 4 + fc) * D:
                                          (e_i * 4 + fc + 1) * D, :]
                                 .rearrange("(dc p) f -> p dc f", p=128)),
                             kind="dma", reads=["st_w1"], writes=[wk1],
                             name=f"ldw1_{tc}_{e_i}_{fc}")
                    seq.emit("sync", lambda e, par=par, e_i=e_i, fc=fc:
                             e.dma_start(
                                 out=w3_3[par],
                                 in_=w3_b[(e_i * 4 + fc) * D:
                                          (e_i * 4 + fc + 1) * D, :]
                                 .rearrange("(dc p) f -> p dc f", p=128)),
                             kind="dma", reads=["st_w3"], writes=[wk3],
                             name=f"ldw3_{tc}_{e_i}_{fc}")
                    pa_col = rb * 1024
                    pb_col = rb * 1024 + 512
                    ppa = psum_a[:, pa_col:pa_col + 512]
                    ppb = psum_a[:, pb_col:pb_col + 512]
                    pka = PA(pa_col, 512)
                    pkb = PA(pb_col, 512)
                    for dc in range(16):
                        seq.emit("tensor", lambda e, ppa=ppa, par=par, dc=dc,
                                 tc=tc: e.matmul(
                                     ppa, w1_3[par][:, dc, :],
                                     x2T3[:, dc, tc * 512:(tc + 1) * 512],
                                     start=(dc == 0), stop=(dc == 15)),
                                 reads=["x2T", wk1], writes=pka,
                                 name=f"x1mm{tc}_{e_i}_{fc}_{dc}")
                    for dc in range(16):
                        seq.emit("tensor", lambda e, ppb=ppb, par=par, dc=dc,
                                 tc=tc: e.matmul(
                                     ppb, w3_3[par][:, dc, :],
                                     x2T3[:, dc, tc * 512:(tc + 1) * 512],
                                     start=(dc == 0), stop=(dc == 15)),
                                 reads=["x2T", wk3], writes=pkb,
                                 name=f"x3mm{tc}_{e_i}_{fc}_{dc}")
                    s1 = s6[:, 0:512]
                    s3 = s6[:, 512:1024]
                    seq.emit("scalar", lambda e, ppa=ppa, s1=s1: e.activation(
                        s1, ppa, AF.Silu),
                        reads=pka, writes=["s6a"], name=f"sil{tc}_{e_i}_{fc}")
                    seq.emit("vector", lambda e, ppb=ppb, s3=s3, e_i=e_i,
                             tc=tc: e.tensor_tensor(
                                 s3, ppb,
                                 Wb[:, e_i * 1024 + tc * 512:
                                    e_i * 1024 + tc * 512 + 512],
                                 ALU.mult),
                             reads=pkb + ["Wbb"], writes=["s6b"],
                             name=f"x3s{tc}_{e_i}_{fc}")
                    seq.emit("vector", lambda e, s1=s1, s3=s3, e_i=e_i, fc=fc:
                             e.tensor_tensor(h3[:, e_i, fc, :], s1, s3,
                                             ALU.mult),
                             reads=["s6a", "s6b"], writes=["h_all"],
                             name=f"hm{tc}_{e_i}_{fc}")
            for pp_i in range(2):
                for e_i in range(NE):
                    wpar = e_i % 2
                    wk2 = f"w2c{wpar}"
                    seq.emit("sync", lambda e, wpar=wpar, e_i=e_i,
                             pp_i=pp_i: e.dma_start(
                                 out=w2_3[wpar],
                                 in_=w2_b[e_i * FF:(e_i + 1) * FF, :]
                                 .rearrange("(fs p) d -> p fs d", p=128)
                                 [:, :, pp_i * 1024:(pp_i + 1) * 1024]),
                             kind="dma", reads=["st_w2"], writes=[wk2],
                             name=f"ldw2_{tc}_{pp_i}_{e_i}")
                    for fc in range(4):
                        for ts in range(4):
                            for dh in range(2):
                                reg = ts * 2 + dh
                                if reg < 4:
                                    pr = psum_a[:, reg * 512:reg * 512 + 512]
                                    prk = PA(reg * 512, 512)
                                else:
                                    pr = psum_b[:, (reg - 4) * 512:
                                                (reg - 4) * 512 + 512]
                                    prk = PB((reg - 4) * 512, 512)
                                seq.emit("tensor", lambda e, pr=pr, wpar=wpar,
                                         e_i=e_i, fc=fc, ts=ts, dh=dh:
                                         e.matmul(
                                             pr,
                                             h3[:, e_i, fc,
                                                ts * 128:(ts + 1) * 128],
                                             w2_3[wpar][:, fc,
                                                        dh * 512:
                                                        (dh + 1) * 512],
                                             start=(e_i == 0 and fc == 0),
                                             stop=(e_i == 7 and fc == 3)),
                                         reads=["h_all", wk2], writes=prk,
                                         name=f"o2_{tc}_{pp_i}_{e_i}"
                                              f"_{fc}_{ts}_{dh}")
                for ts in range(4):
                    for dh in range(2):
                        reg = ts * 2 + dh
                        if reg < 4:
                            pr = psum_a[:, reg * 512:reg * 512 + 512]
                            prk = PA(reg * 512, 512)
                        else:
                            pr = psum_b[:, (reg - 4) * 512:
                                        (reg - 4) * 512 + 512]
                            prk = PB((reg - 4) * 512, 512)
                        to = tc * 4 + ts
                        dcol = pp_i * 1024 + dh * 512
                        seq.emit("vector", lambda e, pr=pr, to=to, dcol=dcol:
                                 e.tensor_tensor(
                                     x13[:, to, dcol:dcol + 512],
                                     x13[:, to, dcol:dcol + 512],
                                     pr, ALU.add),
                                 reads=prk + [f"x1t{to}"],
                                 writes=[f"x1t{to}"],
                                 name=f"o2ev{tc}_{pp_i}_{ts}_{dh}")

        seq.emit("sync", lambda e: e.dma_start(
            out=out_d.ap().rearrange("(to p) d -> p to d", p=128), in_=x13),
            kind="dma", reads=[f"x1t{i}" for i in range(8)], writes=["OUT"],
            name="st_out")
        _finish(seq)
        seq.link_and_emit(nc, stack)
    return nc


def _finish(seq):
    dma_ops = [op for op in seq.eng_ops["sync"] if op.kind == "dma"]
    op = seq.emit("sync", lambda e: e.nop(), name="final_wait")
    op.deps.extend((d, True) for d in dma_ops)


def _host_inputs(L0_attn_x, L0_attn_W_qkv, L0_attn_b_qkv, L0_attn_W_out,
                 L0_attn_b_out, L0_ffn_router_w, L0_ffn_w1, L0_ffn_w2,
                 L0_ffn_w3):
    x = np.asarray(L0_attn_x, np.float32)
    Wqkv = np.asarray(L0_attn_W_qkv, np.float32)
    bqkv = np.asarray(L0_attn_b_qkv, np.float32)
    Wout = np.asarray(L0_attn_W_out, np.float32)
    bout = np.asarray(L0_attn_b_out, np.float32)
    rw = np.asarray(L0_ffn_router_w, np.float32)
    w1 = np.asarray(L0_ffn_w1, np.float32)
    w2 = np.asarray(L0_ffn_w2, np.float32)
    w3 = np.asarray(L0_ffn_w3, np.float32)

    wout_h = np.ascontiguousarray(Wout.T)
    bout_h = np.ascontiguousarray(np.broadcast_to(bout, (128, D)))
    rw_h = np.ascontiguousarray(rw.T)
    ind_h = np.zeros((NE, NE * 128), np.float32)
    for e in range(NE):
        ind_h[e, e * 128:(e + 1) * 128] = 1.0
    w1_h = np.ascontiguousarray(
        w1.reshape(NE, D, 4, 128).transpose(0, 2, 1, 3)).reshape(
            NE * 4 * D, 128)
    w3_h = np.ascontiguousarray(
        w3.reshape(NE, D, 4, 128).transpose(0, 2, 1, 3)).reshape(
            NE * 4 * D, 128)
    w2_h = np.ascontiguousarray(w2).reshape(NE * FF, D)

    in_maps = []
    for c in range(8):
        b, g = c // 2, c % 2
        sl = slice(1024 * g, 1024 * g + 1024)
        wqk_h = np.ascontiguousarray(
            np.concatenate([Wqkv[sl],
                            Wqkv[2048 + 1024 * g:2048 + 1024 * g + 1024]],
                           0).T)
        wv_h = np.ascontiguousarray(
            Wqkv[4096 + 1024 * g:4096 + 1024 * g + 1024].T)
        bqk_h = np.ascontiguousarray(
            np.concatenate([bqkv[sl],
                            bqkv[2048 + 1024 * g:2048 + 1024 * g + 1024]]
                           ).reshape(16, 128).T)
        bv_h = np.ascontiguousarray(
            np.broadcast_to(bqkv[4096 + 1024 * g:4096 + 1024 * g + 1024],
                            (128, 1024)))
        offs_h = np.zeros((1, 8), np.uint32)
        offs_h[0, 0] = g * 1024
        in_maps.append({
            "x_seq": np.ascontiguousarray(x[b]),
            "x_half": np.ascontiguousarray(x[b, sl]),
            "wqk": wqk_h, "wv": wv_h, "bqk": bqk_h, "bv": bv_h,
            "wout": wout_h, "bout": bout_h, "rw": rw_h, "ind": ind_h,
            "w1": w1_h, "w3": w3_h, "w2": w2_h, "offs": offs_h,
        })
    return in_maps


def kernel(L0_attn_x, L0_attn_W_qkv, L0_attn_b_qkv, L0_attn_W_out,
           L0_attn_b_out, L0_ffn_router_w, L0_ffn_w1, L0_ffn_w2, L0_ffn_w3):
    in_maps = _host_inputs(L0_attn_x, L0_attn_W_qkv, L0_attn_b_qkv,
                           L0_attn_W_out, L0_attn_b_out, L0_ffn_router_w,
                           L0_ffn_w1, L0_ffn_w2, L0_ffn_w3)
    nc = build_program()
    res = run_bass_kernel_spmd(nc, in_maps, core_ids=list(range(8)))
    y = np.empty((4, S, D), np.float32)
    for c in range(8):
        b, g = c // 2, c % 2
        y[b, 1024 * g:1024 * g + 1024] = np.asarray(res.results[c]["out"],
                                                    np.float32)
    return y


# version 28
# speedup vs baseline: 1.0352x; 1.0352x over previous
import numpy as np
import concourse.bass as bass
import concourse.mybir as mybir
from concourse.bass_utils import run_bass_kernel_spmd
from contextlib import ExitStack

F32 = mybir.dt.float32
BF16 = mybir.dt.bfloat16
U32 = mybir.dt.uint32
AF = mybir.ActivationFunctionType
ALU = mybir.AluOpType
AX = mybir.AxisListType
AP = bass.AP

S = 2048
D = 2048
HC = 8
TOK = 1024
FF = 512
NE = 8
RSQ = 1.0 / float(np.sqrt(2048.0))

ENGINES = ("sync", "scalar", "vector", "tensor", "gpsimd")


class _Op:
    __slots__ = ("engine", "fn", "kind", "name", "eidx", "deps", "cross",
                 "inc_sem", "inc_val", "reach", "waits", "lane")

    def __init__(self):
        self.cross = False
        self.inc_sem = None
        self.reach = None
        self.waits = []


class _SemDep:

    def __init__(self, sem, val):
        self.engine = None
        self.kind = "sem"
        self.reach = (sem, val)
        self.name = "semdep"
        self.cross = True


class Seq:

    def __init__(self):
        self.eng_ops = {e: [] for e in ENGINES}
        self.res = {}
        self.pending_barrier = {e: [] for e in ENGINES}
        self.dma_counters = {e: 0 for e in ENGINES}
        self.last_async = {}

    def emit(self, engine, fn, reads=(), writes=(), kind="comp", name="",
             stage_lane=False):
        op = _Op()
        op.engine = engine
        op.fn = fn
        op.kind = kind
        op.name = name
        op.eidx = len(self.eng_ops[engine])
        self.eng_ops[engine].append(op)
        if kind == "dma":
            if stage_lane:
                op.lane = self.STAGE_LANES[stage_lane]
            else:
                op.lane = self.dma_counters[engine] % 5
                self.dma_counters[engine] += 1
                self.last_async[(engine, op.lane)] = op
        elif kind == "cc":
            self.last_async[("cc",)] = op
        deps = []
        if self.pending_barrier[engine]:
            deps.extend((d, True) for d in self.pending_barrier[engine])
            self.pending_barrier[engine] = []
        for r in reads:
            st = self.res.get(r)
            if st is not None and st[0] is not None:
                deps.append((st[0], True))
        for w in writes:
            st = self.res.get(w)
            if st is not None:
                if st[0] is not None:
                    deps.append((st[0], False))
                deps.extend((d, False) for d in st[1])
        for r in reads:
            st = self.res.setdefault(r, [None, []])
            st[1].append(op)
        for w in writes:
            self.res[w] = [op, []]
        op.deps = deps
        return op

    def barrier(self):
        lasts = [ops[-1] for ops in self.eng_ops.values() if ops]
        lasts.extend(self.last_async.values())
        for e in ENGINES:
            self.pending_barrier[e] = list(lasts)

    N_DMA_LANES = 8
    STAGE_LANES = {"A": 5, "B": 6, "C": 7}

    def link_and_emit(self, nc, stack):
        sems = {}
        for e in ENGINES:
            sems[(e, "comp")] = stack.enter_context(nc.semaphore(f"sc_{e}"))
            for ln in range(self.N_DMA_LANES):
                sems[(e, "dma", ln)] = stack.enter_context(
                    nc.semaphore(f"sd_{e}{ln}"))
        sems[("gpsimd", "cc")] = stack.enter_context(nc.semaphore("s_cc"))

        for e in ENGINES:
            for op in self.eng_ops[e]:
                for d, is_raw in op.deps:
                    if d.engine == op.engine and d.kind == "comp" \
                            and e == "tensor":
                        continue
                    d.cross = True
        for e in ENGINES:
            counts = {"comp": 0, "cc": 0}
            lane_counts = [0] * self.N_DMA_LANES
            stage_set = set(self.STAGE_LANES.values())
            stage_ops = []
            for op in self.eng_ops[e]:
                if op.kind == "dma":
                    ln = op.lane
                    sem = sems[(e, "dma", ln)]
                    if ln in stage_set:
                        if lane_counts[ln] > 0:
                            op.deps.append(
                                (_SemDep(sem, lane_counts[ln]), True))
                        lane_counts[ln] += 16
                        op.inc_sem, op.inc_val = sem, 16
                        op.reach = (sem, lane_counts[ln])
                        continue
                    if lane_counts[ln] > 0:
                        op.deps.append(
                            (_SemDep(sem, lane_counts[ln]), True))
                    lane_counts[ln] += 16
                    op.inc_sem, op.inc_val = sem, 16
                    op.reach = (sem, lane_counts[ln])
                elif op.kind == "cc":
                    counts["cc"] += 1
                    op.inc_sem, op.inc_val = sems[("gpsimd", "cc")], 1
                    op.reach = (sems[("gpsimd", "cc")], counts["cc"])
                elif op.cross:
                    counts["comp"] += 1
                    op.inc_sem, op.inc_val = sems[(e, "comp")], 1
                    op.reach = (sems[(e, "comp")], counts["comp"])
        for e in ENGINES:
            satisfied = {}
            for op in self.eng_ops[e]:
                waits = {}
                for d, is_raw in op.deps:
                    if d.engine == op.engine and d.kind == "comp" \
                            and e == "tensor":
                        continue
                    assert d.reach is not None, (op.name, d.name)
                    sem, val = d.reach
                    if satisfied.get(sem, 0) >= val:
                        continue
                    if waits.get(sem, 0) < val:
                        waits[sem] = val
                op.waits = sorted(waits.items(), key=lambda kv: str(kv[0]))
                for sem, val in waits.items():
                    satisfied[sem] = val

        with nc.Block() as block:
            for ename in ENGINES:
                ops = self.eng_ops[ename]
                if not ops:
                    continue

                def make_body(ops=ops):
                    def body(eng):
                        for op in ops:
                            for sem, val in op.waits:
                                eng.wait_ge(sem, val)
                            inst = op.fn(eng)
                            if op.inc_sem is not None:
                                inst.then_inc(op.inc_sem, op.inc_val)
                    return body

                getattr(block, ename)(make_body())

    def n_ops(self):
        return sum(len(v) for v in self.eng_ops.values())


ARENA_BYTES = 198 * 1024

LAYOUT = {
    "ident":   (0,     BF16, 128),
    "ones":    (256,   BF16, 128),
    "onesf":   (512,   F32, 128),
    "identf":  (1024,  F32, 128),
    "trimask": (1536,  BF16, 128),
    "bqk":     (1792,  F32, 16),
    "stats":   (1856,  F32, 32),
    "cp":      (1984,  F32, 8),
    "rc":      (2016,  F32, 4),
    "negM2":   (2032,  F32, 2),
    "nb":      (2048,  F32, 16),
    "scoresR": (2112,  F32, 64),
    "W":       (2368,  F32, 64),
    "rwhl":    (2624,  BF16, 256),
    "offs":    (3136,  U32, 8),
    "m1s":     (3200,  F32, 8),
    "m2s":     (3232,  F32, 8),
    "t2a":     (3264,  F32, 64),
    "t2b":     (3520,  F32, 64),
    "t2c":     (3776,  F32, 64),
    "w1w":     (4032,  F32, 8),
    "w2w":     (4064,  F32, 8),
    "rwh":     (4096,  BF16, 128),
    "rwf":     (4352,  F32, 128),
    "fconst":  (4864,  F32, 8),
    "xlnT":    (6 * 1024,   BF16, 16 * 1024),
    "attn_out": (134 * 1024, BF16, 16 * 1024),
    "x2c":     (134 * 1024, F32, 2048),
    "xhc":     (142 * 1024, BF16, 2048),
    "xlc":     (146 * 1024, BF16, 2048),
    "xloT":    (150 * 1024, BF16, 2048),
    "WT":      (184 * 1024, F32, 1024),
    "ind":     (188 * 1024, F32, 1024),
    "Wb":      (134 * 1024, BF16, 8192),
    "w1c":     (150 * 1024, BF16, 2 * 2048),
    "w3c":     (158 * 1024, BF16, 2 * 2048),
    "Q":       (38 * 1024,  BF16, 16384),
    "wout":    (6 * 1024,   BF16, 2 * 8192),
    "h_all":   (102 * 1024, BF16, 16384),
    "K":       (70 * 1024,  BF16, 16384),
    "x2T":     (6 * 1024,   BF16, 16384),
    "V":       (102 * 1024, BF16, 16384),
    "x1":      (38 * 1024,  F32, 16384),
    "attnfull": (134 * 1024, BF16, 16384),
    "xc":      (134 * 1024, F32, 2 * 2048),
    "xlnc":    (150 * 1024, BF16, 2 * 2048),
    "junkS":   (158 * 1024, BF16, 2048),
    "junkV":   (162 * 1024, BF16, 2048),
    "wqk":     (166 * 1024, BF16, 2 * 2048),
    "wv":      (174 * 1024, BF16, 8192),
    "E":       (190 * 1024, BF16, 2 * 2048),
    "scr":     (174 * 1024, BF16, 2 * 512),
    "vs":      (176 * 1024, BF16, 2 * 128),
    "w2c":     (166 * 1024, BF16, 2 * 4096),
    "s6":      (182 * 1024, BF16, 2 * 512),
    "bvb":     (190 * 1024, F32, 1024),
    "bout":    (190 * 1024, F32, 2048),
}


def _fix_layout():
    LAYOUT["attnfull"] = (134 * 1024, BF16, 16384)


_fix_layout()

_ESZ = {BF16: 2, F32: 4, U32: 4}


def _check_layout():
    for name, (off, dt, n) in LAYOUT.items():
        assert off % 4 == 0, name
        assert off + n * _ESZ[dt] <= ARENA_BYTES, name


_check_layout()


def build_program(stop_after=None, dbg=False):
    nc = bass.Bass("TRN2", target_bir_lowering=False)

    x_seq = nc.dram_tensor("x_seq", [S, D], F32, kind="ExternalInput")
    x_half = nc.dram_tensor("x_half", [TOK, D], F32, kind="ExternalInput")
    wqk_d = nc.dram_tensor("wqk", [D, 2048], F32, kind="ExternalInput")
    wv_d = nc.dram_tensor("wv", [D, 1024], F32, kind="ExternalInput")
    bqk_d = nc.dram_tensor("bqk", [128, 16], F32, kind="ExternalInput")
    bv_d = nc.dram_tensor("bv", [128, 1024], F32, kind="ExternalInput")
    wout_d = nc.dram_tensor("wout", [D, D], F32, kind="ExternalInput")
    bout_d = nc.dram_tensor("bout", [128, D], F32, kind="ExternalInput")
    rw_d = nc.dram_tensor("rw", [D, NE], F32, kind="ExternalInput")
    ind_d = nc.dram_tensor("ind", [NE, NE * 128], F32, kind="ExternalInput")
    w1_d = nc.dram_tensor("w1", [NE * 4 * D, 128], F32, kind="ExternalInput")
    w3_d = nc.dram_tensor("w3", [NE * 4 * D, 128], F32, kind="ExternalInput")
    w2_d = nc.dram_tensor("w2", [NE * FF, D], F32, kind="ExternalInput")
    offs_d = nc.dram_tensor("offs", [1, 8], U32, kind="ExternalInput")
    out_d = nc.dram_tensor("out", [TOK, D], F32, kind="ExternalOutput")

    swap_inA = nc.dram_tensor("swap_inA", [640, 2048], BF16, kind="Internal")
    swap_outA = nc.dram_tensor("swap_outA", [1280, 2048], BF16,
                               kind="Internal")
    swap_inB = nc.dram_tensor("swap_inB", [384, 2048], BF16, kind="Internal")
    swap_outB = nc.dram_tensor("swap_outB", [768, 2048], BF16,
                               kind="Internal")

    wqk_b = nc.dram_tensor("wqk_b", [D, 2048], BF16, kind="Internal")
    wv_b = nc.dram_tensor("wv_b", [D, 1024], BF16, kind="Internal")
    wout_b = nc.dram_tensor("wout_b", [D, D], BF16, kind="Internal")
    w1_b = nc.dram_tensor("w1_b", [NE * 4 * D, 128], BF16, kind="Internal")
    w3_b = nc.dram_tensor("w3_b", [NE * 4 * D, 128], BF16, kind="Internal")
    w2_b = nc.dram_tensor("w2_b", [NE * FF, D], BF16, kind="Internal")

    dbg_outs = {}

    seq = Seq()
    r_swap = nc.gpsimd.alloc_register("r_swap")

    with ExitStack() as stack:
        arena_t = stack.enter_context(
            nc.sbuf_tensor("arena", [128, ARENA_BYTES // 2], BF16))

        def V(name):
            off, dt, n = LAYOUT[name]
            base = arena_t.ap().bitcast(dt)
            start = off // _ESZ[dt]
            return base[:, start:start + n]

        def dbg_dump(name, src, rk, dt, n):
            if not dbg:
                return
            t = nc.dram_tensor("dbg_" + name, [128, n], dt,
                               kind="ExternalOutput")
            dbg_outs[name] = t
            seq.emit("sync", lambda e, t=t, src=src: e.dma_start(
                out=t.ap(), in_=src), kind="dma", reads=[rk],
                name=f"dbg_{name}")

        psum_a = stack.enter_context(nc.psum_tensor("ps_a", [128, 2048], F32))
        psum_b = stack.enter_context(nc.psum_tensor("ps_b", [128, 2048], F32))
        psum_abf = psum_a.ap().bitcast(BF16)

        def PA(lo_col, n_col):
            b0 = lo_col // 512
            b1 = (lo_col + n_col - 1) // 512
            return [f"pa{b}" for b in range(b0, b1 + 1)]

        def PB(lo_col, n_col):
            b0 = lo_col // 512
            b1 = (lo_col + n_col - 1) // 512
            return [f"pb{b}" for b in range(b0, b1 + 1)]

        ident = V("ident")
        identf = V("identf")
        ones = V("ones")
        onesf = V("onesf")
        trimask = V("trimask")
        seq.emit("gpsimd", lambda e: e.memset(ident[:, :], 0.0),
                 writes=["ident"], name="set_ident0")
        seq.emit("gpsimd", lambda e: e.affine_select(
            out=ident[:, :], in_=ident[:, :], compare_op=ALU.not_equal,
            fill=1.0, base=0, pattern=[[-1, 128]], channel_multiplier=1),
            reads=["ident"], writes=["ident"], name="set_ident1")
        seq.emit("gpsimd", lambda e: e.memset(identf[:, :], 0.0),
                 writes=["identf"], name="set_identf0")
        seq.emit("gpsimd", lambda e: e.affine_select(
            out=identf[:, :], in_=identf[:, :], compare_op=ALU.not_equal,
            fill=1.0, base=0, pattern=[[-1, 128]], channel_multiplier=1),
            reads=["identf"], writes=["identf"], name="set_identf1")
        seq.emit("gpsimd", lambda e: e.memset(ones[:, :], 1.0),
                 writes=["ones"], name="set_ones")
        seq.emit("gpsimd", lambda e: e.memset(onesf[:, :], 1.0),
                 writes=["onesf"], name="set_onesf")
        fc_t = V("fconst")
        seq.emit("gpsimd", lambda e: e.memset(fc_t[:, 0:1], 1e-5),
                 writes=["fconst"], name="set_fc0")
        seq.emit("gpsimd", lambda e: e.memset(
            fc_t[:, 1:2], -0.5 * float(np.log(2048.0))),
            writes=["fconst"], name="set_fc1")
        seq.emit("gpsimd", lambda e: e.memset(trimask[:, :], 1.0),
                 writes=["trimask"], name="set_tri0")
        seq.emit("gpsimd", lambda e: e.affine_select(
            out=trimask[:, :], in_=trimask[:, :], compare_op=ALU.is_ge,
            fill=0.0, base=0, pattern=[[1, 128]], channel_multiplier=-1),
            reads=["trimask"], writes=["trimask"], name="set_tri1")

        bqk = V("bqk")
        seq.emit("scalar", lambda e: e.dma_start(out=bqk[:, :], in_=bqk_d.ap()),
                 kind="dma", writes=["bqk"], name="ld_bqk")
        bvb = V("bvb")
        seq.emit("scalar", lambda e: e.dma_start(out=bvb[:, :], in_=bv_d.ap()),
                 kind="dma", writes=["bvb"], name="ld_bv")
        offs = V("offs")
        seq.emit("gpsimd", lambda e: e.dma_start(out=offs[0:1, :],
                                                 in_=offs_d.ap()),
                 kind="dma", writes=["offs"], name="ld_offs")

        rwhl = V("rwhl")
        rwh = V("rwh")
        rwf = V("rwf")
        rwhl3 = rwhl.rearrange("p (dc x) -> p dc x", dc=16)
        rwh3 = rwh.rearrange("p (dc e) -> p dc e", dc=16)
        rwf3 = rwf.rearrange("p (dc e) -> p dc e", dc=16)
        seq.emit("scalar", lambda e: e.dma_start(
            out=rwf3, in_=rw_d.ap().rearrange("(dc p) e -> p dc e", p=128)),
            kind="dma", writes=["rwf"], name="ld_rwf")
        seq.emit("gpsimd", lambda e: e.dma_start(
            out=rwh3, in_=rw_d.ap().rearrange("(dc p) e -> p dc e", p=128)),
            kind="dma", writes=["rwh"], name="ld_rwh")
        seq.emit("vector", lambda e: e.tensor_copy(rwhl3[:, :, 0:8], rwh3),
                 reads=["rwh"], writes=["rwhl"], name="rw_hi")
        seq.emit("vector", lambda e: e.tensor_tensor(
            rwhl3[:, :, 8:16], rwf3, rwh3, ALU.subtract),
            reads=["rwf", "rwh"], writes=["rwhl"], name="rw_lo")

        rrctr = [0]

        def stage(name, dst, src, rows, row_chunk, lane="A", defer=False):
            th = []
            for i, r0 in enumerate(range(0, rows, row_chunk)):
                r1 = min(rows, r0 + row_chunk)
                if lane == "RR":
                    ln = "ABC"[rrctr[0] % 3]
                    rrctr[0] += 1
                else:
                    ln = lane
                th.append(lambda dst=dst, src=src, r0=r0, r1=r1, i=i, ln=ln:
                          seq.emit("gpsimd",
                                   lambda e, dst=dst, src=src, r0=r0, r1=r1:
                                   e.dma_start(out=dst[r0:r1, :],
                                               in_=src[r0:r1, :]),
                                   kind="dma", writes=[name],
                                   name=f"{name}_{i}", stage_lane=ln))
            if not defer:
                for t in th:
                    t()
            return th

        xlnT = V("xlnT")
        xlnT3 = xlnT.rearrange("p (dc t) -> p dc t", dc=16)
        Q = V("Q")
        K = V("K")
        Vt = V("V")
        stats = V("stats")
        xc = V("xc")
        xlnc = V("xlnc")
        junkS = V("junkS")
        junkV = V("junkV")
        wqk = V("wqk")
        wv = V("wv")
        wv3 = wv.rearrange("p (dc v) -> p dc v", dc=16)
        E = V("E")
        attnfull = V("attnfull")
        af3 = attnfull.rearrange("p (fc t) -> p fc t", fc=16)
        scr = V("scr")
        vs = V("vs")
        cp = V("cp")
        rc = V("rc")
        negM2 = V("negM2")
        nb = V("nb")
        attn_out = V("attn_out")

        wqk_r = wqk_b.ap().rearrange("(dc p) e -> p dc e", p=128)
        wv_r = wv_b.ap().rearrange("(dc p) e -> p dc e", p=128)

        def emit_ln_half(hh):
            for tc in range(8):
                gtc = hh * 8 + tc
                par = tc % 2
                xcp = xc[:, par * 2048:(par + 1) * 2048]
                xck = f"xc{par}"
                stp = stats[:, par * 16:par * 16 + 16]
                stk = f"st{par}"
                xlnp = xlnc[:, par * 2048:(par + 1) * 2048]
                xlnk = f"xlnc{par}"
                trp = psum_abf[:, par * 2048:(par + 1) * 2048]
                trk = [f"pa{2 * par}", f"pa{2 * par + 1}"]

                seq.emit("sync", lambda e, xcp=xcp, gtc=gtc: e.dma_start(
                    out=xcp, in_=x_seq[gtc * 128:(gtc + 1) * 128, :]),
                    kind="dma", writes=[xck], name=f"ldx{gtc}")
                seq.emit("scalar", lambda e, xcp=xcp, stp=stp: e.activation(
                    junkS[:, :], xcp, AF.Square, accum_out=stp[:, 0:1]),
                    reads=[xck], writes=[stk, "junkS"], name=f"sq{gtc}")
                seq.emit("vector", lambda e, xcp=xcp, stp=stp: e.tensor_scalar(
                    junkV[:, :], xcp, 1.0, None, ALU.mult, ALU.add,
                    accum_out=stp[:, 1:2]),
                    reads=[xck], writes=[stk, "junkV"], name=f"sm{gtc}")
                seq.emit("vector", lambda e, stp=stp: e.tensor_scalar(
                    stp[:, 2:3], stp[:, 1:2], 1.0 / D, None, ALU.mult),
                    reads=[stk], writes=[stk], name=f"mean{gtc}")
                seq.emit("vector", lambda e, stp=stp: e.tensor_scalar(
                    stp[:, 3:4], stp[:, 0:1], 1.0 / D, None, ALU.mult),
                    reads=[stk], writes=[stk], name=f"ex2{gtc}")
                seq.emit("vector", lambda e, stp=stp: e.scalar_tensor_tensor(
                    stp[:, 4:5], stp[:, 2:3], -1.0, stp[:, 2:3],
                    ALU.mult, ALU.mult),
                    reads=[stk], writes=[stk], name=f"nm2{gtc}")
                seq.emit("vector", lambda e, stp=stp: e.tensor_tensor(
                    stp[:, 5:6], stp[:, 3:4], stp[:, 4:5], ALU.add),
                    reads=[stk], writes=[stk], name=f"var{gtc}")
                seq.emit("scalar", lambda e, stp=stp: e.activation(
                    stp[:, 6:7], stp[:, 5:6], AF.Ln, bias=fc_t[:, 0:1],
                    scale=1.0),
                    reads=[stk, "fconst"], writes=[stk], name=f"lnv{gtc}")
                seq.emit("scalar", lambda e, stp=stp: e.activation(
                    stp[:, 7:8], stp[:, 6:7], AF.Exp, scale=-0.5),
                    reads=[stk], writes=[stk], name=f"rstd{gtc}")
                seq.emit("vector", lambda e, stp=stp: e.scalar_tensor_tensor(
                    stp[:, 8:9], stp[:, 2:3], -1.0, stp[:, 7:8],
                    ALU.mult, ALU.mult),
                    reads=[stk], writes=[stk], name=f"nmr{gtc}")
                seq.emit("vector", lambda e, xcp=xcp, stp=stp, xlnp=xlnp:
                         e.tensor_scalar(xlnp, xcp, stp[:, 7:8], stp[:, 8:9],
                                         ALU.mult, ALU.add),
                         reads=[xck, stk], writes=[xlnk], name=f"nrm{gtc}")
                for dc in range(16):
                    seq.emit("tensor", lambda e, xlnp=xlnp, trp=trp, dc=dc:
                             e.transpose(trp[:, dc * 128:(dc + 1) * 128],
                                         xlnp[:, dc * 128:(dc + 1) * 128],
                                         ident[:, :]),
                             reads=[xlnk, "ident"], writes=trk,
                             name=f"tr{gtc}_{dc}")
                seq.emit("vector", lambda e, trp=trp, tc=tc: e.tensor_copy(
                    xlnT3[:, :, tc * 128:(tc + 1) * 128],
                    trp.rearrange("p (dc t) -> p dc t", dc=16)),
                    reads=trk, writes=["xlnT"], name=f"trev{gtc}")

        def emit_vproj(hh):
            for vc in range(2):
                seq.emit("sync", lambda e, vc=vc: e.dma_start(
                    out=wv3, in_=wv_r[:, :, vc * 512:(vc + 1) * 512]),
                    kind="dma", reads=["st_wv"], writes=["wv"],
                    name=f"ldwv{hh}_{vc}")
                for tc in range(8):
                    pcol = (tc % 2) * 512
                    pk = PB(pcol, 512)
                    pp = psum_b[:, pcol:pcol + 512]
                    for dc in range(16):
                        seq.emit("tensor", lambda e, pp=pp, dc=dc, tc=tc:
                                 e.matmul(pp,
                                          xlnT3[:, dc, tc * 128:(tc + 1) * 128],
                                          wv3[:, dc, :],
                                          start=(dc == 0), stop=(dc == 15)),
                                 reads=["xlnT", "wv"], writes=pk,
                                 name=f"vmm{hh}_{vc}_{tc}_{dc}")
                    gtc = hh * 8 + tc
                    seq.emit("vector", lambda e, pp=pp, gtc=gtc, vc=vc:
                             e.tensor_tensor(
                                 Vt[:, gtc * 1024 + vc * 512:
                                    gtc * 1024 + vc * 512 + 512],
                                 pp, bvb[:, vc * 512:(vc + 1) * 512], ALU.add),
                             reads=pk + ["bvb"], writes=["V"],
                             name=f"vev{hh}_{vc}_{tc}")

        wqkrot = [0]
        qkrot = [0]

        def qk_thunks(ec, hh):
            thunks = []
            wpar = wqkrot[0] % 2
            wqkrot[0] += 1
            wqk3 = wqk[:, wpar * 2048:(wpar + 1) * 2048].rearrange(
                "p (dc e) -> p dc e", dc=16)
            wkk = f"wqk{wpar}"
            thunks.append(("safe", lambda: seq.emit(
                "sync", lambda e: e.dma_start(
                    out=wqk3, in_=wqk_r[:, :, ec * 128:(ec + 1) * 128]),
                kind="dma", reads=["st_wqk"], writes=[wkk],
                name=f"ldwqk{hh}_{ec}")))
            dest = Q if ec < 8 else K
            dk = f"Qh{ec}" if ec < 8 else f"Kh{ec - 8}"
            hloc = ec % 8
            for tq in range(2):
                bank = qkrot[0] % 2
                qkrot[0] += 1
                pk = [f"pb{bank}"]
                pp = psum_b[:, bank * 512:bank * 512 + 512]
                for d0 in range(0, 16, 4):
                    def mmth(pp=pp, d0=d0, tq=tq, wqk3=wqk3, pk=pk, wkk=wkk):
                        for dc in range(d0, d0 + 4):
                            seq.emit("tensor", lambda e, pp=pp, wqk3=wqk3,
                                     dc=dc, tq=tq: e.matmul(
                                         pp, wqk3[:, dc, :],
                                         xlnT3[:, dc,
                                               tq * 512:(tq + 1) * 512],
                                         start=(dc == 0), stop=(dc == 15)),
                                     reads=["xlnT", wkk], writes=pk,
                                     name=f"qkmm{hh}_{ec}_{tq}_{dc}")
                    thunks.append(("psum", mmth))
                col = hloc * 2048 + hh * 1024 + tq * 512
                thunks.append(("psum", lambda pp=pp, col=col, pk=pk,
                               dest=dest, dk=dk, ec=ec: seq.emit(
                    "vector", lambda e: e.tensor_scalar(
                        dest[:, col:col + 512], pp, bqk[:, ec:ec + 1],
                        None, ALU.add),
                    reads=pk + ["bqk"], writes=[dk],
                    name=f"qkev{hh}_{ec}_{tq}")))
            return thunks

        def bound_thunks(h):
            thunks = []
            ncol = h % 2
            for wi, (buf, bk) in enumerate(((Q, f"Qh{h}"), (K, f"Kh{h}"))):
                for ch in range(4):
                    spar = ch % 2
                    scrp = scr[:, spar * 512:spar * 512 + 512]
                    sk = f"scr{spar}"
                    bank = qkrot[0] % 2
                    qkrot[0] += 1
                    pk = [f"pb{bank}"]
                    pp = psum_b[:, bank * 512:bank * 512 + 512]
                    slot = 4 + (2 * wi + ch % 2)
                    col0 = h * 2048 + ch * 512

                    def bth(buf=buf, bk=bk, scrp=scrp, sk=sk, pp=pp, pk=pk,
                            slot=slot, col0=col0, wi=wi, ch=ch):
                        seq.emit("scalar", lambda e: e.activation(
                            scrp, buf[:, col0:col0 + 512], AF.Square),
                            reads=[bk], writes=[sk], name=f"nbsq{h}_{wi}_{ch}")
                        seq.emit("tensor", lambda e: e.matmul(
                            pp[0:1, :], ones[:, 0:1], scrp, start=True,
                            stop=True),
                            reads=[sk, "ones"], writes=pk,
                            name=f"nbmm{h}_{wi}_{ch}")
                        seq.emit("vector", lambda e: e.reduce_max(
                            nb[0:1, slot:slot + 1], pp[0:1, :], axis=AX.X),
                            reads=pk, writes=["nb"], name=f"nbrm{h}_{wi}_{ch}")
                        if ch == 0:
                            seq.emit("vector", lambda e: e.tensor_copy(
                                nb[0:1, wi:wi + 1], nb[0:1, slot:slot + 1]),
                                reads=["nb"], writes=["nb"],
                                name=f"nbc{h}_{wi}_{ch}")
                        else:
                            seq.emit("vector", lambda e: e.tensor_tensor(
                                nb[0:1, wi:wi + 1], nb[0:1, wi:wi + 1],
                                nb[0:1, slot:slot + 1], ALU.max),
                                reads=["nb"], writes=["nb"],
                                name=f"nbx{h}_{wi}_{ch}")
                    thunks.append(("psum", bth))

            def bfin():
                seq.emit("vector", lambda e: e.tensor_tensor(
                    nb[0:1, 2:3], nb[0:1, 0:1], nb[0:1, 1:2], ALU.mult),
                    reads=["nb"], writes=["nb"], name=f"nbprod{h}")
                seq.emit("scalar", lambda e: e.activation(
                    nb[0:1, 3:4], nb[0:1, 2:3], AF.Ln),
                    reads=["nb"], writes=["nb"], name=f"nbln{h}")
                seq.emit("scalar", lambda e: e.activation(
                    nb[0:1, 8:9], nb[0:1, 3:4], AF.Exp, scale=0.5,
                    bias=fc_t[0:1, 1:2]),
                    reads=["nb", "fconst"], writes=["nb"], name=f"nbexp{h}")
                bank = qkrot[0] % 2
                qkrot[0] += 1
                pk = [f"pb{bank}"]
                pp = psum_b[:, bank * 512:bank * 512 + 1]
                seq.emit("tensor", lambda e, pp=pp: e.matmul(
                    pp, onesf[0:1, :], nb[0:1, 8:9], start=True, stop=True),
                    reads=["nb", "onesf"], writes=pk, name=f"nbbc{h}")
                seq.emit("vector", lambda e, pp=pp: e.tensor_scalar(
                    negM2[:, ncol:ncol + 1], pp, -1.0, None, ALU.mult),
                    reads=pk, writes=[f"negMp{ncol}"], name=f"nbneg{h}")
            thunks.append(("psum", bfin))
            return thunks

        stage("st_wv", wv_b, wv_d, D, 1024, lane="B")
        stage("st_wqk", wqk_b, wqk_d, D, 512, lane="A")

        emit_ln_half(0)
        emit_vproj(0)
        for ec in range(16):
            for tag, t in qk_thunks(ec, 0):
                t()
        emit_ln_half(1)
        emit_vproj(1)
        for ec in (0, 8):
            for tag, t in qk_thunks(ec, 1):
                t()
        for tag, t in bound_thunks(0):
            t()

        seq.barrier()

        stage_th = []
        stage_th += stage("st_wout", wout_b, wout_d, D, 512, lane="RR",
                          defer=True)
        stage_th += stage("st_w1", w1_b, w1_d, NE * 4 * D, 8192, lane="RR",
                          defer=True)
        stage_th += stage("st_w3", w3_b, w3_d, NE * 4 * D, 8192, lane="RR",
                          defer=True)
        st_after = stage("st_w2", w2_b, w2_d, NE * FF, 512, lane="RR",
                         defer=True)
        si = [0]

        def pop_stage(n):
            for _ in range(n):
                if si[0] < len(stage_th):
                    stage_th[si[0]]()
                    si[0] += 1

        x1 = V("x1")
        x13 = x1.rearrange("p (to d) -> p to d", to=8)
        bout = V("bout")
        wout = V("wout")
        wout_r = wout_b.ap().rearrange("(fc p) e -> p fc e", p=128)

        def emit_xhalf(to):
            if to < 4:
                alias = [f"Qh{2 * to}", f"Qh{2 * to + 1}"]
            else:
                alias = [f"Kh{2 * (to - 4)}", f"Kh{2 * (to - 4) + 1}"]
            seq.emit("sync", lambda e, to=to: e.dma_start(
                out=x13[:, to, :], in_=x_half.ap()[to * 128:(to + 1) * 128, :]),
                kind="dma", writes=[f"x1t{to}"] + alias, name=f"ldxh{to}")

        for h in range(HC):
            pop_stage(7 if h < 3 else 0)
            if h == 6:
                seq.emit("gpsimd", lambda e: e.dma_start(
                    out=af3[:, 0:10, :].rearrange("p (b f) t -> p b f t",
                                                  b=2),
                    in_=AP(swap_outA, r_swap,
                           [[2048, 128], [5 * 128 * 2048, 2],
                            [128 * 2048, 5], [1, 1024]])),
                    kind="dma", reads=["swap_outA", "r_swap"],
                    writes=["afA", "ao0", "ao1", "ao2", "ao3", "ao4"],
                    name="sw_readA")
                for ec in (0, 1):
                    wo3p = wout[:, (ec % 2) * 8192:
                                 (ec % 2 + 1) * 8192].rearrange(
                        "p (fc e) -> p fc e", fc=16)
                    seq.emit("sync", lambda e, wo3p=wo3p, ec=ec: e.dma_start(
                        out=wo3p, in_=wout_r[:, :, ec * 512:(ec + 1) * 512]),
                        kind="dma", reads=["st_wout"],
                        writes=[f"wout{ec % 2}", "xlnT"],
                        name=f"ldwo_pre{ec}")
                for to_ in (0, 1, 2, 4, 5, 6):
                    emit_xhalf(to_)
            fsafe = []
            fpsum = []
            if h + 1 < HC:
                alls = (qk_thunks(h + 1, 1) + qk_thunks(8 + h + 1, 1)
                        + bound_thunks(h + 1))
                fsafe = [t for tag, t in alls if tag == "safe"]
                fpsum = [t for tag, t in alls if tag == "psum"]
            fis = [0]
            fip = [0]

            def pop_safe(n, fsafe=fsafe, fis=fis):
                for _ in range(n):
                    if fis[0] < len(fsafe):
                        fsafe[fis[0]]()
                        fis[0] += 1

            def pop_psum(n, fpsum=fpsum, fip=fip):
                for _ in range(n):
                    if fip[0] < len(fpsum):
                        fpsum[fip[0]]()
                        fip[0] += 1

            negMh = negM2[:, h % 2:h % 2 + 1]
            nk = f"negMp{h % 2}"

            def half_ranges(kc):
                qlo = 128 * kc
                out = []
                for hhq in range(2):
                    lo = max(qlo, hhq * 1024)
                    hi = (hhq + 1) * 1024
                    if lo < hi:
                        out.append((lo, hi))
                return out

            strip_info = {}
            hsctr = [0]

            def emit_strip(kc, h=h, negMh=negMh, nk=nk):
                qlo = 128 * kc
                epar = kc % 2
                Ep = E[:, epar * 2048:epar * 2048 + 2048]
                ek = f"E{epar}"
                cpb = cp[:, epar * 4:epar * 4 + 4]
                ck = f"cp{epar}"
                nparts = 0
                for (lo, hi) in half_ranges(kc):
                    reg = hsctr[0] % 2
                    hsctr[0] += 1
                    rbase = reg * 1024
                    lo0 = (lo // 512) * 512
                    for qb in range(lo0, hi, 512):
                        pcol = rbase + (qb - lo0)
                        pp = psum_a[:, pcol:pcol + 512]
                        pk = PA(pcol, 512)
                        seq.emit("tensor", lambda e, pp=pp, kc=kc, qb=qb:
                                 e.matmul(pp,
                                          K[:, h * 2048 + kc * 128:
                                             h * 2048 + kc * 128 + 128],
                                          Q[:, h * 2048 + qb:
                                             h * 2048 + qb + 512],
                                          start=True, stop=True),
                                 reads=[f"Qh{h}", f"Kh{h}"], writes=pk,
                                 name=f"sc{h}_{kc}_{qb}")
                    if lo == qlo:
                        pcol = rbase + (lo - lo0)
                        pp = psum_a[:, pcol:pcol + 128]
                        seq.emit("scalar", lambda e, pp=pp, Ep=Ep, lo=lo:
                                 e.activation(Ep[:, lo:lo + 128], pp, AF.Exp,
                                              bias=negMh, scale=RSQ),
                                 reads=PA(pcol, 128) + [nk], writes=[ek],
                                 name=f"expd{h}_{kc}")
                        seq.emit("vector", lambda e, Ep=Ep, lo=lo, cpb=cpb,
                                 np_=nparts: e.scalar_tensor_tensor(
                                     Ep[:, lo:lo + 128], Ep[:, lo:lo + 128],
                                     1.0, trimask[:, :], ALU.bypass, ALU.mult,
                                     accum_out=cpb[:, np_:np_ + 1]),
                                 reads=[ek, "trimask"], writes=[ek, ck],
                                 name=f"mask{h}_{kc}")
                        nparts += 1
                        lo2 = lo + 128
                        if lo2 < hi:
                            pcol2 = rbase + (lo2 - lo0)
                            pp2 = psum_a[:, pcol2:pcol2 + (hi - lo2)]
                            seq.emit("scalar", lambda e, pp2=pp2, Ep=Ep,
                                     lo2=lo2, hi=hi, cpb=cpb, np_=nparts:
                                     e.activation(
                                         Ep[:, lo2:hi], pp2, AF.Exp,
                                         bias=negMh, scale=RSQ,
                                         accum_out=cpb[:, np_:np_ + 1]),
                                     reads=PA(pcol2, hi - lo2) + [nk],
                                     writes=[ek, ck], name=f"expr{h}_{kc}")
                            nparts += 1
                    else:
                        pcol = rbase + (lo - lo0)
                        pp = psum_a[:, pcol:pcol + (hi - lo)]
                        seq.emit("scalar", lambda e, pp=pp, Ep=Ep, lo=lo,
                                 hi=hi, cpb=cpb, np_=nparts:
                                 e.activation(Ep[:, lo:hi], pp, AF.Exp,
                                              bias=negMh, scale=RSQ,
                                              accum_out=cpb[:, np_:np_ + 1]),
                                 reads=PA(pcol, hi - lo) + [nk],
                                 writes=[ek, ck], name=f"exph{h}_{kc}")
                        nparts += 1
                seq.emit("vector", lambda e, cpb=cpb, np_=nparts:
                         e.reduce_sum(cpb[:, 3:4], cpb[:, 0:np_], axis=AX.X),
                         reads=[ck], writes=[ck], name=f"csum{h}_{kc}")
                seq.emit("vector", lambda e, cpb=cpb, rpar=epar:
                         e.reciprocal(rc[:, rpar:rpar + 1], cpb[:, 3:4]),
                         reads=[ck], writes=[f"rc{epar}"],
                         name=f"crec{h}_{kc}")
                vsp = vs[:, epar * 128:epar * 128 + 128]
                seq.emit("vector", lambda e, vsp=vsp, kc=kc, rpar=epar:
                         e.tensor_scalar(vsp,
                                         Vt[:, kc * 1024 + h * 128:
                                            kc * 1024 + h * 128 + 128],
                                         rc[:, rpar:rpar + 1], None, ALU.mult),
                         reads=["V", f"rc{epar}"], writes=[f"vs{epar}"],
                         name=f"vs{h}_{kc}")
                strip_info[kc] = (Ep, ek, epar, vsp)
                if dbg and h == 0 and kc == 0:
                    dbg_dump("E0", Ep, ek, BF16, 2048)
                    dbg_dump("cp0", cpb[:, 0:4], ck, F32, 4)
                    dbg_dump("rc0", rc[:, 0:1], f"rc{epar}", F32, 1)
                    dbg_dump("negM", negMh, nk, F32, 1)
                    dbg_dump("vs0", vsp, f"vs{epar}", BF16, 128)

            def emit_pv(kc, h=h):
                qlo = 128 * kc
                Ep, ek, epar, vsp = strip_info[kc]
                for qs in range(kc // 4, 4):
                    rlo = max(qs * 512, qlo)
                    rhi = (qs + 1) * 512
                    pb = psum_b[:, rlo:rhi]
                    last = (kc == min(15, 4 * qs + 3))
                    seq.emit("tensor", lambda e, pb=pb, vsp=vsp, Ep=Ep,
                             rlo=rlo, rhi=rhi, kc=kc, qs=qs, last=last:
                             e.matmul(pb, vsp, Ep[:, rlo:rhi],
                                      start=(kc == 0), stop=last),
                             reads=[ek, f"vs{epar}"], writes=[f"pb{qs}"],
                             name=f"pv{h}_{kc}_{qs}")
                    if last:
                        seq.emit("scalar", lambda e, h=h, qs=qs: e.activation(
                            attn_out[:, h * 2048 + qs * 512:
                                     h * 2048 + qs * 512 + 512],
                            psum_b[:, qs * 512:(qs + 1) * 512], AF.Copy),
                            reads=[f"pb{qs}"], writes=[f"ao{h}"],
                            name=f"aev{h}_{qs}")

            for kc in range(16):
                emit_strip(kc)
                if kc > 0:
                    emit_pv(kc - 1)
                if kc == 0:
                    pop_safe(4)
                elif kc >= 9:
                    pop_psum(4)
            emit_pv(15)
            pop_safe(99)
            pop_psum(99)
            if h == 4:
                seq.emit("sync", lambda e: e.dma_start(
                    out=swap_inA.ap().rearrange("(hq p) t -> p hq t", p=128),
                    in_=attn_out[:, 0:10240].rearrange(
                        "p (hq t) -> p hq t", hq=5)),
                    kind="dma",
                    reads=["ao0", "ao1", "ao2", "ao3", "ao4"],
                    writes=["swap_inA"], name="swb_A")
                seq.emit("gpsimd", lambda e: e.collective_compute(
                    "AllGather", ALU.bypass,
                    replica_groups=[[0, 1], [2, 3], [4, 5], [6, 7]],
                    ins=[swap_inA.ap().opt()], outs=[swap_outA.ap().opt()]),
                    kind="cc", reads=["swap_inA"], writes=["swap_outA"],
                    name="swap_ccA")
                seq.emit("gpsimd", lambda e: e.reg_load(r_swap, offs[0:1, 0:1]),
                         reads=["offs"], writes=["r_swap"], name="ld_rswap")
            if h == 7:
                seq.emit("sync", lambda e: e.dma_start(
                    out=swap_inB.ap().rearrange("(hq p) t -> p hq t", p=128),
                    in_=attn_out[:, 10240:16384].rearrange(
                        "p (hq t) -> p hq t", hq=3)),
                    kind="dma", reads=["ao5", "ao6", "ao7"],
                    writes=["swap_inB"], name="swb_B")
                seq.emit("gpsimd", lambda e: e.collective_compute(
                    "AllGather", ALU.bypass,
                    replica_groups=[[0, 1], [2, 3], [4, 5], [6, 7]],
                    ins=[swap_inB.ap().opt()], outs=[swap_outB.ap().opt()]),
                    kind="cc", reads=["swap_inB"], writes=["swap_outB"],
                    name="swap_ccB")

        if dbg:
            t_ao = nc.dram_tensor("dbg_attn_out", [128, 16384], BF16,
                                  kind="ExternalOutput")
            seq.emit("sync", lambda e: e.dma_start(out=t_ao.ap(), in_=attn_out),
                     kind="dma", reads=[f"ao{hh_}" for hh_ in range(8)],
                     name="dbg_attn_out")

        if stop_after == "p2":
            _finish(seq)
            seq.link_and_emit(nc, stack)
            return nc

        SLOT = {0: 0, 1: 1, 2: 2, 3: 3, 4: 4,
                8: 5, 9: 6, 10: 7, 11: 8, 12: 9,
                5: 10, 6: 11, 7: 12, 13: 13, 14: 14, 15: 15}

        seq.emit("gpsimd", lambda e: e.dma_start(
            out=af3[:, 10:16, :].rearrange("p (b f) t -> p b f t", b=2),
            in_=AP(swap_outB, r_swap,
                   [[2048, 128], [3 * 128 * 2048, 2], [128 * 2048, 3],
                    [1, 1024]])),
            kind="dma", reads=["swap_outB", "r_swap"],
            writes=["afB", "ao5", "ao6", "ao7"], name="sw_readB")

        pop_stage(len(stage_th))
        for t in st_after:
            t()

        seq.emit("sync", lambda e: e.dma_start(
            out=bout[:, :], in_=bout_d.ap()),
            kind="dma", writes=["boutb", "bvb", "E0", "E1"], name="ld_bout")
        for to_ in (3, 7):
            emit_xhalf(to_)
        for to in range(8):
            seq.emit("vector", lambda e, to=to: e.tensor_tensor(
                x13[:, to, :], x13[:, to, :], bout[:, :], ALU.add),
                reads=[f"x1t{to}", "boutb"], writes=[f"x1t{to}"],
                name=f"x1b{to}")

        for pss in range(2):
            fcs = ([0, 1, 2, 3, 4, 8, 9, 10, 11, 12] if pss == 0
                   else [5, 6, 7, 13, 14, 15])
            afk = "afA" if pss == 0 else "afB"
            for ec in range(4):
                wpar = ec % 2
                wo3 = wout[:, wpar * 8192:(wpar + 1) * 8192].rearrange(
                    "p (fc e) -> p fc e", fc=16)
                wk = f"wout{wpar}"
                if not (pss == 0 and ec < 2):
                    seq.emit("sync", lambda e, wo3=wo3, ec=ec: e.dma_start(
                        out=wo3, in_=wout_r[:, :, ec * 512:(ec + 1) * 512]),
                        kind="dma", reads=["st_wout"], writes=[wk, "xlnT"],
                        name=f"ldwo{pss}_{ec}")
                for to in range(8):
                    pcol = (((pss * 4 + ec) * 8 + to) % 4) * 512
                    pk = PA(pcol, 512)
                    pp = psum_a[:, pcol:pcol + 512]
                    nj = len(fcs)
                    for j, fc in enumerate(fcs):
                        seq.emit("tensor", lambda e, pp=pp, wo3=wo3, fc=fc,
                                 to=to, j=j, nj=nj: e.matmul(
                                     pp,
                                     af3[:, SLOT[fc], to * 128:(to + 1) * 128],
                                     wo3[:, fc, :],
                                     start=(j == 0), stop=(j == nj - 1)),
                                 reads=[afk, wk], writes=pk,
                                 name=f"opmm{pss}_{ec}_{to}_{fc}")
                    seq.emit("vector", lambda e, pp=pp, to=to, ec=ec:
                             e.tensor_tensor(
                                 x13[:, to, ec * 512:(ec + 1) * 512],
                                 x13[:, to, ec * 512:(ec + 1) * 512],
                                 pp, ALU.add),
                             reads=pk + [f"x1t{to}"], writes=[f"x1t{to}"],
                             name=f"opev{pss}_{ec}_{to}")

        dbg_dump("x1", x1, "x1", F32, 16384)


        if stop_after == "p4":
            _finish(seq)
            seq.link_and_emit(nc, stack)
            return nc

        seq.barrier()

        x2c = V("x2c")
        xhc = V("xhc")
        xlc = V("xlc")
        xloT = V("xloT")
        xloT3 = xloT.rearrange("p (dc t) -> p dc t", dc=16)
        x2T = V("x2T")
        x2T3 = x2T.rearrange("p (dc t) -> p dc t", dc=16)
        WT = V("WT")
        ind = V("ind")
        scoresR = V("scoresR")
        Wr = V("W")
        m1v = V("m1s")
        m2v = V("m2s")
        t2a = V("t2a")
        t2b = V("t2b")
        t2c = V("t2c")
        w1w = V("w1w")
        w2w = V("w2w")

        seq.emit("sync", lambda e: e.dma_start(out=ind[0:8, :],
                                                 in_=ind_d.ap()),
                 kind="dma", writes=["ind"], name="ld_ind")

        st5 = stats[:, 0:16]
        for to in range(8):
            seq.emit("scalar", lambda e, to=to: e.activation(
                xhc[:, :], x13[:, to, :], AF.Square, accum_out=st5[:, 0:1]),
                reads=[f"x1t{to}"], writes=["st5", "xhc"], name=f"l2sq{to}")
            seq.emit("vector", lambda e, to=to: e.tensor_scalar(
                xlc[:, :], x13[:, to, :], 1.0, None, ALU.mult, ALU.add,
                accum_out=st5[:, 1:2]),
                reads=[f"x1t{to}"], writes=["st5", "xlc"], name=f"l2sm{to}")
            seq.emit("vector", lambda e: e.tensor_scalar(
                st5[:, 2:3], st5[:, 1:2], 1.0 / D, None, ALU.mult),
                reads=["st5"], writes=["st5"], name=f"l2mean{to}")
            seq.emit("vector", lambda e: e.tensor_scalar(
                st5[:, 3:4], st5[:, 0:1], 1.0 / D, None, ALU.mult),
                reads=["st5"], writes=["st5"], name=f"l2ex2{to}")
            seq.emit("vector", lambda e: e.scalar_tensor_tensor(
                st5[:, 4:5], st5[:, 2:3], -1.0, st5[:, 2:3],
                ALU.mult, ALU.mult),
                reads=["st5"], writes=["st5"], name=f"l2nm2{to}")
            seq.emit("vector", lambda e: e.tensor_tensor(
                st5[:, 5:6], st5[:, 3:4], st5[:, 4:5], ALU.add),
                reads=["st5"], writes=["st5"], name=f"l2var{to}")
            seq.emit("scalar", lambda e: e.activation(
                st5[:, 6:7], st5[:, 5:6], AF.Ln, bias=fc_t[:, 0:1],
                scale=1.0),
                reads=["st5", "fconst"], writes=["st5"], name=f"l2lnv{to}")
            seq.emit("scalar", lambda e: e.activation(
                st5[:, 7:8], st5[:, 6:7], AF.Exp, scale=-0.5),
                reads=["st5"], writes=["st5"], name=f"l2rstd{to}")
            seq.emit("vector", lambda e: e.scalar_tensor_tensor(
                st5[:, 8:9], st5[:, 2:3], -1.0, st5[:, 7:8],
                ALU.mult, ALU.mult),
                reads=["st5"], writes=["st5"], name=f"l2nmr{to}")
            seq.emit("vector", lambda e, to=to: e.tensor_scalar(
                x2c[:, :], x13[:, to, :], st5[:, 7:8], st5[:, 8:9],
                ALU.mult, ALU.add),
                reads=[f"x1t{to}", "st5"], writes=["x2c"], name=f"l2nrm{to}")
            seq.emit("scalar", lambda e: e.activation(
                xhc[:, :], x2c[:, :], AF.Copy),
                reads=["x2c"], writes=["xhc"], name=f"l2hi{to}")
            seq.emit("vector", lambda e: e.tensor_tensor(
                xlc[:, :], x2c[:, :], xhc[:, :], ALU.subtract),
                reads=["x2c", "xhc"], writes=["xlc"], name=f"l2lo{to}")
            for half, (src, srk) in enumerate(((xhc, "xhc"), (xlc, "xlc"))):
                trp = psum_abf[:, half * 2048:half * 2048 + 2048]
                trk = [f"pa{2 * half}", f"pa{2 * half + 1}"]
                for dc in range(16):
                    seq.emit("tensor", lambda e, src=src, trp=trp, dc=dc:
                             e.transpose(trp[:, dc * 128:(dc + 1) * 128],
                                         src[:, dc * 128:(dc + 1) * 128],
                                         ident[:, :]),
                             reads=[srk, "ident"], writes=trk,
                             name=f"l2tr{to}_{half}_{dc}")
                if half == 0:
                    seq.emit("vector", lambda e, trp=trp, to=to:
                             e.tensor_copy(
                                 x2T3[:, :, to * 128:(to + 1) * 128],
                                 trp.rearrange("p (dc t) -> p dc t", dc=16)),
                             reads=trk, writes=["x2T"],
                             name=f"l2te{to}_{half}")
                else:
                    seq.emit("vector", lambda e, trp=trp: e.tensor_copy(
                        xloT3, trp.rearrange("p (dc t) -> p dc t", dc=16)),
                        reads=trk, writes=["xloT"], name=f"l2te{to}_{half}")
            for dc in range(16):
                seq.emit("tensor", lambda e, dc=dc, to=to: e.matmul(
                    psum_b[:, 0:16], x2T3[:, dc, to * 128:(to + 1) * 128],
                    rwhl3[:, dc, :], start=(dc == 0), stop=False),
                    reads=["x2T", "rwhl"], writes=["pb0"],
                    name=f"rmmh{to}_{dc}")
                seq.emit("tensor", lambda e, dc=dc: e.matmul(
                    psum_b[:, 0:8], xloT3[:, dc, :], rwhl3[:, dc, 0:8],
                    start=False, stop=(dc == 15)),
                    reads=["xloT", "rwhl"], writes=["pb0"],
                    name=f"rmml{to}_{dc}")
            seq.emit("scalar", lambda e, to=to: e.activation(
                t2a[:, to * 8:(to + 1) * 8], psum_b[:, 0:8], AF.Copy),
                reads=["pb0"], writes=["t2a"], name=f"rsca{to}")
            seq.emit("vector", lambda e, to=to: e.tensor_tensor(
                scoresR[:, to * 8:(to + 1) * 8],
                t2a[:, to * 8:(to + 1) * 8],
                psum_b[:, 8:16], ALU.add),
                reads=["pb0", "t2a"], writes=["scoresR"], name=f"rsc{to}")

        sc3 = scoresR.rearrange("p (t e) -> p t e", t=8)
        t2a3 = t2a.rearrange("p (t e) -> p t e", t=8)
        t2b3 = t2b.rearrange("p (t e) -> p t e", t=8)
        t2c3 = t2c.rearrange("p (t e) -> p t e", t=8)
        Wr3 = Wr.rearrange("p (t e) -> p t e", t=8)

        def bcast8(v):
            return v.rearrange("p (t o) -> p t o", o=1).broadcast_to(
                [128, 8, 8])

        seq.emit("vector", lambda e: e.tensor_reduce(
            m1v.rearrange("p (t o) -> p t o", o=1), sc3, AX.X, ALU.max),
            reads=["scoresR"], writes=["m1s"], name="t2m1")
        seq.emit("vector", lambda e: e.tensor_tensor(
            t2b3, sc3, bcast8(m1v), ALU.is_ge),
            reads=["scoresR", "m1s"], writes=["t2b"], name="t2mask1")
        seq.emit("vector", lambda e: e.scalar_tensor_tensor(
            t2a3, t2b3, -1e30, sc3, ALU.mult, ALU.add),
            reads=["t2b", "scoresR"], writes=["t2a"], name="t2sub")
        seq.emit("vector", lambda e: e.tensor_reduce(
            m2v.rearrange("p (t o) -> p t o", o=1), t2a3, AX.X, ALU.max),
            reads=["t2a"], writes=["m2s"], name="t2m2")
        seq.emit("vector", lambda e: e.tensor_tensor(
            t2c3, t2a3, bcast8(m2v), ALU.is_ge),
            reads=["t2a", "m2s"], writes=["t2c"], name="t2mask2")
        seq.emit("vector", lambda e: e.tensor_tensor(
            m2v[:, :], m2v[:, :], m1v[:, :], ALU.subtract),
            reads=["m2s", "m1s"], writes=["m2s"], name="t2d")
        seq.emit("scalar", lambda e: e.activation(
            w2w[:, :], m2v[:, :], AF.Exp),
            reads=["m2s"], writes=["w2w"], name="t2exp")
        seq.emit("vector", lambda e: e.tensor_scalar(
            w1w[:, :], w2w[:, :], 1.0, None, ALU.add),
            reads=["w2w"], writes=["w1w"], name="t2den")
        seq.emit("vector", lambda e: e.reciprocal(w1w[:, :], w1w[:, :]),
                 reads=["w1w"], writes=["w1w"], name="t2rec")
        seq.emit("vector", lambda e: e.tensor_tensor(
            w2w[:, :], w2w[:, :], w1w[:, :], ALU.mult),
            reads=["w2w", "w1w"], writes=["w2w"], name="t2w2")
        seq.emit("vector", lambda e: e.tensor_tensor(
            t2b3, t2b3, bcast8(w1w), ALU.mult),
            reads=["t2b", "w1w"], writes=["t2b"], name="t2wa")
        seq.emit("vector", lambda e: e.tensor_tensor(
            t2c3, t2c3, bcast8(w2w), ALU.mult),
            reads=["t2c", "w2w"], writes=["t2c"], name="t2wb")
        seq.emit("vector", lambda e: e.tensor_tensor(
            Wr3, t2b3, t2c3, ALU.add),
            reads=["t2b", "t2c"], writes=["Wr"], name="t2W")

        for to in range(8):
            seq.emit("tensor", lambda e, to=to: e.transpose(
                psum_b[0:8, to * 128:to * 128 + 128], Wr3[:, to, :],
                identf[:, :]),
                reads=["Wr", "identf"], writes=PB(to * 128, 128),
                name=f"wtr{to}")
        seq.emit("vector", lambda e: e.tensor_copy(
            WT[0:8, :], psum_b[0:8, 0:1024]),
            reads=PB(0, 1024), writes=["WTb"], name="wtev")

        dbg_dump("Wr", Wr, "Wr", F32, 64)
        dbg_dump("x2T", x2T, "x2T", BF16, 16384)
        dbg_dump("xloT", xloT, "xloT", BF16, 2048)
        dbg_dump("scoresR", scoresR, "scoresR", F32, 64)

        if stop_after == "p5":
            _finish(seq)
            seq.link_and_emit(nc, stack)
            return nc

        seq.barrier()

        Wb = V("Wb")
        h_all = V("h_all")
        h3 = h_all.rearrange("p (e f t) -> p e f t", e=8, f=4)
        w1c = V("w1c")
        w3c = V("w3c")
        w2c = V("w2c")
        s6 = V("s6")
        w1_3 = [w1c[:, par * 2048:(par + 1) * 2048]
                .rearrange("p (dc f) -> p dc f", dc=16) for par in range(2)]
        w3_3 = [w3c[:, par * 2048:(par + 1) * 2048]
                .rearrange("p (dc f) -> p dc f", dc=16) for par in range(2)]
        w2_3 = [w2c[:, par * 4096:(par + 1) * 4096]
                .rearrange("p (fs d) -> p fs d", fs=4) for par in range(2)]

        for e_i in range(NE):
            pcol = (e_i % 2) * 1024
            pk = PA(pcol, 1024)
            pp = psum_a[:, pcol:pcol + 1024]
            for half in range(2):
                seq.emit("tensor", lambda e, pp=pp, e_i=e_i, half=half:
                         e.matmul(pp[:, half * 512:(half + 1) * 512],
                                  ind[0:8, e_i * 128:(e_i + 1) * 128],
                                  WT[0:8, half * 512:(half + 1) * 512],
                                  start=True, stop=True),
                         reads=["WTb", "ind"], writes=pk,
                         name=f"wbmm{e_i}_{half}")
            seq.emit("scalar", lambda e, pp=pp, e_i=e_i: e.activation(
                Wb[:, e_i * 1024:(e_i + 1) * 1024], pp, AF.Copy),
                reads=pk, writes=["Wbb"], name=f"wbev{e_i}")

        for tc in range(2):
            for e_i in range(NE):
                for fc in range(4):
                    par = (e_i * 4 + fc) % 2
                    wk1, wk3 = f"w1c{par}", f"w3c{par}"
                    rb = (e_i * 4 + fc) % 2
                    seq.emit("sync", lambda e, par=par, e_i=e_i, fc=fc:
                             e.dma_start(
                                 out=w1_3[par],
                                 in_=w1_b[(e_i * 4 + fc) * D:
                                          (e_i * 4 + fc + 1) * D, :]
                                 .rearrange("(dc p) f -> p dc f", p=128)),
                             kind="dma", reads=["st_w1"], writes=[wk1],
                             name=f"ldw1_{tc}_{e_i}_{fc}")
                    seq.emit("sync", lambda e, par=par, e_i=e_i, fc=fc:
                             e.dma_start(
                                 out=w3_3[par],
                                 in_=w3_b[(e_i * 4 + fc) * D:
                                          (e_i * 4 + fc + 1) * D, :]
                                 .rearrange("(dc p) f -> p dc f", p=128)),
                             kind="dma", reads=["st_w3"], writes=[wk3],
                             name=f"ldw3_{tc}_{e_i}_{fc}")
                    pa_col = rb * 1024
                    pb_col = rb * 1024 + 512
                    ppa = psum_a[:, pa_col:pa_col + 512]
                    ppb = psum_a[:, pb_col:pb_col + 512]
                    pka = PA(pa_col, 512)
                    pkb = PA(pb_col, 512)
                    for dc in range(16):
                        seq.emit("tensor", lambda e, ppa=ppa, par=par, dc=dc,
                                 tc=tc: e.matmul(
                                     ppa, w1_3[par][:, dc, :],
                                     x2T3[:, dc, tc * 512:(tc + 1) * 512],
                                     start=(dc == 0), stop=(dc == 15)),
                                 reads=["x2T", wk1], writes=pka,
                                 name=f"x1mm{tc}_{e_i}_{fc}_{dc}")
                    for dc in range(16):
                        seq.emit("tensor", lambda e, ppb=ppb, par=par, dc=dc,
                                 tc=tc: e.matmul(
                                     ppb, w3_3[par][:, dc, :],
                                     x2T3[:, dc, tc * 512:(tc + 1) * 512],
                                     start=(dc == 0), stop=(dc == 15)),
                                 reads=["x2T", wk3], writes=pkb,
                                 name=f"x3mm{tc}_{e_i}_{fc}_{dc}")
                    s1 = s6[:, 0:512]
                    s3 = s6[:, 512:1024]
                    seq.emit("scalar", lambda e, ppa=ppa, s1=s1: e.activation(
                        s1, ppa, AF.Silu),
                        reads=pka, writes=["s6a"], name=f"sil{tc}_{e_i}_{fc}")
                    seq.emit("vector", lambda e, ppb=ppb, s3=s3, e_i=e_i,
                             tc=tc: e.tensor_tensor(
                                 s3, ppb,
                                 Wb[:, e_i * 1024 + tc * 512:
                                    e_i * 1024 + tc * 512 + 512],
                                 ALU.mult),
                             reads=pkb + ["Wbb"], writes=["s6b"],
                             name=f"x3s{tc}_{e_i}_{fc}")
                    seq.emit("vector", lambda e, s1=s1, s3=s3, e_i=e_i, fc=fc:
                             e.tensor_tensor(h3[:, e_i, fc, :], s1, s3,
                                             ALU.mult),
                             reads=["s6a", "s6b"], writes=["h_all"],
                             name=f"hm{tc}_{e_i}_{fc}")
            for pp_i in range(2):
                for e_i in range(NE):
                    wpar = e_i % 2
                    wk2 = f"w2c{wpar}"
                    seq.emit("sync", lambda e, wpar=wpar, e_i=e_i,
                             pp_i=pp_i: e.dma_start(
                                 out=w2_3[wpar],
                                 in_=w2_b[e_i * FF:(e_i + 1) * FF, :]
                                 .rearrange("(fs p) d -> p fs d", p=128)
                                 [:, :, pp_i * 1024:(pp_i + 1) * 1024]),
                             kind="dma", reads=["st_w2"], writes=[wk2],
                             name=f"ldw2_{tc}_{pp_i}_{e_i}")
                    for fc in range(4):
                        for ts in range(4):
                            for dh in range(2):
                                reg = ts * 2 + dh
                                if reg < 4:
                                    pr = psum_a[:, reg * 512:reg * 512 + 512]
                                    prk = PA(reg * 512, 512)
                                else:
                                    pr = psum_b[:, (reg - 4) * 512:
                                                (reg - 4) * 512 + 512]
                                    prk = PB((reg - 4) * 512, 512)
                                seq.emit("tensor", lambda e, pr=pr, wpar=wpar,
                                         e_i=e_i, fc=fc, ts=ts, dh=dh:
                                         e.matmul(
                                             pr,
                                             h3[:, e_i, fc,
                                                ts * 128:(ts + 1) * 128],
                                             w2_3[wpar][:, fc,
                                                        dh * 512:
                                                        (dh + 1) * 512],
                                             start=(e_i == 0 and fc == 0),
                                             stop=(e_i == 7 and fc == 3)),
                                         reads=["h_all", wk2], writes=prk,
                                         name=f"o2_{tc}_{pp_i}_{e_i}"
                                              f"_{fc}_{ts}_{dh}")
                for ts in range(4):
                    for dh in range(2):
                        reg = ts * 2 + dh
                        if reg < 4:
                            pr = psum_a[:, reg * 512:reg * 512 + 512]
                            prk = PA(reg * 512, 512)
                        else:
                            pr = psum_b[:, (reg - 4) * 512:
                                        (reg - 4) * 512 + 512]
                            prk = PB((reg - 4) * 512, 512)
                        to = tc * 4 + ts
                        dcol = pp_i * 1024 + dh * 512
                        seq.emit("vector", lambda e, pr=pr, to=to, dcol=dcol:
                                 e.tensor_tensor(
                                     x13[:, to, dcol:dcol + 512],
                                     x13[:, to, dcol:dcol + 512],
                                     pr, ALU.add),
                                 reads=prk + [f"x1t{to}"],
                                 writes=[f"x1t{to}"],
                                 name=f"o2ev{tc}_{pp_i}_{ts}_{dh}")

        seq.emit("sync", lambda e: e.dma_start(
            out=out_d.ap().rearrange("(to p) d -> p to d", p=128), in_=x13),
            kind="dma", reads=[f"x1t{i}" for i in range(8)], writes=["OUT"],
            name="st_out")
        _finish(seq)
        seq.link_and_emit(nc, stack)
    return nc


def _finish(seq):
    dma_ops = [op for op in seq.eng_ops["sync"] if op.kind == "dma"]
    op = seq.emit("sync", lambda e: e.nop(), name="final_wait")
    op.deps.extend((d, True) for d in dma_ops)


def _host_inputs(L0_attn_x, L0_attn_W_qkv, L0_attn_b_qkv, L0_attn_W_out,
                 L0_attn_b_out, L0_ffn_router_w, L0_ffn_w1, L0_ffn_w2,
                 L0_ffn_w3):
    x = np.asarray(L0_attn_x, np.float32)
    Wqkv = np.asarray(L0_attn_W_qkv, np.float32)
    bqkv = np.asarray(L0_attn_b_qkv, np.float32)
    Wout = np.asarray(L0_attn_W_out, np.float32)
    bout = np.asarray(L0_attn_b_out, np.float32)
    rw = np.asarray(L0_ffn_router_w, np.float32)
    w1 = np.asarray(L0_ffn_w1, np.float32)
    w2 = np.asarray(L0_ffn_w2, np.float32)
    w3 = np.asarray(L0_ffn_w3, np.float32)

    wout_h = np.ascontiguousarray(Wout.T)
    bout_h = np.ascontiguousarray(np.broadcast_to(bout, (128, D)))
    rw_h = np.ascontiguousarray(rw.T)
    ind_h = np.zeros((NE, NE * 128), np.float32)
    for e in range(NE):
        ind_h[e, e * 128:(e + 1) * 128] = 1.0
    w1_h = np.ascontiguousarray(
        w1.reshape(NE, D, 4, 128).transpose(0, 2, 1, 3)).reshape(
            NE * 4 * D, 128)
    w3_h = np.ascontiguousarray(
        w3.reshape(NE, D, 4, 128).transpose(0, 2, 1, 3)).reshape(
            NE * 4 * D, 128)
    w2_h = np.ascontiguousarray(w2).reshape(NE * FF, D)

    in_maps = []
    for c in range(8):
        b, g = c // 2, c % 2
        sl = slice(1024 * g, 1024 * g + 1024)
        wqk_h = np.ascontiguousarray(
            np.concatenate([Wqkv[sl],
                            Wqkv[2048 + 1024 * g:2048 + 1024 * g + 1024]],
                           0).T)
        wv_h = np.ascontiguousarray(
            Wqkv[4096 + 1024 * g:4096 + 1024 * g + 1024].T)
        bqk_h = np.ascontiguousarray(
            np.concatenate([bqkv[sl],
                            bqkv[2048 + 1024 * g:2048 + 1024 * g + 1024]]
                           ).reshape(16, 128).T)
        bv_h = np.ascontiguousarray(
            np.broadcast_to(bqkv[4096 + 1024 * g:4096 + 1024 * g + 1024],
                            (128, 1024)))
        offs_h = np.zeros((1, 8), np.uint32)
        offs_h[0, 0] = g * 1024
        in_maps.append({
            "x_seq": np.ascontiguousarray(x[b]),
            "x_half": np.ascontiguousarray(x[b, sl]),
            "wqk": wqk_h, "wv": wv_h, "bqk": bqk_h, "bv": bv_h,
            "wout": wout_h, "bout": bout_h, "rw": rw_h, "ind": ind_h,
            "w1": w1_h, "w3": w3_h, "w2": w2_h, "offs": offs_h,
        })
    return in_maps


def kernel(L0_attn_x, L0_attn_W_qkv, L0_attn_b_qkv, L0_attn_W_out,
           L0_attn_b_out, L0_ffn_router_w, L0_ffn_w1, L0_ffn_w2, L0_ffn_w3):
    in_maps = _host_inputs(L0_attn_x, L0_attn_W_qkv, L0_attn_b_qkv,
                           L0_attn_W_out, L0_attn_b_out, L0_ffn_router_w,
                           L0_ffn_w1, L0_ffn_w2, L0_ffn_w3)
    nc = build_program()
    res = run_bass_kernel_spmd(nc, in_maps, core_ids=list(range(8)))
    y = np.empty((4, S, D), np.float32)
    for c in range(8):
        b, g = c // 2, c % 2
        y[b, 1024 * g:1024 * g + 1024] = np.asarray(res.results[c]["out"],
                                                    np.float32)
    return y


# version 30 (final)
# speedup vs baseline: 1.0518x; 1.0161x over previous
import numpy as np
import concourse.bass as bass
import concourse.mybir as mybir
from concourse.bass_utils import run_bass_kernel_spmd
from contextlib import ExitStack

F32 = mybir.dt.float32
BF16 = mybir.dt.bfloat16
U32 = mybir.dt.uint32
AF = mybir.ActivationFunctionType
ALU = mybir.AluOpType
AX = mybir.AxisListType
AP = bass.AP

S = 2048
D = 2048
HC = 8
TOK = 1024
FF = 512
NE = 8
RSQ = 1.0 / float(np.sqrt(2048.0))

ENGINES = ("sync", "scalar", "vector", "tensor", "gpsimd")


class _Op:
    __slots__ = ("engine", "fn", "kind", "name", "eidx", "deps", "cross",
                 "inc_sem", "inc_val", "reach", "waits", "lane")

    def __init__(self):
        self.cross = False
        self.inc_sem = None
        self.reach = None
        self.waits = []


class _SemDep:

    def __init__(self, sem, val):
        self.engine = None
        self.kind = "sem"
        self.reach = (sem, val)
        self.name = "semdep"
        self.cross = True


class Seq:

    def __init__(self):
        self.eng_ops = {e: [] for e in ENGINES}
        self.res = {}
        self.pending_barrier = {e: [] for e in ENGINES}
        self.dma_counters = {e: 0 for e in ENGINES}
        self.last_async = {}

    def emit(self, engine, fn, reads=(), writes=(), kind="comp", name="",
             stage_lane=False):
        op = _Op()
        op.engine = engine
        op.fn = fn
        op.kind = kind
        op.name = name
        op.eidx = len(self.eng_ops[engine])
        self.eng_ops[engine].append(op)
        if kind == "dma":
            if stage_lane:
                op.lane = self.STAGE_LANES[stage_lane]
            else:
                op.lane = self.dma_counters[engine] % 7
                self.dma_counters[engine] += 1
                self.last_async[(engine, op.lane)] = op
        elif kind == "cc":
            self.last_async[("cc",)] = op
        deps = []
        if self.pending_barrier[engine]:
            deps.extend((d, True) for d in self.pending_barrier[engine])
            self.pending_barrier[engine] = []
        for r in reads:
            st = self.res.get(r)
            if st is not None and st[0] is not None:
                deps.append((st[0], True))
        for w in writes:
            st = self.res.get(w)
            if st is not None:
                if st[0] is not None:
                    deps.append((st[0], False))
                deps.extend((d, False) for d in st[1])
        for r in reads:
            st = self.res.setdefault(r, [None, []])
            st[1].append(op)
        for w in writes:
            self.res[w] = [op, []]
        op.deps = deps
        return op

    def barrier(self):
        lasts = [ops[-1] for ops in self.eng_ops.values() if ops]
        lasts.extend(self.last_async.values())
        for e in ENGINES:
            self.pending_barrier[e] = list(lasts)

    N_DMA_LANES = 10
    STAGE_LANES = {"A": 7, "B": 8, "C": 9}

    def link_and_emit(self, nc, stack):
        sems = {}
        for e in ENGINES:
            sems[(e, "comp")] = stack.enter_context(nc.semaphore(f"sc_{e}"))
            for ln in range(self.N_DMA_LANES):
                sems[(e, "dma", ln)] = stack.enter_context(
                    nc.semaphore(f"sd_{e}{ln}"))
        sems[("gpsimd", "cc")] = stack.enter_context(nc.semaphore("s_cc"))

        for e in ENGINES:
            for op in self.eng_ops[e]:
                for d, is_raw in op.deps:
                    if d.engine == op.engine and d.kind == "comp" \
                            and e == "tensor":
                        continue
                    d.cross = True
        for e in ENGINES:
            counts = {"comp": 0, "cc": 0}
            lane_counts = [0] * self.N_DMA_LANES
            stage_set = set(self.STAGE_LANES.values())
            stage_ops = []
            for op in self.eng_ops[e]:
                if op.kind == "dma":
                    ln = op.lane
                    sem = sems[(e, "dma", ln)]
                    if ln in stage_set:
                        if lane_counts[ln] > 0:
                            op.deps.append(
                                (_SemDep(sem, lane_counts[ln]), True))
                        lane_counts[ln] += 16
                        op.inc_sem, op.inc_val = sem, 16
                        op.reach = (sem, lane_counts[ln])
                        continue
                    if lane_counts[ln] > 0:
                        op.deps.append(
                            (_SemDep(sem, lane_counts[ln]), True))
                    lane_counts[ln] += 16
                    op.inc_sem, op.inc_val = sem, 16
                    op.reach = (sem, lane_counts[ln])
                elif op.kind == "cc":
                    counts["cc"] += 1
                    op.inc_sem, op.inc_val = sems[("gpsimd", "cc")], 1
                    op.reach = (sems[("gpsimd", "cc")], counts["cc"])
                elif op.cross:
                    counts["comp"] += 1
                    op.inc_sem, op.inc_val = sems[(e, "comp")], 1
                    op.reach = (sems[(e, "comp")], counts["comp"])
        for e in ENGINES:
            satisfied = {}
            for op in self.eng_ops[e]:
                waits = {}
                for d, is_raw in op.deps:
                    if d.engine == op.engine and d.kind == "comp" \
                            and e == "tensor":
                        continue
                    assert d.reach is not None, (op.name, d.name)
                    sem, val = d.reach
                    if satisfied.get(sem, 0) >= val:
                        continue
                    if waits.get(sem, 0) < val:
                        waits[sem] = val
                op.waits = sorted(waits.items(), key=lambda kv: str(kv[0]))
                for sem, val in waits.items():
                    satisfied[sem] = val

        with nc.Block() as block:
            for ename in ENGINES:
                ops = self.eng_ops[ename]
                if not ops:
                    continue

                def make_body(ops=ops):
                    def body(eng):
                        for op in ops:
                            for sem, val in op.waits:
                                eng.wait_ge(sem, val)
                            inst = op.fn(eng)
                            if op.inc_sem is not None:
                                inst.then_inc(op.inc_sem, op.inc_val)
                    return body

                getattr(block, ename)(make_body())

    def n_ops(self):
        return sum(len(v) for v in self.eng_ops.values())


ARENA_BYTES = 198 * 1024

LAYOUT = {
    "ident":   (0,     BF16, 128),
    "ones":    (256,   BF16, 128),
    "onesf":   (512,   F32, 128),
    "identf":  (1024,  F32, 128),
    "trimask": (1536,  BF16, 128),
    "bqk":     (1792,  F32, 16),
    "stats":   (1856,  F32, 32),
    "cp":      (1984,  F32, 8),
    "rc":      (2016,  F32, 4),
    "negM2":   (2032,  F32, 2),
    "nb":      (2048,  F32, 16),
    "scoresR": (2112,  F32, 64),
    "W":       (2368,  F32, 64),
    "rwhl":    (2624,  BF16, 256),
    "offs":    (3136,  U32, 8),
    "m1s":     (3200,  F32, 8),
    "m2s":     (3232,  F32, 8),
    "t2a":     (3264,  F32, 64),
    "t2b":     (3520,  F32, 64),
    "t2c":     (3776,  F32, 64),
    "w1w":     (4032,  F32, 8),
    "w2w":     (4064,  F32, 8),
    "rwh":     (4096,  BF16, 128),
    "rwf":     (4352,  F32, 128),
    "fconst":  (4864,  F32, 8),
    "xlnT":    (6 * 1024,   BF16, 16 * 1024),
    "attn_out": (134 * 1024, BF16, 16 * 1024),
    "x2c":     (134 * 1024, F32, 2048),
    "xhc":     (142 * 1024, BF16, 2048),
    "xlc":     (146 * 1024, BF16, 2048),
    "xloT":    (150 * 1024, BF16, 2048),
    "WT":      (184 * 1024, F32, 1024),
    "ind":     (188 * 1024, F32, 1024),
    "Wb":      (134 * 1024, BF16, 8192),
    "w1c":     (150 * 1024, BF16, 2 * 2048),
    "w3c":     (158 * 1024, BF16, 2 * 2048),
    "Q":       (38 * 1024,  BF16, 16384),
    "wout":    (6 * 1024,   BF16, 2 * 8192),
    "h_all":   (102 * 1024, BF16, 16384),
    "K":       (70 * 1024,  BF16, 16384),
    "x2T":     (6 * 1024,   BF16, 16384),
    "V":       (102 * 1024, BF16, 16384),
    "x1":      (38 * 1024,  F32, 16384),
    "attnfull": (134 * 1024, BF16, 16384),
    "xc":      (134 * 1024, F32, 2 * 2048),
    "xlnc":    (150 * 1024, BF16, 2 * 2048),
    "junkS":   (158 * 1024, BF16, 2048),
    "junkV":   (162 * 1024, BF16, 2048),
    "wqk":     (166 * 1024, BF16, 2 * 2048),
    "wv":      (174 * 1024, BF16, 8192),
    "E":       (190 * 1024, BF16, 2 * 2048),
    "scr":     (174 * 1024, BF16, 2 * 512),
    "vs":      (176 * 1024, BF16, 2 * 128),
    "w2c":     (166 * 1024, BF16, 2 * 4096),
    "s6":      (182 * 1024, BF16, 2 * 512),
    "bvb":     (190 * 1024, F32, 1024),
    "bout":    (190 * 1024, F32, 2048),
}


def _fix_layout():
    LAYOUT["attnfull"] = (134 * 1024, BF16, 16384)


_fix_layout()

_ESZ = {BF16: 2, F32: 4, U32: 4}


def _check_layout():
    for name, (off, dt, n) in LAYOUT.items():
        assert off % 4 == 0, name
        assert off + n * _ESZ[dt] <= ARENA_BYTES, name


_check_layout()


def build_program(stop_after=None, dbg=False):
    nc = bass.Bass("TRN2", target_bir_lowering=False)

    x_seq = nc.dram_tensor("x_seq", [S, D], F32, kind="ExternalInput")
    x_half = nc.dram_tensor("x_half", [TOK, D], F32, kind="ExternalInput")
    wqk_d = nc.dram_tensor("wqk", [D, 2048], F32, kind="ExternalInput")
    wv_d = nc.dram_tensor("wv", [D, 1024], F32, kind="ExternalInput")
    bqk_d = nc.dram_tensor("bqk", [128, 16], F32, kind="ExternalInput")
    bv_d = nc.dram_tensor("bv", [128, 1024], F32, kind="ExternalInput")
    wout_d = nc.dram_tensor("wout", [D, D], F32, kind="ExternalInput")
    bout_d = nc.dram_tensor("bout", [128, D], F32, kind="ExternalInput")
    rw_d = nc.dram_tensor("rw", [D, NE], F32, kind="ExternalInput")
    ind_d = nc.dram_tensor("ind", [NE, NE * 128], F32, kind="ExternalInput")
    w1_d = nc.dram_tensor("w1", [NE * 4 * D, 128], F32, kind="ExternalInput")
    w3_d = nc.dram_tensor("w3", [NE * 4 * D, 128], F32, kind="ExternalInput")
    w2_d = nc.dram_tensor("w2", [NE * FF, D], F32, kind="ExternalInput")
    offs_d = nc.dram_tensor("offs", [1, 8], U32, kind="ExternalInput")
    out_d = nc.dram_tensor("out", [TOK, D], F32, kind="ExternalOutput")

    swap_inA = nc.dram_tensor("swap_inA", [768, 2048], BF16, kind="Internal")
    swap_outA = nc.dram_tensor("swap_outA", [1536, 2048], BF16,
                               kind="Internal")
    swap_inB = nc.dram_tensor("swap_inB", [256, 2048], BF16, kind="Internal")
    swap_outB = nc.dram_tensor("swap_outB", [512, 2048], BF16,
                               kind="Internal")

    wqk_b = nc.dram_tensor("wqk_b", [D, 2048], BF16, kind="Internal")
    wv_b = nc.dram_tensor("wv_b", [D, 1024], BF16, kind="Internal")
    wout_b = nc.dram_tensor("wout_b", [D, D], BF16, kind="Internal")
    w1_b = nc.dram_tensor("w1_b", [NE * 4 * D, 128], BF16, kind="Internal")
    w3_b = nc.dram_tensor("w3_b", [NE * 4 * D, 128], BF16, kind="Internal")
    w2_b = nc.dram_tensor("w2_b", [NE * FF, D], BF16, kind="Internal")

    dbg_outs = {}

    seq = Seq()
    r_swap = nc.gpsimd.alloc_register("r_swap")

    with ExitStack() as stack:
        arena_t = stack.enter_context(
            nc.sbuf_tensor("arena", [128, ARENA_BYTES // 2], BF16))

        def V(name):
            off, dt, n = LAYOUT[name]
            base = arena_t.ap().bitcast(dt)
            start = off // _ESZ[dt]
            return base[:, start:start + n]

        def dbg_dump(name, src, rk, dt, n):
            if not dbg:
                return
            t = nc.dram_tensor("dbg_" + name, [128, n], dt,
                               kind="ExternalOutput")
            dbg_outs[name] = t
            seq.emit("sync", lambda e, t=t, src=src: e.dma_start(
                out=t.ap(), in_=src), kind="dma", reads=[rk],
                name=f"dbg_{name}")

        psum_a = stack.enter_context(nc.psum_tensor("ps_a", [128, 2048], F32))
        psum_b = stack.enter_context(nc.psum_tensor("ps_b", [128, 2048], F32))
        psum_abf = psum_a.ap().bitcast(BF16)

        def PA(lo_col, n_col):
            b0 = lo_col // 512
            b1 = (lo_col + n_col - 1) // 512
            return [f"pa{b}" for b in range(b0, b1 + 1)]

        def PB(lo_col, n_col):
            b0 = lo_col // 512
            b1 = (lo_col + n_col - 1) // 512
            return [f"pb{b}" for b in range(b0, b1 + 1)]

        ident = V("ident")
        identf = V("identf")
        ones = V("ones")
        onesf = V("onesf")
        trimask = V("trimask")
        seq.emit("gpsimd", lambda e: e.memset(ident[:, :], 0.0),
                 writes=["ident"], name="set_ident0")
        seq.emit("gpsimd", lambda e: e.affine_select(
            out=ident[:, :], in_=ident[:, :], compare_op=ALU.not_equal,
            fill=1.0, base=0, pattern=[[-1, 128]], channel_multiplier=1),
            reads=["ident"], writes=["ident"], name="set_ident1")
        seq.emit("gpsimd", lambda e: e.memset(identf[:, :], 0.0),
                 writes=["identf"], name="set_identf0")
        seq.emit("gpsimd", lambda e: e.affine_select(
            out=identf[:, :], in_=identf[:, :], compare_op=ALU.not_equal,
            fill=1.0, base=0, pattern=[[-1, 128]], channel_multiplier=1),
            reads=["identf"], writes=["identf"], name="set_identf1")
        seq.emit("gpsimd", lambda e: e.memset(ones[:, :], 1.0),
                 writes=["ones"], name="set_ones")
        seq.emit("gpsimd", lambda e: e.memset(onesf[:, :], 1.0),
                 writes=["onesf"], name="set_onesf")
        fc_t = V("fconst")
        seq.emit("gpsimd", lambda e: e.memset(fc_t[:, 0:1], 1e-5),
                 writes=["fconst"], name="set_fc0")
        seq.emit("gpsimd", lambda e: e.memset(
            fc_t[:, 1:2], -0.5 * float(np.log(2048.0))),
            writes=["fconst"], name="set_fc1")
        seq.emit("gpsimd", lambda e: e.memset(trimask[:, :], 1.0),
                 writes=["trimask"], name="set_tri0")
        seq.emit("gpsimd", lambda e: e.affine_select(
            out=trimask[:, :], in_=trimask[:, :], compare_op=ALU.is_ge,
            fill=0.0, base=0, pattern=[[1, 128]], channel_multiplier=-1),
            reads=["trimask"], writes=["trimask"], name="set_tri1")

        bqk = V("bqk")
        seq.emit("scalar", lambda e: e.dma_start(out=bqk[:, :], in_=bqk_d.ap()),
                 kind="dma", writes=["bqk"], name="ld_bqk")
        bvb = V("bvb")
        seq.emit("scalar", lambda e: e.dma_start(out=bvb[:, :], in_=bv_d.ap()),
                 kind="dma", writes=["bvb"], name="ld_bv")
        offs = V("offs")
        seq.emit("gpsimd", lambda e: e.dma_start(out=offs[0:1, :],
                                                 in_=offs_d.ap()),
                 kind="dma", writes=["offs"], name="ld_offs")

        rwhl = V("rwhl")
        rwh = V("rwh")
        rwf = V("rwf")
        rwhl3 = rwhl.rearrange("p (dc x) -> p dc x", dc=16)
        rwh3 = rwh.rearrange("p (dc e) -> p dc e", dc=16)
        rwf3 = rwf.rearrange("p (dc e) -> p dc e", dc=16)
        seq.emit("scalar", lambda e: e.dma_start(
            out=rwf3, in_=rw_d.ap().rearrange("(dc p) e -> p dc e", p=128)),
            kind="dma", writes=["rwf"], name="ld_rwf")
        seq.emit("gpsimd", lambda e: e.dma_start(
            out=rwh3, in_=rw_d.ap().rearrange("(dc p) e -> p dc e", p=128)),
            kind="dma", writes=["rwh"], name="ld_rwh")
        seq.emit("vector", lambda e: e.tensor_copy(rwhl3[:, :, 0:8], rwh3),
                 reads=["rwh"], writes=["rwhl"], name="rw_hi")
        seq.emit("vector", lambda e: e.tensor_tensor(
            rwhl3[:, :, 8:16], rwf3, rwh3, ALU.subtract),
            reads=["rwf", "rwh"], writes=["rwhl"], name="rw_lo")

        rrctr = [0]

        def stage(name, dst, src, rows, row_chunk, lane="A", defer=False):
            th = []
            for i, r0 in enumerate(range(0, rows, row_chunk)):
                r1 = min(rows, r0 + row_chunk)
                if lane == "RR":
                    ln = "ABC"[rrctr[0] % 3]
                    rrctr[0] += 1
                else:
                    ln = lane
                th.append(lambda dst=dst, src=src, r0=r0, r1=r1, i=i, ln=ln:
                          seq.emit("gpsimd",
                                   lambda e, dst=dst, src=src, r0=r0, r1=r1:
                                   e.dma_start(out=dst[r0:r1, :],
                                               in_=src[r0:r1, :]),
                                   kind="dma", writes=[name],
                                   name=f"{name}_{i}", stage_lane=ln))
            if not defer:
                for t in th:
                    t()
            return th

        xlnT = V("xlnT")
        xlnT3 = xlnT.rearrange("p (dc t) -> p dc t", dc=16)
        Q = V("Q")
        K = V("K")
        Vt = V("V")
        stats = V("stats")
        xc = V("xc")
        xlnc = V("xlnc")
        junkS = V("junkS")
        junkV = V("junkV")
        wqk = V("wqk")
        wv = V("wv")
        wv3 = wv.rearrange("p (dc v) -> p dc v", dc=16)
        E = V("E")
        attnfull = V("attnfull")
        af3 = attnfull.rearrange("p (fc t) -> p fc t", fc=16)
        scr = V("scr")
        vs = V("vs")
        cp = V("cp")
        rc = V("rc")
        negM2 = V("negM2")
        nb = V("nb")
        attn_out = V("attn_out")

        wqk_r = wqk_b.ap().rearrange("(dc p) e -> p dc e", p=128)
        wv_r = wv_b.ap().rearrange("(dc p) e -> p dc e", p=128)

        def emit_ln_half(hh):
            for tc in range(8):
                gtc = hh * 8 + tc
                par = tc % 2
                xcp = xc[:, par * 2048:(par + 1) * 2048]
                xck = f"xc{par}"
                stp = stats[:, par * 16:par * 16 + 16]
                stk = f"st{par}"
                xlnp = xlnc[:, par * 2048:(par + 1) * 2048]
                xlnk = f"xlnc{par}"
                trp = psum_abf[:, par * 2048:(par + 1) * 2048]
                trk = [f"pa{2 * par}", f"pa{2 * par + 1}"]

                seq.emit("sync", lambda e, xcp=xcp, gtc=gtc: e.dma_start(
                    out=xcp, in_=x_seq[gtc * 128:(gtc + 1) * 128, :]),
                    kind="dma", writes=[xck], name=f"ldx{gtc}")
                seq.emit("scalar", lambda e, xcp=xcp, stp=stp: e.activation(
                    junkS[:, :], xcp, AF.Square, accum_out=stp[:, 0:1]),
                    reads=[xck], writes=[stk, "junkS"], name=f"sq{gtc}")
                seq.emit("vector", lambda e, xcp=xcp, stp=stp: e.tensor_scalar(
                    junkV[:, :], xcp, 1.0, None, ALU.mult, ALU.add,
                    accum_out=stp[:, 1:2]),
                    reads=[xck], writes=[stk, "junkV"], name=f"sm{gtc}")
                seq.emit("vector", lambda e, stp=stp: e.tensor_scalar(
                    stp[:, 2:3], stp[:, 1:2], 1.0 / D, None, ALU.mult),
                    reads=[stk], writes=[stk], name=f"mean{gtc}")
                seq.emit("vector", lambda e, stp=stp: e.tensor_scalar(
                    stp[:, 3:4], stp[:, 0:1], 1.0 / D, None, ALU.mult),
                    reads=[stk], writes=[stk], name=f"ex2{gtc}")
                seq.emit("vector", lambda e, stp=stp: e.scalar_tensor_tensor(
                    stp[:, 4:5], stp[:, 2:3], -1.0, stp[:, 2:3],
                    ALU.mult, ALU.mult),
                    reads=[stk], writes=[stk], name=f"nm2{gtc}")
                seq.emit("vector", lambda e, stp=stp: e.tensor_tensor(
                    stp[:, 5:6], stp[:, 3:4], stp[:, 4:5], ALU.add),
                    reads=[stk], writes=[stk], name=f"var{gtc}")
                seq.emit("scalar", lambda e, stp=stp: e.activation(
                    stp[:, 6:7], stp[:, 5:6], AF.Ln, bias=fc_t[:, 0:1],
                    scale=1.0),
                    reads=[stk, "fconst"], writes=[stk], name=f"lnv{gtc}")
                seq.emit("scalar", lambda e, stp=stp: e.activation(
                    stp[:, 7:8], stp[:, 6:7], AF.Exp, scale=-0.5),
                    reads=[stk], writes=[stk], name=f"rstd{gtc}")
                seq.emit("vector", lambda e, stp=stp: e.scalar_tensor_tensor(
                    stp[:, 8:9], stp[:, 2:3], -1.0, stp[:, 7:8],
                    ALU.mult, ALU.mult),
                    reads=[stk], writes=[stk], name=f"nmr{gtc}")
                seq.emit("vector", lambda e, xcp=xcp, stp=stp, xlnp=xlnp:
                         e.tensor_scalar(xlnp, xcp, stp[:, 7:8], stp[:, 8:9],
                                         ALU.mult, ALU.add),
                         reads=[xck, stk], writes=[xlnk], name=f"nrm{gtc}")
                for dc in range(16):
                    seq.emit("tensor", lambda e, xlnp=xlnp, trp=trp, dc=dc:
                             e.transpose(trp[:, dc * 128:(dc + 1) * 128],
                                         xlnp[:, dc * 128:(dc + 1) * 128],
                                         ident[:, :]),
                             reads=[xlnk, "ident"], writes=trk,
                             name=f"tr{gtc}_{dc}")
                seq.emit("vector", lambda e, trp=trp, tc=tc: e.tensor_copy(
                    xlnT3[:, :, tc * 128:(tc + 1) * 128],
                    trp.rearrange("p (dc t) -> p dc t", dc=16)),
                    reads=trk, writes=["xlnT"], name=f"trev{gtc}")

        def emit_vproj(hh):
            for vc in range(2):
                seq.emit("sync", lambda e, vc=vc: e.dma_start(
                    out=wv3, in_=wv_r[:, :, vc * 512:(vc + 1) * 512]),
                    kind="dma", reads=["st_wv"], writes=["wv"],
                    name=f"ldwv{hh}_{vc}")
                for tc in range(8):
                    pcol = (tc % 2) * 512
                    pk = PB(pcol, 512)
                    pp = psum_b[:, pcol:pcol + 512]
                    for dc in range(16):
                        seq.emit("tensor", lambda e, pp=pp, dc=dc, tc=tc:
                                 e.matmul(pp,
                                          xlnT3[:, dc, tc * 128:(tc + 1) * 128],
                                          wv3[:, dc, :],
                                          start=(dc == 0), stop=(dc == 15)),
                                 reads=["xlnT", "wv"], writes=pk,
                                 name=f"vmm{hh}_{vc}_{tc}_{dc}")
                    gtc = hh * 8 + tc
                    seq.emit("vector", lambda e, pp=pp, gtc=gtc, vc=vc:
                             e.tensor_tensor(
                                 Vt[:, gtc * 1024 + vc * 512:
                                    gtc * 1024 + vc * 512 + 512],
                                 pp, bvb[:, vc * 512:(vc + 1) * 512], ALU.add),
                             reads=pk + ["bvb"], writes=["V"],
                             name=f"vev{hh}_{vc}_{tc}")

        wqkrot = [0]
        qkrot = [0]

        def qk_thunks(ec, hh):
            thunks = []
            wpar = wqkrot[0] % 2
            wqkrot[0] += 1
            wqk3 = wqk[:, wpar * 2048:(wpar + 1) * 2048].rearrange(
                "p (dc e) -> p dc e", dc=16)
            wkk = f"wqk{wpar}"
            thunks.append(("safe", lambda: seq.emit(
                "sync", lambda e: e.dma_start(
                    out=wqk3, in_=wqk_r[:, :, ec * 128:(ec + 1) * 128]),
                kind="dma", reads=["st_wqk"], writes=[wkk],
                name=f"ldwqk{hh}_{ec}")))
            dest = Q if ec < 8 else K
            dk = f"Qh{ec}" if ec < 8 else f"Kh{ec - 8}"
            hloc = ec % 8
            for tq in range(2):
                bank = qkrot[0] % 2
                qkrot[0] += 1
                pk = [f"pb{bank}"]
                pp = psum_b[:, bank * 512:bank * 512 + 512]
                for d0 in range(0, 16, 4):
                    def mmth(pp=pp, d0=d0, tq=tq, wqk3=wqk3, pk=pk, wkk=wkk):
                        for dc in range(d0, d0 + 4):
                            seq.emit("tensor", lambda e, pp=pp, wqk3=wqk3,
                                     dc=dc, tq=tq: e.matmul(
                                         pp, wqk3[:, dc, :],
                                         xlnT3[:, dc,
                                               tq * 512:(tq + 1) * 512],
                                         start=(dc == 0), stop=(dc == 15)),
                                     reads=["xlnT", wkk], writes=pk,
                                     name=f"qkmm{hh}_{ec}_{tq}_{dc}")
                    thunks.append(("psum", mmth))
                col = hloc * 2048 + hh * 1024 + tq * 512
                thunks.append(("psum", lambda pp=pp, col=col, pk=pk,
                               dest=dest, dk=dk, ec=ec: seq.emit(
                    "vector", lambda e: e.tensor_scalar(
                        dest[:, col:col + 512], pp, bqk[:, ec:ec + 1],
                        None, ALU.add),
                    reads=pk + ["bqk"], writes=[dk],
                    name=f"qkev{hh}_{ec}_{tq}")))
            return thunks

        def bound_thunks(h):
            thunks = []
            ncol = h % 2
            for wi, (buf, bk) in enumerate(((Q, f"Qh{h}"), (K, f"Kh{h}"))):
                for ch in range(4):
                    spar = ch % 2
                    scrp = scr[:, spar * 512:spar * 512 + 512]
                    sk = f"scr{spar}"
                    bank = qkrot[0] % 2
                    qkrot[0] += 1
                    pk = [f"pb{bank}"]
                    pp = psum_b[:, bank * 512:bank * 512 + 512]
                    slot = 4 + (2 * wi + ch % 2)
                    col0 = h * 2048 + ch * 512

                    def bth(buf=buf, bk=bk, scrp=scrp, sk=sk, pp=pp, pk=pk,
                            slot=slot, col0=col0, wi=wi, ch=ch):
                        seq.emit("scalar", lambda e: e.activation(
                            scrp, buf[:, col0:col0 + 512], AF.Square),
                            reads=[bk], writes=[sk], name=f"nbsq{h}_{wi}_{ch}")
                        seq.emit("tensor", lambda e: e.matmul(
                            pp[0:1, :], ones[:, 0:1], scrp, start=True,
                            stop=True),
                            reads=[sk, "ones"], writes=pk,
                            name=f"nbmm{h}_{wi}_{ch}")
                        seq.emit("vector", lambda e: e.reduce_max(
                            nb[0:1, slot:slot + 1], pp[0:1, :], axis=AX.X),
                            reads=pk, writes=["nb"], name=f"nbrm{h}_{wi}_{ch}")
                        if ch == 0:
                            seq.emit("vector", lambda e: e.tensor_copy(
                                nb[0:1, wi:wi + 1], nb[0:1, slot:slot + 1]),
                                reads=["nb"], writes=["nb"],
                                name=f"nbc{h}_{wi}_{ch}")
                        else:
                            seq.emit("vector", lambda e: e.tensor_tensor(
                                nb[0:1, wi:wi + 1], nb[0:1, wi:wi + 1],
                                nb[0:1, slot:slot + 1], ALU.max),
                                reads=["nb"], writes=["nb"],
                                name=f"nbx{h}_{wi}_{ch}")
                    thunks.append(("psum", bth))

            def bfin():
                seq.emit("vector", lambda e: e.tensor_tensor(
                    nb[0:1, 2:3], nb[0:1, 0:1], nb[0:1, 1:2], ALU.mult),
                    reads=["nb"], writes=["nb"], name=f"nbprod{h}")
                seq.emit("scalar", lambda e: e.activation(
                    nb[0:1, 3:4], nb[0:1, 2:3], AF.Ln),
                    reads=["nb"], writes=["nb"], name=f"nbln{h}")
                seq.emit("scalar", lambda e: e.activation(
                    nb[0:1, 8:9], nb[0:1, 3:4], AF.Exp, scale=0.5,
                    bias=fc_t[0:1, 1:2]),
                    reads=["nb", "fconst"], writes=["nb"], name=f"nbexp{h}")
                bank = qkrot[0] % 2
                qkrot[0] += 1
                pk = [f"pb{bank}"]
                pp = psum_b[:, bank * 512:bank * 512 + 1]
                seq.emit("tensor", lambda e, pp=pp: e.matmul(
                    pp, onesf[0:1, :], nb[0:1, 8:9], start=True, stop=True),
                    reads=["nb", "onesf"], writes=pk, name=f"nbbc{h}")
                seq.emit("vector", lambda e, pp=pp: e.tensor_scalar(
                    negM2[:, ncol:ncol + 1], pp, -1.0, None, ALU.mult),
                    reads=pk, writes=[f"negMp{ncol}"], name=f"nbneg{h}")
            thunks.append(("psum", bfin))
            return thunks

        stage("st_wv", wv_b, wv_d, D, 1024, lane="B")
        stage("st_wqk", wqk_b, wqk_d, D, 512, lane="A")

        emit_ln_half(0)
        emit_vproj(0)
        for ec in range(16):
            for tag, t in qk_thunks(ec, 0):
                t()
        emit_ln_half(1)
        emit_vproj(1)
        for ec in (0, 8):
            for tag, t in qk_thunks(ec, 1):
                t()
        for tag, t in bound_thunks(0):
            t()

        seq.barrier()

        stage_th = []
        stage_th += stage("st_wout", wout_b, wout_d, D, 512, lane="RR",
                          defer=True)
        stage_th += stage("st_w1", w1_b, w1_d, NE * 4 * D, 8192, lane="RR",
                          defer=True)
        stage_th += stage("st_w3", w3_b, w3_d, NE * 4 * D, 8192, lane="RR",
                          defer=True)
        st_after = stage("st_w2", w2_b, w2_d, NE * FF, 512, lane="RR",
                         defer=True)
        si = [0]

        def pop_stage(n):
            for _ in range(n):
                if si[0] < len(stage_th):
                    stage_th[si[0]]()
                    si[0] += 1

        x1 = V("x1")
        x13 = x1.rearrange("p (to d) -> p to d", to=8)
        bout = V("bout")
        wout = V("wout")
        wout_r = wout_b.ap().rearrange("(fc p) e -> p fc e", p=128)

        def emit_xhalf(to):
            if to < 4:
                alias = [f"Qh{2 * to}", f"Qh{2 * to + 1}"]
            else:
                alias = [f"Kh{2 * (to - 4)}", f"Kh{2 * (to - 4) + 1}"]
            seq.emit("sync", lambda e, to=to: e.dma_start(
                out=x13[:, to, :], in_=x_half.ap()[to * 128:(to + 1) * 128, :]),
                kind="dma", writes=[f"x1t{to}"] + alias, name=f"ldxh{to}")

        for h in range(HC):
            pop_stage(7 if h < 3 else 0)
            if h == 7:
                seq.emit("gpsimd", lambda e: e.dma_start(
                    out=af3[:, 0:12, :].rearrange("p (b f) t -> p b f t",
                                                  b=2),
                    in_=AP(swap_outA, r_swap,
                           [[2048, 128], [6 * 128 * 2048, 2],
                            [128 * 2048, 6], [1, 1024]])),
                    kind="dma", reads=["swap_outA", "r_swap"],
                    writes=["afA", "ao0", "ao1", "ao2", "ao3", "ao4", "ao5"],
                    name="sw_readA")
                for ec in (0, 1):
                    wo3p = wout[:, (ec % 2) * 8192:
                                 (ec % 2 + 1) * 8192].rearrange(
                        "p (fc e) -> p fc e", fc=16)
                    seq.emit("sync", lambda e, wo3p=wo3p, ec=ec: e.dma_start(
                        out=wo3p, in_=wout_r[:, :, ec * 512:(ec + 1) * 512]),
                        kind="dma", reads=["st_wout"],
                        writes=[f"wout{ec % 2}", "xlnT"],
                        name=f"ldwo_pre{ec}")
                for to_ in (0, 1, 2, 4, 5, 6):
                    emit_xhalf(to_)
            fsafe = []
            fpsum = []
            if h + 1 < HC:
                alls = (qk_thunks(h + 1, 1) + qk_thunks(8 + h + 1, 1)
                        + bound_thunks(h + 1))
                fsafe = [t for tag, t in alls if tag == "safe"]
                fpsum = [t for tag, t in alls if tag == "psum"]
            fis = [0]
            fip = [0]

            def pop_safe(n, fsafe=fsafe, fis=fis):
                for _ in range(n):
                    if fis[0] < len(fsafe):
                        fsafe[fis[0]]()
                        fis[0] += 1

            def pop_psum(n, fpsum=fpsum, fip=fip):
                for _ in range(n):
                    if fip[0] < len(fpsum):
                        fpsum[fip[0]]()
                        fip[0] += 1

            negMh = negM2[:, h % 2:h % 2 + 1]
            nk = f"negMp{h % 2}"

            def half_ranges(kc):
                qlo = 128 * kc
                out = []
                for hhq in range(2):
                    lo = max(qlo, hhq * 1024)
                    hi = (hhq + 1) * 1024
                    if lo < hi:
                        out.append((lo, hi))
                return out

            strip_info = {}
            hsctr = [0]

            def emit_strip(kc, h=h, negMh=negMh, nk=nk):
                qlo = 128 * kc
                epar = kc % 2
                Ep = E[:, epar * 2048:epar * 2048 + 2048]
                ek = f"E{epar}"
                cpb = cp[:, epar * 4:epar * 4 + 4]
                ck = f"cp{epar}"
                nparts = 0
                for (lo, hi) in half_ranges(kc):
                    reg = hsctr[0] % 2
                    hsctr[0] += 1
                    rbase = reg * 1024
                    lo0 = (lo // 512) * 512
                    for qb in range(lo0, hi, 512):
                        pcol = rbase + (qb - lo0)
                        pp = psum_a[:, pcol:pcol + 512]
                        pk = PA(pcol, 512)
                        seq.emit("tensor", lambda e, pp=pp, kc=kc, qb=qb:
                                 e.matmul(pp,
                                          K[:, h * 2048 + kc * 128:
                                             h * 2048 + kc * 128 + 128],
                                          Q[:, h * 2048 + qb:
                                             h * 2048 + qb + 512],
                                          start=True, stop=True),
                                 reads=[f"Qh{h}", f"Kh{h}"], writes=pk,
                                 name=f"sc{h}_{kc}_{qb}")
                    if lo == qlo:
                        pcol = rbase + (lo - lo0)
                        pp = psum_a[:, pcol:pcol + 128]
                        seq.emit("scalar", lambda e, pp=pp, Ep=Ep, lo=lo:
                                 e.activation(Ep[:, lo:lo + 128], pp, AF.Exp,
                                              bias=negMh, scale=RSQ),
                                 reads=PA(pcol, 128) + [nk], writes=[ek],
                                 name=f"expd{h}_{kc}")
                        seq.emit("vector", lambda e, Ep=Ep, lo=lo, cpb=cpb,
                                 np_=nparts: e.scalar_tensor_tensor(
                                     Ep[:, lo:lo + 128], Ep[:, lo:lo + 128],
                                     1.0, trimask[:, :], ALU.bypass, ALU.mult,
                                     accum_out=cpb[:, np_:np_ + 1]),
                                 reads=[ek, "trimask"], writes=[ek, ck],
                                 name=f"mask{h}_{kc}")
                        nparts += 1
                        lo2 = lo + 128
                        if lo2 < hi:
                            pcol2 = rbase + (lo2 - lo0)
                            pp2 = psum_a[:, pcol2:pcol2 + (hi - lo2)]
                            seq.emit("scalar", lambda e, pp2=pp2, Ep=Ep,
                                     lo2=lo2, hi=hi, cpb=cpb, np_=nparts:
                                     e.activation(
                                         Ep[:, lo2:hi], pp2, AF.Exp,
                                         bias=negMh, scale=RSQ,
                                         accum_out=cpb[:, np_:np_ + 1]),
                                     reads=PA(pcol2, hi - lo2) + [nk],
                                     writes=[ek, ck], name=f"expr{h}_{kc}")
                            nparts += 1
                    else:
                        pcol = rbase + (lo - lo0)
                        pp = psum_a[:, pcol:pcol + (hi - lo)]
                        seq.emit("scalar", lambda e, pp=pp, Ep=Ep, lo=lo,
                                 hi=hi, cpb=cpb, np_=nparts:
                                 e.activation(Ep[:, lo:hi], pp, AF.Exp,
                                              bias=negMh, scale=RSQ,
                                              accum_out=cpb[:, np_:np_ + 1]),
                                 reads=PA(pcol, hi - lo) + [nk],
                                 writes=[ek, ck], name=f"exph{h}_{kc}")
                        nparts += 1
                seq.emit("vector", lambda e, cpb=cpb, np_=nparts:
                         e.reduce_sum(cpb[:, 3:4], cpb[:, 0:np_], axis=AX.X),
                         reads=[ck], writes=[ck], name=f"csum{h}_{kc}")
                seq.emit("vector", lambda e, cpb=cpb, rpar=epar:
                         e.reciprocal(rc[:, rpar:rpar + 1], cpb[:, 3:4]),
                         reads=[ck], writes=[f"rc{epar}"],
                         name=f"crec{h}_{kc}")
                vsp = vs[:, epar * 128:epar * 128 + 128]
                seq.emit("vector", lambda e, vsp=vsp, kc=kc, rpar=epar:
                         e.tensor_scalar(vsp,
                                         Vt[:, kc * 1024 + h * 128:
                                            kc * 1024 + h * 128 + 128],
                                         rc[:, rpar:rpar + 1], None, ALU.mult),
                         reads=["V", f"rc{epar}"], writes=[f"vs{epar}"],
                         name=f"vs{h}_{kc}")
                strip_info[kc] = (Ep, ek, epar, vsp)
                if dbg and h == 0 and kc == 0:
                    dbg_dump("E0", Ep, ek, BF16, 2048)
                    dbg_dump("cp0", cpb[:, 0:4], ck, F32, 4)
                    dbg_dump("rc0", rc[:, 0:1], f"rc{epar}", F32, 1)
                    dbg_dump("negM", negMh, nk, F32, 1)
                    dbg_dump("vs0", vsp, f"vs{epar}", BF16, 128)

            def emit_pv(kc, h=h):
                qlo = 128 * kc
                Ep, ek, epar, vsp = strip_info[kc]
                for qs in range(kc // 4, 4):
                    rlo = max(qs * 512, qlo)
                    rhi = (qs + 1) * 512
                    pb = psum_b[:, rlo:rhi]
                    last = (kc == min(15, 4 * qs + 3))
                    seq.emit("tensor", lambda e, pb=pb, vsp=vsp, Ep=Ep,
                             rlo=rlo, rhi=rhi, kc=kc, qs=qs, last=last:
                             e.matmul(pb, vsp, Ep[:, rlo:rhi],
                                      start=(kc == 0), stop=last),
                             reads=[ek, f"vs{epar}"], writes=[f"pb{qs}"],
                             name=f"pv{h}_{kc}_{qs}")
                    if last:
                        seq.emit("scalar", lambda e, h=h, qs=qs: e.activation(
                            attn_out[:, h * 2048 + qs * 512:
                                     h * 2048 + qs * 512 + 512],
                            psum_b[:, qs * 512:(qs + 1) * 512], AF.Copy),
                            reads=[f"pb{qs}"], writes=[f"ao{h}"],
                            name=f"aev{h}_{qs}")

            for kc in range(16):
                emit_strip(kc)
                if kc > 0:
                    emit_pv(kc - 1)
                if kc == 0:
                    pop_safe(4)
                elif kc >= 9:
                    pop_psum(4)
            emit_pv(15)
            pop_safe(99)
            pop_psum(99)
            if h == 5:
                seq.emit("sync", lambda e: e.dma_start(
                    out=swap_inA.ap().rearrange("(hq p) t -> p hq t", p=128),
                    in_=attn_out[:, 0:12288].rearrange(
                        "p (hq t) -> p hq t", hq=6)),
                    kind="dma",
                    reads=["ao0", "ao1", "ao2", "ao3", "ao4", "ao5"],
                    writes=["swap_inA"], name="swb_A")
                seq.emit("gpsimd", lambda e: e.collective_compute(
                    "AllGather", ALU.bypass,
                    replica_groups=[[0, 1], [2, 3], [4, 5], [6, 7]],
                    ins=[swap_inA.ap().opt()], outs=[swap_outA.ap().opt()]),
                    kind="cc", reads=["swap_inA"], writes=["swap_outA"],
                    name="swap_ccA")
                seq.emit("gpsimd", lambda e: e.reg_load(r_swap, offs[0:1, 0:1]),
                         reads=["offs"], writes=["r_swap"], name="ld_rswap")
            if h == 7:
                seq.emit("sync", lambda e: e.dma_start(
                    out=swap_inB.ap().rearrange("(hq p) t -> p hq t", p=128),
                    in_=attn_out[:, 12288:16384].rearrange(
                        "p (hq t) -> p hq t", hq=2)),
                    kind="dma", reads=["ao6", "ao7"],
                    writes=["swap_inB"], name="swb_B")
                seq.emit("gpsimd", lambda e: e.collective_compute(
                    "AllGather", ALU.bypass,
                    replica_groups=[[0, 1], [2, 3], [4, 5], [6, 7]],
                    ins=[swap_inB.ap().opt()], outs=[swap_outB.ap().opt()]),
                    kind="cc", reads=["swap_inB"], writes=["swap_outB"],
                    name="swap_ccB")

        if dbg:
            t_ao = nc.dram_tensor("dbg_attn_out", [128, 16384], BF16,
                                  kind="ExternalOutput")
            seq.emit("sync", lambda e: e.dma_start(out=t_ao.ap(), in_=attn_out),
                     kind="dma", reads=[f"ao{hh_}" for hh_ in range(8)],
                     name="dbg_attn_out")

        if stop_after == "p2":
            _finish(seq)
            seq.link_and_emit(nc, stack)
            return nc

        SLOT = {0: 0, 1: 1, 2: 2, 3: 3, 4: 4, 5: 5,
                8: 6, 9: 7, 10: 8, 11: 9, 12: 10, 13: 11,
                6: 12, 7: 13, 14: 14, 15: 15}

        seq.emit("gpsimd", lambda e: e.dma_start(
            out=af3[:, 12:16, :].rearrange("p (b f) t -> p b f t", b=2),
            in_=AP(swap_outB, r_swap,
                   [[2048, 128], [2 * 128 * 2048, 2], [128 * 2048, 2],
                    [1, 1024]])),
            kind="dma", reads=["swap_outB", "r_swap"],
            writes=["afB", "ao6", "ao7"], name="sw_readB")

        pop_stage(len(stage_th))
        for t in st_after:
            t()

        seq.emit("sync", lambda e: e.dma_start(
            out=bout[:, :], in_=bout_d.ap()),
            kind="dma", writes=["boutb", "bvb", "E0", "E1"], name="ld_bout")
        for to_ in (3, 7):
            emit_xhalf(to_)
        for to in range(8):
            seq.emit("vector", lambda e, to=to: e.tensor_tensor(
                x13[:, to, :], x13[:, to, :], bout[:, :], ALU.add),
                reads=[f"x1t{to}", "boutb"], writes=[f"x1t{to}"],
                name=f"x1b{to}")

        for pss in range(2):
            fcs = ([0, 1, 2, 3, 4, 5, 8, 9, 10, 11, 12, 13] if pss == 0
                   else [6, 7, 14, 15])
            afk = "afA" if pss == 0 else "afB"
            for ec in range(4):
                wpar = ec % 2
                wo3 = wout[:, wpar * 8192:(wpar + 1) * 8192].rearrange(
                    "p (fc e) -> p fc e", fc=16)
                wk = f"wout{wpar}"
                if not (pss == 0 and ec < 2):
                    seq.emit("sync", lambda e, wo3=wo3, ec=ec: e.dma_start(
                        out=wo3, in_=wout_r[:, :, ec * 512:(ec + 1) * 512]),
                        kind="dma", reads=["st_wout"], writes=[wk, "xlnT"],
                        name=f"ldwo{pss}_{ec}")
                for to in range(8):
                    pcol = (((pss * 4 + ec) * 8 + to) % 4) * 512
                    pk = PA(pcol, 512)
                    pp = psum_a[:, pcol:pcol + 512]
                    nj = len(fcs)
                    for j, fc in enumerate(fcs):
                        seq.emit("tensor", lambda e, pp=pp, wo3=wo3, fc=fc,
                                 to=to, j=j, nj=nj: e.matmul(
                                     pp,
                                     af3[:, SLOT[fc], to * 128:(to + 1) * 128],
                                     wo3[:, fc, :],
                                     start=(j == 0), stop=(j == nj - 1)),
                                 reads=[afk, wk], writes=pk,
                                 name=f"opmm{pss}_{ec}_{to}_{fc}")
                    seq.emit("vector", lambda e, pp=pp, to=to, ec=ec:
                             e.tensor_tensor(
                                 x13[:, to, ec * 512:(ec + 1) * 512],
                                 x13[:, to, ec * 512:(ec + 1) * 512],
                                 pp, ALU.add),
                             reads=pk + [f"x1t{to}"], writes=[f"x1t{to}"],
                             name=f"opev{pss}_{ec}_{to}")

        dbg_dump("x1", x1, "x1", F32, 16384)


        if stop_after == "p4":
            _finish(seq)
            seq.link_and_emit(nc, stack)
            return nc

        seq.barrier()

        x2c = V("x2c")
        xhc = V("xhc")
        xlc = V("xlc")
        xloT = V("xloT")
        xloT3 = xloT.rearrange("p (dc t) -> p dc t", dc=16)
        x2T = V("x2T")
        x2T3 = x2T.rearrange("p (dc t) -> p dc t", dc=16)
        WT = V("WT")
        ind = V("ind")
        scoresR = V("scoresR")
        Wr = V("W")
        m1v = V("m1s")
        m2v = V("m2s")
        t2a = V("t2a")
        t2b = V("t2b")
        t2c = V("t2c")
        w1w = V("w1w")
        w2w = V("w2w")

        seq.emit("sync", lambda e: e.dma_start(out=ind[0:8, :],
                                                 in_=ind_d.ap()),
                 kind="dma", writes=["ind"], name="ld_ind")

        st5 = stats[:, 0:16]
        for to in range(8):
            seq.emit("scalar", lambda e, to=to: e.activation(
                xhc[:, :], x13[:, to, :], AF.Square, accum_out=st5[:, 0:1]),
                reads=[f"x1t{to}"], writes=["st5", "xhc"], name=f"l2sq{to}")
            seq.emit("vector", lambda e, to=to: e.tensor_scalar(
                xlc[:, :], x13[:, to, :], 1.0, None, ALU.mult, ALU.add,
                accum_out=st5[:, 1:2]),
                reads=[f"x1t{to}"], writes=["st5", "xlc"], name=f"l2sm{to}")
            seq.emit("vector", lambda e: e.tensor_scalar(
                st5[:, 2:3], st5[:, 1:2], 1.0 / D, None, ALU.mult),
                reads=["st5"], writes=["st5"], name=f"l2mean{to}")
            seq.emit("vector", lambda e: e.tensor_scalar(
                st5[:, 3:4], st5[:, 0:1], 1.0 / D, None, ALU.mult),
                reads=["st5"], writes=["st5"], name=f"l2ex2{to}")
            seq.emit("vector", lambda e: e.scalar_tensor_tensor(
                st5[:, 4:5], st5[:, 2:3], -1.0, st5[:, 2:3],
                ALU.mult, ALU.mult),
                reads=["st5"], writes=["st5"], name=f"l2nm2{to}")
            seq.emit("vector", lambda e: e.tensor_tensor(
                st5[:, 5:6], st5[:, 3:4], st5[:, 4:5], ALU.add),
                reads=["st5"], writes=["st5"], name=f"l2var{to}")
            seq.emit("scalar", lambda e: e.activation(
                st5[:, 6:7], st5[:, 5:6], AF.Ln, bias=fc_t[:, 0:1],
                scale=1.0),
                reads=["st5", "fconst"], writes=["st5"], name=f"l2lnv{to}")
            seq.emit("scalar", lambda e: e.activation(
                st5[:, 7:8], st5[:, 6:7], AF.Exp, scale=-0.5),
                reads=["st5"], writes=["st5"], name=f"l2rstd{to}")
            seq.emit("vector", lambda e: e.scalar_tensor_tensor(
                st5[:, 8:9], st5[:, 2:3], -1.0, st5[:, 7:8],
                ALU.mult, ALU.mult),
                reads=["st5"], writes=["st5"], name=f"l2nmr{to}")
            seq.emit("vector", lambda e, to=to: e.tensor_scalar(
                x2c[:, :], x13[:, to, :], st5[:, 7:8], st5[:, 8:9],
                ALU.mult, ALU.add),
                reads=[f"x1t{to}", "st5"], writes=["x2c"], name=f"l2nrm{to}")
            seq.emit("scalar", lambda e: e.activation(
                xhc[:, :], x2c[:, :], AF.Copy),
                reads=["x2c"], writes=["xhc"], name=f"l2hi{to}")
            seq.emit("vector", lambda e: e.tensor_tensor(
                xlc[:, :], x2c[:, :], xhc[:, :], ALU.subtract),
                reads=["x2c", "xhc"], writes=["xlc"], name=f"l2lo{to}")
            for half, (src, srk) in enumerate(((xhc, "xhc"), (xlc, "xlc"))):
                trp = psum_abf[:, half * 2048:half * 2048 + 2048]
                trk = [f"pa{2 * half}", f"pa{2 * half + 1}"]
                for dc in range(16):
                    seq.emit("tensor", lambda e, src=src, trp=trp, dc=dc:
                             e.transpose(trp[:, dc * 128:(dc + 1) * 128],
                                         src[:, dc * 128:(dc + 1) * 128],
                                         ident[:, :]),
                             reads=[srk, "ident"], writes=trk,
                             name=f"l2tr{to}_{half}_{dc}")
                if half == 0:
                    seq.emit("vector", lambda e, trp=trp, to=to:
                             e.tensor_copy(
                                 x2T3[:, :, to * 128:(to + 1) * 128],
                                 trp.rearrange("p (dc t) -> p dc t", dc=16)),
                             reads=trk, writes=["x2T"],
                             name=f"l2te{to}_{half}")
                else:
                    seq.emit("vector", lambda e, trp=trp: e.tensor_copy(
                        xloT3, trp.rearrange("p (dc t) -> p dc t", dc=16)),
                        reads=trk, writes=["xloT"], name=f"l2te{to}_{half}")
            for dc in range(16):
                seq.emit("tensor", lambda e, dc=dc, to=to: e.matmul(
                    psum_b[:, 0:16], x2T3[:, dc, to * 128:(to + 1) * 128],
                    rwhl3[:, dc, :], start=(dc == 0), stop=False),
                    reads=["x2T", "rwhl"], writes=["pb0"],
                    name=f"rmmh{to}_{dc}")
                seq.emit("tensor", lambda e, dc=dc: e.matmul(
                    psum_b[:, 0:8], xloT3[:, dc, :], rwhl3[:, dc, 0:8],
                    start=False, stop=(dc == 15)),
                    reads=["xloT", "rwhl"], writes=["pb0"],
                    name=f"rmml{to}_{dc}")
            seq.emit("scalar", lambda e, to=to: e.activation(
                t2a[:, to * 8:(to + 1) * 8], psum_b[:, 0:8], AF.Copy),
                reads=["pb0"], writes=["t2a"], name=f"rsca{to}")
            seq.emit("vector", lambda e, to=to: e.tensor_tensor(
                scoresR[:, to * 8:(to + 1) * 8],
                t2a[:, to * 8:(to + 1) * 8],
                psum_b[:, 8:16], ALU.add),
                reads=["pb0", "t2a"], writes=["scoresR"], name=f"rsc{to}")

        sc3 = scoresR.rearrange("p (t e) -> p t e", t=8)
        t2a3 = t2a.rearrange("p (t e) -> p t e", t=8)
        t2b3 = t2b.rearrange("p (t e) -> p t e", t=8)
        t2c3 = t2c.rearrange("p (t e) -> p t e", t=8)
        Wr3 = Wr.rearrange("p (t e) -> p t e", t=8)

        def bcast8(v):
            return v.rearrange("p (t o) -> p t o", o=1).broadcast_to(
                [128, 8, 8])

        seq.emit("vector", lambda e: e.tensor_reduce(
            m1v.rearrange("p (t o) -> p t o", o=1), sc3, AX.X, ALU.max),
            reads=["scoresR"], writes=["m1s"], name="t2m1")
        seq.emit("vector", lambda e: e.tensor_tensor(
            t2b3, sc3, bcast8(m1v), ALU.is_ge),
            reads=["scoresR", "m1s"], writes=["t2b"], name="t2mask1")
        seq.emit("vector", lambda e: e.scalar_tensor_tensor(
            t2a3, t2b3, -1e30, sc3, ALU.mult, ALU.add),
            reads=["t2b", "scoresR"], writes=["t2a"], name="t2sub")
        seq.emit("vector", lambda e: e.tensor_reduce(
            m2v.rearrange("p (t o) -> p t o", o=1), t2a3, AX.X, ALU.max),
            reads=["t2a"], writes=["m2s"], name="t2m2")
        seq.emit("vector", lambda e: e.tensor_tensor(
            t2c3, t2a3, bcast8(m2v), ALU.is_ge),
            reads=["t2a", "m2s"], writes=["t2c"], name="t2mask2")
        seq.emit("vector", lambda e: e.tensor_tensor(
            m2v[:, :], m2v[:, :], m1v[:, :], ALU.subtract),
            reads=["m2s", "m1s"], writes=["m2s"], name="t2d")
        seq.emit("scalar", lambda e: e.activation(
            w2w[:, :], m2v[:, :], AF.Exp),
            reads=["m2s"], writes=["w2w"], name="t2exp")
        seq.emit("vector", lambda e: e.tensor_scalar(
            w1w[:, :], w2w[:, :], 1.0, None, ALU.add),
            reads=["w2w"], writes=["w1w"], name="t2den")
        seq.emit("vector", lambda e: e.reciprocal(w1w[:, :], w1w[:, :]),
                 reads=["w1w"], writes=["w1w"], name="t2rec")
        seq.emit("vector", lambda e: e.tensor_tensor(
            w2w[:, :], w2w[:, :], w1w[:, :], ALU.mult),
            reads=["w2w", "w1w"], writes=["w2w"], name="t2w2")
        seq.emit("vector", lambda e: e.tensor_tensor(
            t2b3, t2b3, bcast8(w1w), ALU.mult),
            reads=["t2b", "w1w"], writes=["t2b"], name="t2wa")
        seq.emit("vector", lambda e: e.tensor_tensor(
            t2c3, t2c3, bcast8(w2w), ALU.mult),
            reads=["t2c", "w2w"], writes=["t2c"], name="t2wb")
        seq.emit("vector", lambda e: e.tensor_tensor(
            Wr3, t2b3, t2c3, ALU.add),
            reads=["t2b", "t2c"], writes=["Wr"], name="t2W")

        for to in range(8):
            seq.emit("tensor", lambda e, to=to: e.transpose(
                psum_b[0:8, to * 128:to * 128 + 128], Wr3[:, to, :],
                identf[:, :]),
                reads=["Wr", "identf"], writes=PB(to * 128, 128),
                name=f"wtr{to}")
        seq.emit("vector", lambda e: e.tensor_copy(
            WT[0:8, :], psum_b[0:8, 0:1024]),
            reads=PB(0, 1024), writes=["WTb"], name="wtev")

        dbg_dump("Wr", Wr, "Wr", F32, 64)
        dbg_dump("x2T", x2T, "x2T", BF16, 16384)
        dbg_dump("xloT", xloT, "xloT", BF16, 2048)
        dbg_dump("scoresR", scoresR, "scoresR", F32, 64)

        if stop_after == "p5":
            _finish(seq)
            seq.link_and_emit(nc, stack)
            return nc

        seq.barrier()

        Wb = V("Wb")
        h_all = V("h_all")
        h3 = h_all.rearrange("p (e f t) -> p e f t", e=8, f=4)
        w1c = V("w1c")
        w3c = V("w3c")
        w2c = V("w2c")
        s6 = V("s6")
        w1_3 = [w1c[:, par * 2048:(par + 1) * 2048]
                .rearrange("p (dc f) -> p dc f", dc=16) for par in range(2)]
        w3_3 = [w3c[:, par * 2048:(par + 1) * 2048]
                .rearrange("p (dc f) -> p dc f", dc=16) for par in range(2)]
        w2_3 = [w2c[:, par * 4096:(par + 1) * 4096]
                .rearrange("p (fs d) -> p fs d", fs=4) for par in range(2)]

        for e_i in range(NE):
            pcol = (e_i % 2) * 1024
            pk = PA(pcol, 1024)
            pp = psum_a[:, pcol:pcol + 1024]
            for half in range(2):
                seq.emit("tensor", lambda e, pp=pp, e_i=e_i, half=half:
                         e.matmul(pp[:, half * 512:(half + 1) * 512],
                                  ind[0:8, e_i * 128:(e_i + 1) * 128],
                                  WT[0:8, half * 512:(half + 1) * 512],
                                  start=True, stop=True),
                         reads=["WTb", "ind"], writes=pk,
                         name=f"wbmm{e_i}_{half}")
            seq.emit("scalar", lambda e, pp=pp, e_i=e_i: e.activation(
                Wb[:, e_i * 1024:(e_i + 1) * 1024], pp, AF.Copy),
                reads=pk, writes=["Wbb"], name=f"wbev{e_i}")

        for tc in range(2):
            for e_i in range(NE):
                for fc in range(4):
                    par = (e_i * 4 + fc) % 2
                    wk1, wk3 = f"w1c{par}", f"w3c{par}"
                    rb = (e_i * 4 + fc) % 2
                    seq.emit("sync", lambda e, par=par, e_i=e_i, fc=fc:
                             e.dma_start(
                                 out=w1_3[par],
                                 in_=w1_b[(e_i * 4 + fc) * D:
                                          (e_i * 4 + fc + 1) * D, :]
                                 .rearrange("(dc p) f -> p dc f", p=128)),
                             kind="dma", reads=["st_w1"], writes=[wk1],
                             name=f"ldw1_{tc}_{e_i}_{fc}")
                    seq.emit("sync", lambda e, par=par, e_i=e_i, fc=fc:
                             e.dma_start(
                                 out=w3_3[par],
                                 in_=w3_b[(e_i * 4 + fc) * D:
                                          (e_i * 4 + fc + 1) * D, :]
                                 .rearrange("(dc p) f -> p dc f", p=128)),
                             kind="dma", reads=["st_w3"], writes=[wk3],
                             name=f"ldw3_{tc}_{e_i}_{fc}")
                    pa_col = rb * 1024
                    pb_col = rb * 1024 + 512
                    ppa = psum_a[:, pa_col:pa_col + 512]
                    ppb = psum_a[:, pb_col:pb_col + 512]
                    pka = PA(pa_col, 512)
                    pkb = PA(pb_col, 512)
                    for dc in range(16):
                        seq.emit("tensor", lambda e, ppa=ppa, par=par, dc=dc,
                                 tc=tc: e.matmul(
                                     ppa, w1_3[par][:, dc, :],
                                     x2T3[:, dc, tc * 512:(tc + 1) * 512],
                                     start=(dc == 0), stop=(dc == 15)),
                                 reads=["x2T", wk1], writes=pka,
                                 name=f"x1mm{tc}_{e_i}_{fc}_{dc}")
                    for dc in range(16):
                        seq.emit("tensor", lambda e, ppb=ppb, par=par, dc=dc,
                                 tc=tc: e.matmul(
                                     ppb, w3_3[par][:, dc, :],
                                     x2T3[:, dc, tc * 512:(tc + 1) * 512],
                                     start=(dc == 0), stop=(dc == 15)),
                                 reads=["x2T", wk3], writes=pkb,
                                 name=f"x3mm{tc}_{e_i}_{fc}_{dc}")
                    s1 = s6[:, 0:512]
                    s3 = s6[:, 512:1024]
                    seq.emit("scalar", lambda e, ppa=ppa, s1=s1: e.activation(
                        s1, ppa, AF.Silu),
                        reads=pka, writes=["s6a"], name=f"sil{tc}_{e_i}_{fc}")
                    seq.emit("vector", lambda e, ppb=ppb, s3=s3, e_i=e_i,
                             tc=tc: e.tensor_tensor(
                                 s3, ppb,
                                 Wb[:, e_i * 1024 + tc * 512:
                                    e_i * 1024 + tc * 512 + 512],
                                 ALU.mult),
                             reads=pkb + ["Wbb"], writes=["s6b"],
                             name=f"x3s{tc}_{e_i}_{fc}")
                    seq.emit("vector", lambda e, s1=s1, s3=s3, e_i=e_i, fc=fc:
                             e.tensor_tensor(h3[:, e_i, fc, :], s1, s3,
                                             ALU.mult),
                             reads=["s6a", "s6b"], writes=["h_all"],
                             name=f"hm{tc}_{e_i}_{fc}")
            for pp_i in range(2):
                for e_i in range(NE):
                    wpar = e_i % 2
                    wk2 = f"w2c{wpar}"
                    seq.emit("sync", lambda e, wpar=wpar, e_i=e_i,
                             pp_i=pp_i: e.dma_start(
                                 out=w2_3[wpar],
                                 in_=w2_b[e_i * FF:(e_i + 1) * FF, :]
                                 .rearrange("(fs p) d -> p fs d", p=128)
                                 [:, :, pp_i * 1024:(pp_i + 1) * 1024]),
                             kind="dma", reads=["st_w2"], writes=[wk2],
                             name=f"ldw2_{tc}_{pp_i}_{e_i}")
                    for fc in range(4):
                        for ts in range(4):
                            for dh in range(2):
                                reg = ts * 2 + dh
                                if reg < 4:
                                    pr = psum_a[:, reg * 512:reg * 512 + 512]
                                    prk = PA(reg * 512, 512)
                                else:
                                    pr = psum_b[:, (reg - 4) * 512:
                                                (reg - 4) * 512 + 512]
                                    prk = PB((reg - 4) * 512, 512)
                                seq.emit("tensor", lambda e, pr=pr, wpar=wpar,
                                         e_i=e_i, fc=fc, ts=ts, dh=dh:
                                         e.matmul(
                                             pr,
                                             h3[:, e_i, fc,
                                                ts * 128:(ts + 1) * 128],
                                             w2_3[wpar][:, fc,
                                                        dh * 512:
                                                        (dh + 1) * 512],
                                             start=(e_i == 0 and fc == 0),
                                             stop=(e_i == 7 and fc == 3)),
                                         reads=["h_all", wk2], writes=prk,
                                         name=f"o2_{tc}_{pp_i}_{e_i}"
                                              f"_{fc}_{ts}_{dh}")
                for ts in range(4):
                    for dh in range(2):
                        reg = ts * 2 + dh
                        if reg < 4:
                            pr = psum_a[:, reg * 512:reg * 512 + 512]
                            prk = PA(reg * 512, 512)
                        else:
                            pr = psum_b[:, (reg - 4) * 512:
                                        (reg - 4) * 512 + 512]
                            prk = PB((reg - 4) * 512, 512)
                        to = tc * 4 + ts
                        dcol = pp_i * 1024 + dh * 512
                        seq.emit("vector", lambda e, pr=pr, to=to, dcol=dcol:
                                 e.tensor_tensor(
                                     x13[:, to, dcol:dcol + 512],
                                     x13[:, to, dcol:dcol + 512],
                                     pr, ALU.add),
                                 reads=prk + [f"x1t{to}"],
                                 writes=[f"x1t{to}"],
                                 name=f"o2ev{tc}_{pp_i}_{ts}_{dh}")

        seq.emit("sync", lambda e: e.dma_start(
            out=out_d.ap().rearrange("(to p) d -> p to d", p=128), in_=x13),
            kind="dma", reads=[f"x1t{i}" for i in range(8)], writes=["OUT"],
            name="st_out")
        _finish(seq)
        seq.link_and_emit(nc, stack)
    return nc


def _finish(seq):
    dma_ops = [op for op in seq.eng_ops["sync"] if op.kind == "dma"]
    op = seq.emit("sync", lambda e: e.nop(), name="final_wait")
    op.deps.extend((d, True) for d in dma_ops)


def _host_inputs(L0_attn_x, L0_attn_W_qkv, L0_attn_b_qkv, L0_attn_W_out,
                 L0_attn_b_out, L0_ffn_router_w, L0_ffn_w1, L0_ffn_w2,
                 L0_ffn_w3):
    x = np.asarray(L0_attn_x, np.float32)
    Wqkv = np.asarray(L0_attn_W_qkv, np.float32)
    bqkv = np.asarray(L0_attn_b_qkv, np.float32)
    Wout = np.asarray(L0_attn_W_out, np.float32)
    bout = np.asarray(L0_attn_b_out, np.float32)
    rw = np.asarray(L0_ffn_router_w, np.float32)
    w1 = np.asarray(L0_ffn_w1, np.float32)
    w2 = np.asarray(L0_ffn_w2, np.float32)
    w3 = np.asarray(L0_ffn_w3, np.float32)

    wout_h = np.ascontiguousarray(Wout.T)
    bout_h = np.ascontiguousarray(np.broadcast_to(bout, (128, D)))
    rw_h = np.ascontiguousarray(rw.T)
    ind_h = np.zeros((NE, NE * 128), np.float32)
    for e in range(NE):
        ind_h[e, e * 128:(e + 1) * 128] = 1.0
    w1_h = np.ascontiguousarray(
        w1.reshape(NE, D, 4, 128).transpose(0, 2, 1, 3)).reshape(
            NE * 4 * D, 128)
    w3_h = np.ascontiguousarray(
        w3.reshape(NE, D, 4, 128).transpose(0, 2, 1, 3)).reshape(
            NE * 4 * D, 128)
    w2_h = np.ascontiguousarray(w2).reshape(NE * FF, D)

    in_maps = []
    for c in range(8):
        b, g = c // 2, c % 2
        sl = slice(1024 * g, 1024 * g + 1024)
        wqk_h = np.ascontiguousarray(
            np.concatenate([Wqkv[sl],
                            Wqkv[2048 + 1024 * g:2048 + 1024 * g + 1024]],
                           0).T)
        wv_h = np.ascontiguousarray(
            Wqkv[4096 + 1024 * g:4096 + 1024 * g + 1024].T)
        bqk_h = np.ascontiguousarray(
            np.concatenate([bqkv[sl],
                            bqkv[2048 + 1024 * g:2048 + 1024 * g + 1024]]
                           ).reshape(16, 128).T)
        bv_h = np.ascontiguousarray(
            np.broadcast_to(bqkv[4096 + 1024 * g:4096 + 1024 * g + 1024],
                            (128, 1024)))
        offs_h = np.zeros((1, 8), np.uint32)
        offs_h[0, 0] = g * 1024
        in_maps.append({
            "x_seq": np.ascontiguousarray(x[b]),
            "x_half": np.ascontiguousarray(x[b, sl]),
            "wqk": wqk_h, "wv": wv_h, "bqk": bqk_h, "bv": bv_h,
            "wout": wout_h, "bout": bout_h, "rw": rw_h, "ind": ind_h,
            "w1": w1_h, "w3": w3_h, "w2": w2_h, "offs": offs_h,
        })
    return in_maps


def kernel(L0_attn_x, L0_attn_W_qkv, L0_attn_b_qkv, L0_attn_W_out,
           L0_attn_b_out, L0_ffn_router_w, L0_ffn_w1, L0_ffn_w2, L0_ffn_w3):
    in_maps = _host_inputs(L0_attn_x, L0_attn_W_qkv, L0_attn_b_qkv,
                           L0_attn_W_out, L0_attn_b_out, L0_ffn_router_w,
                           L0_ffn_w1, L0_ffn_w2, L0_ffn_w3)
    nc = build_program()
    res = run_bass_kernel_spmd(nc, in_maps, core_ids=list(range(8)))
    y = np.empty((4, S, D), np.float32)
    for c in range(8):
        b, g = c // 2, c % 2
        y[b, 1024 * g:1024 * g + 1024] = np.asarray(res.results[c]["out"],
                                                    np.float32)
    return y
